# version 05
# baseline (speedup 1.0000x reference)
import os
from contextlib import ExitStack

import numpy as np
import concourse.bass as bass
import concourse.mybir as mybir
from concourse.bass_utils import run_bass_kernel_spmd

F32 = mybir.dt.float32
BF16 = mybir.dt.bfloat16
AF = mybir.ActivationFunctionType
ALU = mybir.AluOpType
AP = bass.AP

S = 4096
H = 2048
INTER = 4096
NH, HD, NS, NG = 64, 64, 128, 8
GN = NG * NS
CONV = INTER + 2 * GN
DFF = 8192
Q = 128
NCH = S // Q
KH = H // 128
KI = INTER // 128
NXT = CONV // 128
HALF = S // 2
NTB = HALF // 512
NTT = HALF // 128
EPS = 1e-5
CLAMP = 50.0


def build_program(dbg=(), phases=6, nch3=NCH):
    nc = bass.Bass("TRN2", target_bir_lowering=False,
                   detect_race_conditions=False)
    dbg = set(dbg)
    E = {}

    def sem(name):
        return nc.alloc_semaphore(name=name)

    def scratch(name, shape, dtype):
        kind = "ExternalOutput" if name in dbg else "Internal"
        return nc.dram_tensor(name, shape, dtype, kind=kind)

    E["d_h0"] = nc.dram_tensor("h0", [S, H], F32, kind="ExternalInput")
    E["d_wz"] = nc.dram_tensor("wz", [H, INTER], F32, kind="ExternalInput")
    E["d_wx"] = nc.dram_tensor("wx", [H, INTER], F32, kind="ExternalInput")
    E["d_wb"] = nc.dram_tensor("wb", [H, GN], F32, kind="ExternalInput")
    E["d_wc"] = nc.dram_tensor("wc", [H, GN], F32, kind="ExternalInput")
    E["d_wdt"] = nc.dram_tensor("wdt", [H, NH], F32, kind="ExternalInput")
    E["d_cw"] = nc.dram_tensor("convw", [CONV, 4], F32, kind="ExternalInput")
    E["d_cb"] = nc.dram_tensor("convb", [CONV], F32, kind="ExternalInput")
    E["d_dtb"] = nc.dram_tensor("dtb", [NH], F32, kind="ExternalInput")
    E["d_alog"] = nc.dram_tensor("alog", [NH], F32, kind="ExternalInput")
    E["d_dvec"] = nc.dram_tensor("dvec", [NH], F32, kind="ExternalInput")
    E["d_nw"] = nc.dram_tensor("nw", [INTER], F32, kind="ExternalInput")
    E["d_wout"] = nc.dram_tensor("wout", [INTER, H], F32, kind="ExternalInput")
    E["d_wg"] = nc.dram_tensor("wgate", [DFF, H], F32, kind="ExternalInput")
    E["d_wu"] = nc.dram_tensor("wup", [DFF, H], F32, kind="ExternalInput")
    E["d_wd"] = nc.dram_tensor("wdown", [H, DFF], F32, kind="ExternalInput")
    E["d_out"] = nc.dram_tensor("out", [S, H], F32, kind="ExternalOutput")

    E["d_hbf"] = scratch("h_bf", [S, H], BF16)
    E["d_woutb"] = scratch("wout_bf", [INTER, H], BF16)
    E["d_wgb"] = scratch("wg_bf", [DFF, H], BF16)
    E["d_wub"] = scratch("wu_bf", [DFF, H], BF16)
    E["d_wdb"] = scratch("wd_bf", [H, DFF], BF16)
    E["d_wgT"] = scratch("wgT", [H, DFF], BF16)
    E["d_wuT"] = scratch("wuT", [H, DFF], BF16)
    E["d_wdT"] = scratch("wdT", [DFF, H], BF16)
    E["d_xbc"] = scratch("xbc_bf", [CONV, S], BF16)
    E["d_sg"] = scratch("sg_h", [S, INTER], F32)
    E["d_tabs"] = scratch("tab_s", [NH, S], F32)
    E["d_tabsC"] = scratch("tab_sc", [NCH * NH, Q], F32)
    E["d_dec"] = scratch("dec_h", [NCH, NH], F32)
    E["d_norm"] = scratch("normed_h", [S, INTER], BF16)
    E["d_x1f"] = scratch("x1f_h", [S, H], F32)
    E["d_x1b"] = scratch("x1b_h", [S, H], BF16)
    E["d_gated"] = scratch("gated_h", [DFF, S], BF16)
    E["d_part0"] = scratch("part0_h", [S, H // 2], F32)

    pA = nc.alloc_psum_tensor("pA", [128, 2048], F32)
    pB = nc.alloc_psum_tensor("pB", [128, 2048], F32)
    E["pA"], E["pB"] = pA, pB
    E["bankA"] = lambda i: pA[:, i * 512:(i + 1) * 512]
    E["bankB"] = lambda i: pB[:, i * 512:(i + 1) * 512]

    sb = nc.alloc_sbuf_tensor
    E["cw_sb"] = sb("cw_sb", [128, 4 * NXT], F32)
    E["cb_sb"] = sb("cb_sb", [128, NXT], F32)
    E["dtb_sb"] = sb("dtb_sb", [64, 1], F32)
    E["alog_sb"] = sb("alog_sb", [64, 1], F32)
    E["negA_sb"] = sb("negA_sb", [64, 1], F32)
    E["dbcb_sb"] = sb("dbcb_sb", [128, 64], BF16)
    E["dbcf_sb"] = sb("dbcf_sb", [128, 64], F32)
    E["nw_sb"] = sb("nw_sb", [128, KI], F32)
    E["onesb_sb"] = sb("onesb_sb", [128, 128], BF16)
    E["onesf_sb"] = sb("onesf_sb", [64, 64], F32)
    E["onesf128"] = sb("onesf128", [128, 128], F32)
    E["id128f"] = sb("id128f", [128, 128], F32)
    E["caus_sb"] = sb("caus_sb", [128, 128], BF16)
    E["id64_sb"] = sb("id64_sb", [64, 64], F32)
    E["s_pc"] = sem("pc")

    CONSTS = [EPS, -0.5, -1.0, 1.0 / INTER, 1.0 / H, CLAMP]
    E["const_tiles"] = []
    for i, val in enumerate(CONSTS):
        t = sb(f"cst{i}", [128, 1], F32)
        nc.const_aps.register(F32, val, t[:, :])
        E["const_tiles"].append((t, val))

    with nc.named_scope("P1"):
        build_p1(nc, sem, E)
    if phases >= 2:
        with (
            nc.sbuf_tensor("pack1", [128, S], F32) as pack1,
            nc.sbuf_tensor("pack2", [128, S], F32) as pack2,
        ):
            E["pack1"], E["pack2"] = pack1, pack2
            with nc.named_scope("P2"):
                build_p2(nc, sem, E)
            if phases >= 3:
                with nc.named_scope("P3"):
                    build_p3(nc, sem, E, nch3)
    if phases >= 4:
        with nc.named_scope("P4"):
            build_p4(nc, sem, E)
    if phases >= 5:
        with nc.named_scope("P5"):
            build_p5(nc, sem, E)
    if phases >= 6:
        with nc.named_scope("P6"):
            build_p6(nc, sem, E)
    if phases >= 2:
        with nc.Block() as blk:
            @blk.gpsimd
            def _(g):
                g.wait_ge(E["s_pc"], 16 * E["NPC"])
    return nc


def build_p1(nc, sem, E):
    s_ld = sem("p1_ld")
    E["s_hc"] = s_hc = sem("p1_hc")

    with nc.Block() as blk:
        @blk.sync
        def _(sy):
            n = 0
            sy.dma_start(E["cw_sb"][:, :],
                         AP(E["d_cw"], 0, [[4, 128], [512, NXT], [1, 4]])
                         ).then_inc(s_ld, 16); n += 16
            sy.dma_start(E["cb_sb"][:, :],
                         AP(E["d_cb"], 0, [[1, 128], [128, NXT]]),
                         allow_slow_non_contiguous=True).then_inc(s_ld, 16); n += 16
            sy.dma_start(E["dtb_sb"][:, :], AP(E["d_dtb"], 0, [[1, 64], [1, 1]])
                         ).then_inc(s_ld, 16); n += 16
            sy.dma_start(E["alog_sb"][:, :], AP(E["d_alog"], 0, [[1, 64], [1, 1]])
                         ).then_inc(s_ld, 16); n += 16
            sy.dma_start(E["dbcf_sb"][:, :], AP(E["d_dvec"], 0, [[0, 128], [1, 64]])
                         ).then_inc(s_ld, 16); n += 16
            sy.dma_start(E["nw_sb"][:, :], AP(E["d_nw"], 0, [[1, 128], [128, KI]]),
                         allow_slow_non_contiguous=True).then_inc(s_ld, 16); n += 16
            sy.wait_ge(s_ld, n)

        @blk.gpsimd
        def _(g):
            for i in range(4):
                g.dma_start(E["d_hbf"].ap()[i * 1024:(i + 1) * 1024, :],
                            E["d_h0"].ap()[i * 1024:(i + 1) * 1024, :]
                            ).then_inc(s_hc, 16)
            g.memset(E["onesb_sb"][:, :], 1.0)
            g.memset(E["onesf_sb"][:, :], 1.0)
            g.memset(E["onesf128"][:, :], 1.0)
            for t, val in E["const_tiles"]:
                g.memset(t[:, :], val)

    with nc.Block() as blk:
        @blk.gpsimd
        def _(g):
            g.affine_select(E["caus_sb"][:, :], E["onesb_sb"][:, :], [[1, 128]],
                            ALU.is_ge, 0.0, channel_multiplier=-1)
            g.affine_select(E["id64_sb"][:, :], E["onesf_sb"][:, :], [[1, 64]],
                            ALU.is_equal, 0.0, channel_multiplier=-1)
            g.affine_select(E["id128f"][:, :], E["onesf128"][:, :], [[1, 128]],
                            ALU.is_equal, 0.0, channel_multiplier=-1)

        @blk.scalar
        def _(sc):
            sc.activation(E["negA_sb"][:, :], E["alog_sb"][:, :], AF.Exp)

        @blk.vector
        def _(v):
            v.tensor_copy(E["dbcb_sb"][:, :], E["dbcf_sb"][:, :])

    with nc.Block() as blk:
        @blk.vector
        def _(v):
            v.tensor_scalar_mul(E["negA_sb"][:, :], E["negA_sb"][:, :], -1.0)


def build_p2(nc, sem, E):
    bankA, bankB, pA = E["bankA"], E["bankB"], E["pA"]
    pack1, pack2 = E["pack1"], E["pack2"]
    s_pc, s_hc = E["s_pc"], E["s_hc"]

    s_ht = sem("p2_ht"); s_mm = sem("p2_mm")
    s_wld2 = [sem("p2_wld0"), sem("p2_wld1")]; s_wldt = sem("p2_wldt")
    s_ev = sem("p2_ev"); s_cv = sem("p2_cv"); s_si = sem("p2_si")
    s_sp = sem("p2_sp"); s_gld = sem("p2_gld"); s_gmm = sem("p2_gmm")
    s_gev = sem("p2_gev"); s_dmm = sem("p2_dmm")
    s_gst4 = [sem(f"p2_gst{i}") for i in range(4)]
    s_dev = sem("p2_dev"); s_tsp = sem("p2_tsp")

    W_SRCS = []
    for t in range(NXT):
        if t < 32:
            W_SRCS.append((E["d_wx"], t * 128, INTER))
        elif t < 40:
            W_SRCS.append((E["d_wb"], (t - 32) * 128, GN))
        else:
            W_SRCS.append((E["d_wc"], (t - 40) * 128, GN))

    PCH = ([(E["d_wg"], E["d_wgb"], i * 1024) for i in range(8)]
           + [(E["d_wu"], E["d_wub"], i * 1024) for i in range(8)])
    PCH_LATE = ([(E["d_wout"], E["d_woutb"], i * 1024) for i in range(4)]
                + [(E["d_wd"], E["d_wdb"], i * 1024) for i in range(2)])
    E["NPC"] = NPC = len(PCH) + len(PCH_LATE)
    E["NPC_EARLY"] = NPC_EARLY = len(PCH)

    def w_lhsT_ap(dw, of0, ncols):
        return AP(dw, of0, [[ncols, 128], [128 * ncols, KH], [1, 128]])

    def gw_rhs_ap(dw, ofb, ncols):
        return AP(dw, ofb * 512, [[ncols, 128], [128 * ncols, KH], [1, 512]])


    with ExitStack() as _st:
        hTb = _st.enter_context(nc.sbuf_tensor("hT", [128, KH * HALF], BF16))
        rb0 = _st.enter_context(nc.sbuf_tensor("rowb0", [128, HALF + 8], F32))
        rb1 = _st.enter_context(nc.sbuf_tensor("rowb1", [128, HALF + 8], F32))
        rowc = _st.enter_context(nc.sbuf_tensor("rowc", [128, HALF], F32))
        wxbb = _st.enter_context(nc.sbuf_tensor("wxb", [128, 2 * KH * 128], BF16))
        gwbb = _st.enter_context(nc.sbuf_tensor("gwb", [128, KH * 512], BF16))
        gevb = _st.enter_context(nc.sbuf_tensor("gev", [128, 4 * 512], F32))
        wdtb = _st.enter_context(nc.sbuf_tensor("wdtb", [128, KH * 64], BF16))
        carry = _st.enter_context(nc.sbuf_tensor("carry", [128, 3 * NXT], F32))
        slotA = _st.enter_context(nc.sbuf_tensor("slotA", [64, S], F32))
        slotB = _st.enter_context(nc.sbuf_tensor("slotB", [64, S], F32))
        slotC = _st.enter_context(nc.sbuf_tensor("slotC", [64, S], F32))
        zseg = _st.enter_context(nc.sbuf_tensor("zseg", [64, Q], F32))
        slast = _st.enter_context(nc.sbuf_tensor("slast", [64, NCH], F32))
        decT = _st.enter_context(nc.sbuf_tensor("decT", [NCH, 64], F32))
        hT = [hTb[:, k * HALF:(k + 1) * HALF] for k in range(KH)]
        rowbuf = [rb0, rb1]
        wxbuf = [wxbb[:, 0:KH * 128], wxbb[:, KH * 128:2 * KH * 128]]
        gwbuf = [gwbb, gwbb]
        gev = [gevb[:, i * 512:(i + 1) * 512] for i in range(4)]

        with nc.Block() as blk:
            @blk.gpsimd
            def _(g):
                g.memset(rb0[:, 0:3], 0.0)
                g.memset(rb1[:, 0:3], 0.0)
                g.memset(zseg[:, :], 0.0)
                g.memset(pack2[64:128, :], 0.0)

        pci = [0]

        for half in range(2):
            tok0 = half * HALF
            with nc.Block() as blk:
                @blk.sync
                def _(sy):
                    if half == 0:
                        sy.wait_ge(s_hc, 64)
                    for k in range(KH):
                        sy.dma_start_transpose(
                            hT[k],
                            E["d_hbf"].ap()[tok0:tok0 + HALF,
                                            k * 128:(k + 1) * 128]
                        ).then_inc(s_ht, 16)
                    for ofb in range(8):
                        for tt in range(NTT):
                            gidx = (half * 8 + ofb) * NTT + tt
                            sy.wait_ge(s_gev, gidx + 1)
                            sy.dma_start(
                                E["d_sg"].ap()[tok0 + tt * 128:
                                               tok0 + (tt + 1) * 128,
                                               ofb * 512:(ofb + 1) * 512],
                                gev[gidx % 4]).then_inc(s_gst4[gidx % 4], 16)
                    for i in range(4):
                        sy.wait_ge(s_gst4[i], 16 * (half + 1) * 32)

                @blk.gpsimd
                def _(g):
                    for t in range(NXT):
                        gi = half * NXT + t
                        if gi >= 2:
                            g.wait_ge(s_cv, gi - 1)
                        dw, of0, ncols = W_SRCS[t]
                        g.dma_start(wxbuf[gi % 2][:, :],
                                    w_lhsT_ap(dw, of0, ncols)
                                    ).then_inc(s_wld2[gi % 2], 16)
                        if half == 0 and t == 0:
                            g.dma_start(
                                wdtb[:, :],
                                AP(E["d_wdt"], 0,
                                   [[NH, 128], [128 * NH, KH], [1, 64]])
                            ).then_inc(s_wldt, 16)
                        target = (half * NXT + t) // 2
                        while pci[0] < min(target, NPC_EARLY):
                            src, dst, r0 = PCH[pci[0]]
                            g.dma_start(dst.ap()[r0:r0 + 1024, :],
                                        src.ap()[r0:r0 + 1024, :]
                                        ).then_inc(s_pc, 16)
                            pci[0] += 1
                        if gi >= 1:
                            g.wait_ge(s_si, gi)
                            tp, hp = (gi - 1) % NXT, (gi - 1) // NXT
                            g.dma_start(
                                E["d_xbc"].ap()[tp * 128:(tp + 1) * 128,
                                                hp * HALF:(hp + 1) * HALF],
                                rowc[:, :]).then_inc(s_sp, 16)
                    for ofb in range(8):
                        gio = half * 8 + ofb
                        if gio >= 1:
                            g.wait_ge(s_gmm, gio * NTT)
                        g.dma_start(gwbuf[gio % 2][:, :],
                                    gw_rhs_ap(E["d_wz"], ofb, INTER)
                                    ).then_inc(s_gld, 16)
                    gi_end = (half + 1) * NXT
                    g.wait_ge(s_si, gi_end)
                    g.dma_start(
                        E["d_xbc"].ap()[(NXT - 1) * 128:NXT * 128,
                                        half * HALF:(half + 1) * HALF],
                        rowc[:, :]).then_inc(s_sp, 16)
                    if half == 1:
                        while pci[0] < NPC_EARLY:
                            src, dst, r0 = PCH[pci[0]]
                            g.dma_start(dst.ap()[r0:r0 + 1024, :],
                                        src.ap()[r0:r0 + 1024, :]
                                        ).then_inc(s_pc, 16)
                            pci[0] += 1
                        for src, dst, r0 in PCH_LATE:
                            g.dma_start(dst.ap()[r0:r0 + 1024, :],
                                        src.ap()[r0:r0 + 1024, :]
                                        ).then_inc(s_pc, 16)
                    g.wait_ge(s_sp, 16 * gi_end)

                @blk.tensor
                def _(t_):
                    t_.wait_ge(s_ht, 16 * KH * (half + 1))
                    for t in range(NXT):
                        gi = half * NXT + t
                        t_.wait_ge(s_wld2[gi % 2], 16 * (gi // 2 + 1))
                        for tb in range(NTB):
                            idx = gi * NTB + tb
                            if idx >= 2:
                                t_.wait_ge(s_ev, idx - 1)
                            for k in range(KH):
                                inst = t_.matmul(
                                    bankA(idx % 2),
                                    wxbuf[gi % 2][:, k * 128:(k + 1) * 128],
                                    hT[k][:, tb * 512:(tb + 1) * 512],
                                    start=(k == 0), stop=(k == KH - 1))
                            inst.then_inc(s_mm, 1)
                    for ofb in range(8):
                        gio = half * 8 + ofb
                        t_.wait_ge(s_gld, 16 * (gio + 1))
                        for tt in range(NTT):
                            gidx = gio * NTT + tt
                            if gidx >= 2:
                                t_.wait_ge(s_gev, gidx - 1)
                            for k in range(KH):
                                inst = t_.matmul(
                                    bankB(gidx % 2),
                                    hT[k][:, tt * 128:(tt + 1) * 128],
                                    gwbuf[gio % 2][:, k * 512:(k + 1) * 512],
                                    start=(k == 0), stop=(k == KH - 1))
                            inst.then_inc(s_gmm, 1)
                    if half == 0:
                        t_.wait_ge(s_wldt, 16)
                    for tb in range(NTB):
                        didx = half * NTB + tb
                        if didx >= 2:
                            t_.wait_ge(s_dev, didx - 1)
                        for k in range(KH):
                            inst = t_.matmul(
                                bankB(2 + didx % 2)[0:64, :],
                                wdtb[:, k * 64:(k + 1) * 64],
                                hT[k][:, tb * 512:(tb + 1) * 512],
                                start=(k == 0), stop=(k == KH - 1))
                        inst.then_inc(s_dmm, 1)

                @blk.scalar
                def _(sc):
                    for t in range(NXT):
                        gi = half * NXT + t
                        for tb in range(NTB):
                            idx = gi * NTB + tb
                            sc.wait_ge(s_mm, idx + 1)
                            if gi >= 2 and tb == 0:
                                sc.wait_ge(s_cv, gi - 1)
                            sc.activation(
                                rowbuf[gi % 2][:, 3 + tb * 512:
                                               3 + (tb + 1) * 512],
                                bankA(idx % 2), AF.Copy).then_inc(s_ev, 1)
                        sc.wait_ge(s_cv, gi + 1)
                        sc.activation(rowc[:, :], rowc[:, :], AF.Silu)
                        sc.drain()
                        sc.nop().then_inc(s_si, 1)
                    for ofb in range(8):
                        for tt in range(NTT):
                            gidx = (half * 8 + ofb) * NTT + tt
                            sc.wait_ge(s_gmm, gidx + 1)
                            if gidx >= 4:
                                sc.wait_ge(s_gst4[gidx % 4], 16 * (gidx // 4))
                            sc.activation(gev[gidx % 4], bankB(gidx % 2),
                                          AF.Silu).then_inc(s_gev, 1)

                @blk.vector
                def _(v):
                    for t in range(NXT):
                        gi = half * NXT + t
                        v.wait_ge(s_ev, gi * NTB + NTB)
                        if gi >= 1:
                            v.wait_ge(s_sp, 16 * gi)
                        rb = rowbuf[gi % 2]
                        if half == 1:
                            v.tensor_copy(rb[:, 0:3], carry[:, 3 * t:3 * t + 3])
                        v.tensor_scalar(rowc[:, :], rb[:, 0:HALF],
                                        E["cw_sb"][:, 4 * t:4 * t + 1],
                                        E["cb_sb"][:, t:t + 1],
                                        ALU.mult, ALU.add)
                        for kk in range(1, 4):
                            v.scalar_tensor_tensor(
                                rowc[:, :], rb[:, kk:kk + HALF],
                                E["cw_sb"][:, 4 * t + kk:4 * t + kk + 1],
                                rowc[:, :], ALU.mult, ALU.add)
                        if half == 0:
                            v.tensor_copy(carry[:, 3 * t:3 * t + 3],
                                          rb[:, HALF:HALF + 3])
                        v.engine_nop().then_inc(s_cv, 1)
                    for tb in range(NTB):
                        didx = half * NTB + tb
                        v.wait_ge(s_dmm, didx + 1)
                        v.tensor_copy(
                            slotA[:, tok0 + tb * 512:tok0 + (tb + 1) * 512],
                            bankB(2 + didx % 2)[0:64, :]).then_inc(s_dev, 1)

        def vb(f):
            with nc.Block() as blk:
                blk.vector(f)

        def scb(f):
            with nc.Block() as blk:
                blk.scalar(f)

        vb(lambda v: v.tensor_scalar(slotA[:, :], slotA[:, :],
                                     E["dtb_sb"][:, :], None, ALU.add))
        scb(lambda sc: sc.activation(slotB[:, :], slotA[:, :], AF.Abs))
        scb(lambda sc: sc.activation(slotB[:, :], slotB[:, :], AF.Exp,
                                     scale=-1.0))
        scb(lambda sc: sc.activation(slotB[:, :], slotB[:, :], AF.Ln, bias=1.0))
        vb(lambda v: v.tensor_relu(slotC[:, :], slotA[:, :]))
        vb(lambda v: v.tensor_add(slotC[:, :], slotC[:, :], slotB[:, :]))
        vb(lambda v: v.tensor_scalar_mul(slotB[:, :], slotC[:, :],
                                         E["negA_sb"][:, :]))
        scb(lambda sc: sc.activation(slotC[:, :], slotC[:, :], AF.Ln))

        def scans(v):
            for ch in range(NCH):
                o = ch * Q
                v.tensor_tensor_scan(pack1[0:64, o:o + Q], slotB[:, o:o + Q],
                                     zseg[:, :], 0.0, ALU.add, ALU.add)
                rin = AP(slotB, o + Q - 1, [[S, 64], [-1, Q]])
                rz = AP(zseg, Q - 1, [[Q, 64], [-1, Q]])
                rout = AP(slotA, o + Q - 1, [[S, 64], [-1, Q]])
                v.tensor_tensor_scan(rout, rin, rz, 0.0, ALU.add, ALU.add)
        vb(scans)
        vb(lambda v: v.tensor_sub(slotA[:, :], slotA[:, :], slotB[:, :]))

        def finish_tabs(v):
            v.tensor_sub(slotB[:, :], slotC[:, :], pack1[0:64, :])
            v.tensor_add(pack2[0:64, :], slotA[:, :], slotC[:, :])
            v.tensor_copy(slast[:, :], AP(pack1, Q - 1, [[S, 64], [Q, NCH]]))
        vb(finish_tabs)
        scb(lambda sc: sc.activation(slast[:, :], slast[:, :], AF.Exp))
        s_nwm = sem("p2_nwm")
        with nc.Block() as blk:
            @blk.sync
            def _(sy):
                sy.dma_start(pack1[64:128, :], slotB[:, :]).then_inc(s_nwm, 16)
                sy.wait_ge(s_nwm, 16)

        with nc.Block() as blk:
            @blk.tensor
            def _(t_):
                t_.transpose(pA[0:NCH, 0:64], slast[:, :], E["id64_sb"][:, :])

        vb(lambda v: v.tensor_copy(decT[:, :], pA[0:NCH, 0:64]))

        with nc.Block() as blk:
            @blk.sync
            def _(sy):
                sy.dma_start(E["d_tabs"].ap()[:, :], pack1[0:64, :]
                             ).then_inc(s_tsp, 16)
                sy.dma_start(AP(E["d_tabsC"], 0,
                                [[Q, 64], [NH * Q, NCH], [1, Q]]),
                             pack1[0:64, :],
                             allow_slow_non_contiguous=True
                             ).then_inc(s_tsp, 16)
                sy.dma_start(E["d_dec"].ap()[:, :], decT[:, :]).then_inc(s_tsp, 16)
                sy.wait_ge(s_tsp, 48)


def build_p3(nc, sem, E, nch3=NCH):
    SKIP = set(os.environ.get("P3_SKIP", "").split(","))
    bankA, bankB, pA = E["bankA"], E["bankB"], E["pA"]
    pack1, pack2 = E["pack1"], E["pack2"]

    s_l1 = [sem("p3_l1a"), sem("p3_l1b")]
    s_l2 = [sem("p3_l2a"), sem("p3_l2b")]
    s_l3 = [sem(f"p3_l3{i}") for i in range(4)]
    s_l4 = sem("p3_l4")
    s_l5 = [sem("p3_l5a"), sem("p3_l5b")]
    s_ttr = sem("p3_ttr"); s_tev = sem("p3_tev"); s_ext = sem("p3_ext")
    s_gmm = sem("p3_gmm"); s_gme = sem("p3_gme")
    s_u4 = sem("p3_u4"); s_exp = sem("p3_exp"); s_m4 = sem("p3_m4")
    s_xs = sem("p3_xs")
    s_yim = sem("p3_yim"); s_yis = sem("p3_yis")
    s_yxm = sem("p3_yxm"); s_yfn = sem("p3_yfn")
    s_cmm = sem("p3_cmm"); s_stu = sem("p3_stu"); s_stc = sem("p3_stc")
    s_g = sem("p3_g"); s_rst = sem("p3_rst"); s_nrm = sem("p3_nrm")
    s_nst = sem("p3_nst"); s_cd = sem("p3_cd")
    s_wt3 = sem("p3_wt3")
    s_wtT = [sem("p3_wtT0"), sem("p3_wtT1")]
    s_wtS = [sem("p3_wtS0"), sem("p3_wtS1")]
    E["s_wtS"] = s_wtS
    WT_GU = []
    for mat_src, mat_dst in ((E["d_wgb"], E["d_wgT"]), (E["d_wub"], E["d_wuT"])):
        for k in range(KH):
            for hq in range(4):
                WT_GU.append((mat_src, mat_dst, k, hq))
    E["N_WT_GU"] = len(WT_GU)

    with ExitStack() as _st:
        DIall = _st.enter_context(nc.sbuf_tensor("DIall", [128, NH * 128], BF16))
        state = _st.enter_context(nc.sbuf_tensor("state", [128, NH * 64], F32))
        state_bf = _st.enter_context(nc.sbuf_tensor("state_bf", [128, NH * 64], BF16))
        xtokb = _st.enter_context(nc.sbuf_tensor("xtok", [128, 2 * 5120], BF16))
        bcFb = _st.enter_context(nc.sbuf_tensor("bcF", [128, 2 * 2048], BF16))
        sgch = _st.enter_context(nc.sbuf_tensor("sgch", [128, INTER], F32))
        decbcb = _st.enter_context(nc.sbuf_tensor("decbc", [128, 2 * 64], F32))
        ubcb = _st.enter_context(nc.sbuf_tensor("ubc", [128, 4 * 512], F32))
        tabT = _st.enter_context(nc.sbuf_tensor("tabT", [128, 192], F32))
        expsT = _st.enter_context(nc.sbuf_tensor("expsT", [128, 64], F32))
        erdtT = _st.enter_context(nc.sbuf_tensor("erdtT", [128, 64], F32))
        Gm = _st.enter_context(nc.sbuf_tensor("Gm", [128, 8 * 128], BF16))
        u4b = _st.enter_context(nc.sbuf_tensor("u4", [128, 2 * 512], F32))
        e4b = _st.enter_context(nc.sbuf_tensor("e4", [128, 2 * 512], BF16))
        M4 = _st.enter_context(nc.sbuf_tensor("M4", [128, NH * 128], BF16))
        xs8 = _st.enter_context(nc.sbuf_tensor("xs8", [128, NH * 64], BF16))
        ybig = _st.enter_context(nc.sbuf_tensor("ybig", [128, INTER], F32))
        tmpg = _st.enter_context(nc.sbuf_tensor("tmpg", [128, 512], F32))
        normbb = _st.enter_context(nc.sbuf_tensor("normbb", [128, 2 * INTER], BF16))
        wtb3b = _st.enter_context(nc.sbuf_tensor("wtb3", [128, DFF // 2], BF16,
                                                 is_transpose_dest=True))
        wtb3 = [wtb3b[:, 0:DFF // 4], wtb3b[:, DFF // 4:DFF // 2]]
        sscol = _st.enter_context(nc.sbuf_tensor("sscol", [128, 1], F32))
        rstdc = _st.enter_context(nc.sbuf_tensor("rstdc", [128, 1], F32))
        xtok = [xtokb[:, 0:5120], xtokb[:, 5120:10240]]
        bcF = [bcFb[:, 0:2048], bcFb[:, 2048:4096]]
        decbc = [decbcb[:, 0:64], decbcb[:, 64:128]]
        ubc = [ubcb[:, i * 512:(i + 1) * 512] for i in range(4)]
        u4 = [u4b[:, 0:512], u4b[:, 512:1024]]
        e4 = [e4b[:, 0:512], e4b[:, 512:1024]]
        normb = [normbb[:, 0:INTER], normbb[:, INTER:2 * INTER]]

        with nc.Block() as blk:
            @blk.gpsimd
            def _(g):
                g.memset(state[:, :], 0.0)
                g.memset(state_bf[:, :], 0.0)
                g.affine_select(
                    AP(DIall, 0, [[NH * 128, 128], [128, 64], [1, 128]]),
                    AP(E["dbcb_sb"], 0, [[64, 128], [1, 64], [0, 128]]),
                    [[0, 64], [1, 128]], ALU.is_equal, 0.0,
                    channel_multiplier=-1)

        with nc.Block() as blk:
            @blk.sync
            def _(sy):
                for ch in range(nch3):
                    e = ch % 2
                    if ch >= 2:
                        sy.wait_ge(s_cd, ch - 1)
                    sy.dma_start(
                        bcF[e],
                        AP(E["d_xbc"], INTER * S + ch * Q,
                           [[S, 128], [128 * S, 16], [1, Q]])).then_inc(s_l1[e], 16)
                    sy.dma_start_transpose(
                        xtok[e],
                        E["d_xbc"].ap()[0:5120, ch * Q:(ch + 1) * Q]
                    ).then_inc(s_l2[e], 16)
                    if ch >= 1:
                        sy.wait_ge(s_g, ch)
                    sy.dma_start(sgch[:, :],
                                 E["d_sg"].ap()[ch * Q:(ch + 1) * Q, :]
                                 ).then_inc(s_l4, 16)
                    sy.dma_start(decbc[e],
                                 AP(E["d_dec"], ch * 64, [[0, 128], [1, 64]])
                                 ).then_inc(s_l5[e], 16)
                    for q in range(16):
                        uq = ch * 16 + q
                        if uq >= 4:
                            sy.wait_ge(s_u4, uq - 3)
                        sy.dma_start(
                            ubc[q % 4],
                            AP(E["d_tabsC"], ch * NH * Q + (4 * q) * Q,
                               [[0, 128], [Q, 4], [1, Q]])
                        ).then_inc(s_l3[q % 4], 16)
                    if ch >= 1:
                        sy.wait_ge(s_nrm, ch)
                        sy.dma_start(E["d_norm"].ap()[(ch - 1) * Q:ch * Q, :],
                                     normb[(ch - 1) % 2][:, :]
                                     ).then_inc(s_nst, 16)
                        sy.wait_ge(s_nst, 16 * ch)
                        sy.sem_inc(s_cd, 1)
                    for j in range(4 * ch, min(4 * ch + 4, len(WT_GU))):
                        src_, dst, k, hq = WT_GU[j]
                        r0 = hq * (DFF // 4)
                        e2 = j % 2
                        if j == 0:
                            sy.wait_ge(E["s_pc"], 16 * E["NPC_EARLY"])
                        if j >= 2:
                            sy.wait_ge(s_wtS[e2], 16 * (j // 2))
                        sy.dma_start_transpose(
                            wtb3[e2],
                            src_.ap()[r0:r0 + DFF // 4, k * 128:(k + 1) * 128]
                        ).then_inc(s_wtT[e2], 16)
                        sy.wait_ge(s_wtT[e2], 16 * (j // 2 + 1))
                        sy.dma_start(
                            dst.ap()[k * 128:(k + 1) * 128, r0:r0 + DFF // 4],
                            wtb3[e2]).then_inc(s_wtS[e2], 16)
                sy.wait_ge(s_nrm, nch3)
                sy.dma_start(E["d_norm"].ap()[(nch3 - 1) * Q:nch3 * Q, :],
                             normb[(nch3 - 1) % 2][:, :]).then_inc(s_nst, 16)
                sy.wait_ge(s_nst, 16 * nch3)
                sy.sem_inc(s_cd, 1)

            @blk.tensor
            def _(t_):
                for ch in range(nch3):
                    e = ch % 2
                    t_.wait_ge(s_tev, 3 * ch)
                    t_.wait_ge(s_gme, 8 * ch)
                    if "tr" not in SKIP:
                        t_.transpose(pA[:, 0:128], pack1[:, ch * Q:(ch + 1) * Q],
                                     E["id128f"][:, :])
                        t_.transpose(pA[:, 128:256],
                                     pack2[:, ch * Q:(ch + 1) * Q],
                                     E["id128f"][:, :]).then_inc(s_ttr, 1)
                    else:
                        t_.nop().then_inc(s_ttr, 1)
                    t_.wait_ge(s_l1[e], 16 * (ch // 2 + 1))
                    t_.wait_ge(s_tev, 3 * ch + 3)
                    for g in range(8):
                        if "g" not in SKIP:
                            t_.matmul(
                                bankA(g // 4)[:, (g % 4) * 128:(g % 4 + 1) * 128],
                                bcF[e][:, g * 128:(g + 1) * 128],
                                bcF[e][:, 1024 + g * 128:1024 + (g + 1) * 128],
                                start=True, stop=True).then_inc(s_gmm, 1)
                        else:
                            t_.nop().then_inc(s_gmm, 1)
                    t_.wait_ge(s_stc, ch)
                    for g in range(8):
                        idx = ch * 8 + g
                        if idx >= 2:
                            t_.wait_ge(s_yis, idx - 1)
                        if "y" not in SKIP:
                            for hh in range(8):
                                h = g * 8 + hh
                                inst = t_.matmul(
                                    bankA(2 + g % 2)[:, hh * 64:(hh + 1) * 64],
                                    bcF[e][:, 1024 + g * 128:1024 + (g + 1) * 128],
                                    state_bf[:, h * 64:(h + 1) * 64],
                                    start=True, stop=True)
                            inst.then_inc(s_yim, 1)
                        else:
                            t_.nop().then_inc(s_yim, 1)
                    t_.wait_ge(s_l2[e], 16 * (ch // 2 + 1))
                    for g in range(8):
                        idx = ch * 8 + g
                        t_.wait_ge(s_m4, ch * 16 + 2 * (g + 1))
                        if idx >= 2:
                            t_.wait_ge(s_yfn, idx - 1)
                        if "y" not in SKIP:
                            for hh in range(8):
                                h = g * 8 + hh
                                inst = t_.matmul(
                                    bankB(g % 2)[:, hh * 64:(hh + 1) * 64],
                                    M4[:, h * 128:(h + 1) * 128],
                                    xtok[e][:, h * 64:(h + 1) * 64],
                                    start=True, stop=True)
                            inst.then_inc(s_yxm, 1)
                        else:
                            t_.nop().then_inc(s_yxm, 1)
                    for g in range(8):
                        idx = ch * 8 + g
                        t_.wait_ge(s_xs, idx + 1)
                        if idx >= 2:
                            t_.wait_ge(s_stu, idx - 1)
                        if "c" not in SKIP:
                            for hh in range(8):
                                h = g * 8 + hh
                                inst = t_.matmul(
                                    bankB(2 + g % 2)[:, hh * 64:(hh + 1) * 64],
                                    xtok[e][:, 4096 + g * 128:4096 + (g + 1) * 128],
                                    xs8[:, h * 64:(h + 1) * 64],
                                    start=True, stop=True)
                            inst.then_inc(s_cmm, 1)
                        else:
                            t_.nop().then_inc(s_cmm, 1)

            @blk.vector
            def _(v):
                for ch in range(nch3):
                    e = ch % 2
                    if ch >= 2:
                        v.wait_ge(s_cd, ch - 1)
                    v.wait_ge(s_ttr, ch + 1)
                    if ch >= 1:
                        v.wait_ge(s_ext, ch)
                    if ch >= 1:
                        v.wait_ge(s_u4, 16 * ch)
                    if "tr" not in SKIP:
                        v.tensor_copy(tabT[:, :], pA[:, 0:192])
                    v.engine_nop().then_inc(s_tev, 3)
                    for g in range(8):
                        v.wait_ge(s_gmm, ch * 8 + (4 if g < 4 else 8))
                        if "g" not in SKIP:
                            v.tensor_mul(
                                Gm[:, g * 128:(g + 1) * 128],
                                bankA(g // 4)[:, (g % 4) * 128:(g % 4 + 1) * 128],
                                E["caus_sb"][:, :]).then_inc(s_gme, 1)
                        else:
                            v.engine_nop().then_inc(s_gme, 1)
                    if ch >= 1:
                        v.wait_ge(s_yxm, 8 * ch)
                    for q in range(16):
                            v.wait_ge(s_exp, ch * 16 + q + 1)
                            g = q // 2
                            v.wait_ge(s_gme, ch * 8 + g + 1)
                            if "m" not in SKIP:
                                gm_ap = AP(Gm, g * 128,
                                           [[1024, 128], [0, 4], [1, 128]])
                                v.tensor_tensor(M4[:, q * 512:(q + 1) * 512],
                                                e4[q % 2], gm_ap, ALU.mult)
                                v.tensor_add(M4[:, q * 512:(q + 1) * 512],
                                             M4[:, q * 512:(q + 1) * 512],
                                             DIall[:, q * 512:(q + 1) * 512]
                                             ).then_inc(s_m4, 1)
                            else:
                                v.engine_nop().then_inc(s_m4, 1)
                    v.wait_ge(s_l2[e], 16 * (ch // 2 + 1))
                    v.wait_ge(s_ext, ch + 1)
                    if ch >= 1:
                        v.wait_ge(s_cmm, 8 * ch)
                    for g in range(8):
                        if "c" not in SKIP:
                            erd_ap = AP(erdtT, 8 * g,
                                        [[64, 128], [1, 8], [0, 64]])
                            v.tensor_mul(xs8[:, g * 512:(g + 1) * 512],
                                         xtok[e][:, g * 512:(g + 1) * 512],
                                         erd_ap).then_inc(s_xs, 1)
                        else:
                            v.engine_nop().then_inc(s_xs, 1)
                    for g in range(8):
                        v.wait_ge(s_yim, ch * 8 + g + 1)
                        if "y" not in SKIP:
                            ex_ap = AP(expsT, 8 * g,
                                       [[64, 128], [1, 8], [0, 64]])
                            v.tensor_mul(ybig[:, g * 512:(g + 1) * 512],
                                         bankA(2 + g % 2), ex_ap
                                         ).then_inc(s_yis, 1)
                        else:
                            v.engine_nop().then_inc(s_yis, 1)
                    for g in range(8):
                        v.wait_ge(s_yxm, ch * 8 + g + 1)
                        if "y" not in SKIP:
                            v.tensor_add(ybig[:, g * 512:(g + 1) * 512],
                                         bankB(g % 2),
                                         ybig[:, g * 512:(g + 1) * 512]
                                         ).then_inc(s_yfn, 1)
                        else:
                            v.engine_nop().then_inc(s_yfn, 1)
                    v.wait_ge(s_yim, ch * 8 + 8)
                    v.wait_ge(s_l5[e], 16 * (ch // 2 + 1))
                    for g in range(8):
                        v.wait_ge(s_cmm, ch * 8 + g + 1)
                        if "c" not in SKIP:
                            dec_ap = AP(decbcb, e * 64 + 8 * g,
                                        [[128, 128], [1, 8], [0, 64]])
                            v.tensor_mul(tmpg[:, :],
                                         state[:, g * 512:(g + 1) * 512],
                                         dec_ap)
                            v.tensor_add(state[:, g * 512:(g + 1) * 512],
                                         tmpg[:, :], bankB(2 + g % 2)
                                         ).then_inc(s_stu, 1)
                        else:
                            v.engine_nop().then_inc(s_stu, 1)
                    v.tensor_copy(state_bf[:, :], state[:, :]).then_inc(s_stc, 1)
                    v.wait_ge(s_l4, 16 * (ch + 1))
                    if "n" not in SKIP:
                        v.tensor_mul(ybig[:, :], ybig[:, :], sgch[:, :])
                    v.engine_nop().then_inc(s_g, 1)
                    v.wait_ge(s_rst, ch + 1)
                    if "n" not in SKIP:
                        v.tensor_mul(normb[e], ybig[:, :],
                                     AP(rstdc, 0, [[1, 128], [0, INTER]]))
                    v.engine_nop().then_inc(s_nrm, 1)

            @blk.gpsimd
            def _(gp):
                for ch in range(nch3):
                    gp.wait_ge(s_tev, 3 * ch + 3)
                    for q in range(16):
                        uq = ch * 16 + q
                        gp.wait_ge(s_l3[q % 4], 16 * (uq // 4 + 1))
                        if uq >= 2:
                            gp.wait_ge(s_exp, uq - 1)
                        if "u" not in SKIP:
                            for hh in range(4):
                                h = q * 4 + hh
                                inst = gp.tensor_scalar(
                                    u4[q % 2][:, hh * 128:(hh + 1) * 128],
                                    ubc[q % 4][:, hh * 128:(hh + 1) * 128],
                                    tabT[:, 64 + h:64 + h + 1], CLAMP,
                                    ALU.add, ALU.min)
                            inst.then_inc(s_u4, 1)
                        else:
                            gp.engine_nop().then_inc(s_u4, 1)

            @blk.scalar
            def _(sc):
                for ch in range(nch3):
                    e = ch % 2
                    if ch >= 1:
                        sc.wait_ge(s_yis, 8 * ch)
                        sc.wait_ge(s_xs, 8 * ch)
                    sc.wait_ge(s_tev, 3 * ch + 3)
                    if "tr" not in SKIP:
                        sc.activation(expsT[:, :], tabT[:, 0:64], AF.Exp)
                        sc.activation(erdtT[:, :], tabT[:, 128:192],
                                      AF.Exp).then_inc(s_ext, 1)
                    else:
                        sc.nop().then_inc(s_ext, 1)
                    for q in range(16):
                        sc.wait_ge(s_u4, ch * 16 + q + 1)
                        if ch * 16 + q >= 2:
                            sc.wait_ge(s_m4, ch * 16 + q - 1)
                        if "e" not in SKIP:
                            sc.activation(e4[q % 2], u4[q % 2],
                                          AF.Exp).then_inc(s_exp, 1)
                        else:
                            sc.nop().then_inc(s_exp, 1)
                    sc.wait_ge(s_g, ch + 1)
                    if ch >= 2:
                        sc.wait_ge(s_cd, ch - 1)
                    if "n" not in SKIP:
                        sc.activation(normb[e], ybig[:, :], AF.Square,
                                      accum_out=sscol[:, :])
                        sc.drain()
                        sc.activation(rstdc[:, :], sscol[:, :], AF.Ln,
                                      scale=1.0 / INTER, bias=EPS)
                        sc.drain()
                        sc.activation(rstdc[:, :], rstdc[:, :], AF.Exp,
                                      scale=-0.5)
                        sc.drain()
                        sc.nop().then_inc(s_rst, 1)
                    else:
                        sc.nop().then_inc(s_rst, 1)


def build_p4(nc, sem, E):
    SKIP = set(os.environ.get("P4_SKIP", "").split(","))
    pA, pB = E["pA"], E["pB"]
    s_pc, NPC = E["s_pc"], E["NPC"]

    s_wl = sem("p4_wl"); s_ws = sem("p4_ws"); s_nt = sem("p4_nt")
    s_h0 = sem("p4_h0"); s_mm = sem("p4_mm"); s_x0 = sem("p4_x0")
    s_st = sem("p4_st"); s_xc = sem("p4_xc"); s_rs = sem("p4_rs")
    s_ap = sem("p4_ap"); s_dbg = sem("p4_dbg")
    E["d_dbgln"] = nc.dram_tensor("dbgln", [S // 128 * 128, 16], F32,
                                  kind="ExternalOutput")
    s_sf = sem("p4_sf"); s_sb2 = sem("p4_sb")

    NSUP = 16
    TPS = S // NSUP
    TT = TPS // 128
    NTILE = S // 128


    with (
        nc.sbuf_tensor("wouts", [128, KI * 2048], BF16) as woutsb,
        nc.sbuf_tensor("nT", [128, 2 * KI * TPS], BF16) as nTb,
        nc.sbuf_tensor("h0t", [128, 2048], F32) as h0t,
        nc.sbuf_tensor("x0b", [128, 2 * 2048], F32) as x0bb,
        nc.sbuf_tensor("x1b", [128, 2 * 2048], F32) as x1bb,
        nc.sbuf_tensor("lncol", [128, 16], F32) as lncol,
    ):
        wouts = [woutsb[:, k * 2048:(k + 1) * 2048] for k in range(KI)]
        nT = [[nTb[:, (e * KI + k) * TPS:(e * KI + k + 1) * TPS]
               for k in range(KI)] for e in range(2)]
        x0b = [x0bb[:, 0:2048], x0bb[:, 2048:4096]]
        x1b = [x1bb[:, 0:2048], x1bb[:, 2048:4096]]

        def LC(par, i):
            return lncol[:, par * 8 + i:par * 8 + i + 1]

        def LCbc(par, i):
            return AP(lncol, par * 8 + i, [[16, 128], [0, H]])

        with nc.Block() as blk:
            @blk.sync
            def _(sy):
                sy.wait_ge(s_pc, 16 * NPC)
                for k in range(KI):
                    sy.dma_start(wouts[k],
                                 E["d_woutb"].ap()[k * 128:(k + 1) * 128, :]
                                 ).then_inc(s_wl, 16)
                for sp in range(NSUP):
                    if sp >= 2:
                        sy.wait_ge(s_mm, (sp - 1) * TT)
                    for k in range(KI):
                        sy.dma_start_transpose(
                            nT[sp % 2][k],
                            E["d_norm"].ap()[sp * TPS:(sp + 1) * TPS,
                                             k * 128:(k + 1) * 128]
                        ).then_inc(s_nt, 16)
                    for tt in range(TT):
                        gi = sp * TT + tt
                        if gi >= 1:
                            sy.wait_ge(s_x0, gi)
                        if gi >= 2:
                            sy.wait_ge(s_rs, gi - 1)
                        sy.dma_start(
                            h0t[:, :],
                            E["d_h0"].ap()[sp * TPS + tt * 128:
                                           sp * TPS + (tt + 1) * 128, :]
                        ).then_inc(s_h0, 16)
                        if gi >= 2:
                            sy.wait_ge(s_ap, gi - 1)
                            sy.dma_start(
                                E["d_x1f"].ap()[(gi - 2) * 128:
                                                (gi - 1) * 128, :],
                                x1b[gi % 2]).then_inc(s_sf, 16)
                for gi in range(NTILE - 2, NTILE):
                    sy.wait_ge(s_ap, gi + 1)
                    sy.dma_start(E["d_x1f"].ap()[gi * 128:(gi + 1) * 128, :],
                                 x1b[gi % 2]).then_inc(s_sf, 16)
                sy.wait_ge(s_sf, 16 * NTILE)

            @blk.tensor
            def _(t_):
                t_.wait_ge(s_ws, KI)
                for sp in range(NSUP):
                    t_.wait_ge(s_nt, 16 * KI * (sp + 1))
                    for tt in range(TT):
                        gi = sp * TT + tt
                        psum = pA if gi % 2 == 0 else pB
                        if gi >= 2:
                            t_.wait_ge(s_x0, gi - 1)
                        for db in range(4):
                            for k in range(KI):
                                inst = t_.matmul(
                                    psum[:, db * 512:(db + 1) * 512],
                                    nT[sp % 2][k][:, tt * 128:(tt + 1) * 128],
                                    wouts[k][:, db * 512:(db + 1) * 512],
                                    start=(k == 0), stop=(k == KI - 1))
                        inst.then_inc(s_mm, 1)

            @blk.vector
            def _(v):
                v.wait_ge(s_wl, 16 * KI)
                for k in range(KI):
                    v.tensor_scalar_mul(wouts[k], wouts[k],
                                        E["nw_sb"][:, k:k + 1]).then_inc(s_ws, 1)
                for gi in range(NTILE):
                    par = gi % 2
                    v.wait_ge(s_mm, gi + 1)
                    v.wait_ge(s_h0, 16 * (gi + 1))
                    if gi >= 2:
                        v.wait_ge(s_ap, gi - 1)
                    psum = pA if par == 0 else pB
                    v.tensor_add(x0b[par], psum[:, :], h0t[:, :]).then_inc(s_x0, 1)
                    v.wait_ge(s_st, gi + 1)
                    v.scalar_tensor_tensor(x1b[par],
                                           LCbc(par, 0), 1.0 / H, x0b[par],
                                           ALU.mult, ALU.subtract
                                           ).then_inc(s_xc, 1)
                    v.wait_ge(s_rs, gi + 1)
                    v.tensor_mul(x1b[par], x1b[par], LCbc(par, 6)
                                 ).then_inc(s_ap, 1)

            @blk.scalar
            def _(sc):
                for gi in range(NTILE):
                    par = gi % 2
                    sc.wait_ge(s_x0, gi + 1)
                    if gi >= 2:
                        sc.wait_ge(s_sf, 16 * (gi - 1))
                        sc.wait_ge(s_sb2, 16 * (gi - 1))
                    sc.activation(x1b[par], x0b[par], AF.Copy,
                                  accum_out=LC(par, 0))
                    sc.drain()
                    sc.nop().then_inc(s_st, 1)
                    sc.wait_ge(s_xc, gi + 1)
                    sc.activation(x0b[par], x1b[par], AF.Square,
                                  accum_out=LC(par, 1))
                    sc.drain()
                    sc.activation(LC(par, 4), LC(par, 1), AF.Ln,
                                  scale=1.0 / H, bias=EPS)
                    sc.drain()
                    sc.activation(LC(par, 5), LC(par, 4), AF.Exp, scale=-0.5)
                    sc.drain()
                    sc.activation(LC(par, 6), LC(par, 5), AF.Copy, scale=-1.0)
                    sc.drain()
                    sc.nop().then_inc(s_rs, 1)
                    if "dbgln" in SKIP:
                        sc.dma_start(E["d_dbgln"].ap()[gi * 128:(gi + 1) * 128, :],
                                     lncol[:, :]).then_inc(s_dbg, 16)

            @blk.gpsimd
            def _(g):
                for gi in range(NTILE):
                    g.wait_ge(s_ap, gi + 1)
                    g.dma_start(E["d_x1b"].ap()[gi * 128:(gi + 1) * 128, :],
                                x1b[gi % 2]).then_inc(s_sb2, 16)
                g.wait_ge(s_sb2, 16 * NTILE)


def build_p5(nc, sem, E):
    bankA, bankB = E["bankA"], E["bankB"]

    s_xt = sem("p5_xt"); s_gm = sem("p5_gm")
    s_wl = [sem("p5_wla"), sem("p5_wlb")]
    s_um = sem("p5_um"); s_se = sem("p5_se"); s_gd = sem("p5_gd")
    s_gs = [sem("p5_gsa"), sem("p5_gsb")]; s_wt = sem("p5_wt")

    NOF = DFF // 128
    TB = S // 512

    WT_JOBS = [(E["d_wdb"], E["d_wdT"], k) for k in range(NOF)]
    E["s_wt5"] = s_wt

    def wT_lhsT_ap(dw, o):
        return AP(dw, o * 128, [[DFF, 128], [128 * DFF, KH], [1, 128]])

    with (
        nc.sbuf_tensor("x1T", [128, KH * S], BF16) as x1Tb,
        nc.sbuf_tensor("wgb5", [128, 2 * KH * 128], BF16) as wgb5,
        nc.sbuf_tensor("wub5", [128, 2 * KH * 128], BF16) as wub5,
        nc.sbuf_tensor("sgb", [128, 4 * 512], BF16) as sgbb,
        nc.sbuf_tensor("gtd", [128, 2 * S], BF16) as gtdb,
        nc.sbuf_tensor("wtb5", [128, 2 * H], BF16, is_transpose_dest=True
                       ) as wtb5,
    ):
        x1T = [x1Tb[:, k * S:(k + 1) * S] for k in range(KH)]
        wgb = [wgb5[:, 0:KH * 128], wgb5[:, KH * 128:2 * KH * 128]]
        wub = [wub5[:, 0:KH * 128], wub5[:, KH * 128:2 * KH * 128]]
        sgb = [sgbb[:, i * 512:(i + 1) * 512] for i in range(4)]
        gtd = [gtdb[:, 0:S], gtdb[:, S:2 * S]]
        wtb = [wtb5[:, 0:H], wtb5[:, H:2 * H]]

        with nc.Block() as blk:
            @blk.sync
            def _(sy):
                for k in range(KH):
                    sy.dma_start_transpose(
                        x1T[k],
                        E["d_x1b"].ap()[0:S, k * 128:(k + 1) * 128]
                    ).then_inc(s_xt, 16)
                sy.wait_ge(E["s_wtS"][0], 16 * (E["N_WT_GU"] // 2))
                sy.wait_ge(E["s_wtS"][1], 16 * (E["N_WT_GU"] // 2))
                for o in range(NOF):
                    if o >= 2:
                        sy.wait_ge(s_gm, (o - 1) * TB)
                    sy.dma_start(wgb[o % 2][:, :],
                                 wT_lhsT_ap(E["d_wgT"], o)
                                 ).then_inc(s_wl[o % 2], 16)
                    sy.dma_start(wub[o % 2][:, :],
                                 wT_lhsT_ap(E["d_wuT"], o)
                                 ).then_inc(s_wl[o % 2], 16)
                    if o >= 2:
                        sy.wait_ge(s_gd, (o - 1) * TB)
                        sy.dma_start(
                            E["d_gated"].ap()[(o - 2) * 128:(o - 1) * 128, :],
                            gtd[o % 2]).then_inc(s_gs[o % 2], 16)
                    src_, dst, k = WT_JOBS[o]
                    sy.dma_start_transpose(
                        wtb[0],
                        src_.ap()[0:H, k * 128:(k + 1) * 128]).then_inc(s_wt, 16)
                    sy.wait_ge(s_wt, 32 * o + 16)
                    sy.dma_start(dst.ap()[k * 128:(k + 1) * 128, :],
                                 wtb[0]).then_inc(s_wt, 16)
                    sy.wait_ge(s_wt, 32 * (o + 1))
                for o in range(NOF - 2, NOF):
                    sy.wait_ge(s_gd, o * TB + TB)
                    sy.dma_start(E["d_gated"].ap()[o * 128:(o + 1) * 128, :],
                                 gtd[o % 2]).then_inc(s_gs[o % 2], 16)
                for i in range(2):
                    sy.wait_ge(s_gs[i], 16 * NOF // 2)

            @blk.tensor
            def _(t_):
                t_.wait_ge(s_xt, 16 * KH)
                for o in range(NOF):
                    t_.wait_ge(s_wl[o % 2], 32 * (o // 2 + 1))
                    for tb in range(TB):
                        idx = o * TB + tb
                        if idx >= 2:
                            t_.wait_ge(s_se, idx - 1)
                        for k in range(KH):
                            inst = t_.matmul(
                                bankA(idx % 2),
                                wgb[o % 2][:, k * 128:(k + 1) * 128],
                                x1T[k][:, tb * 512:(tb + 1) * 512],
                                start=(k == 0), stop=(k == KH - 1))
                        inst.then_inc(s_gm, 1)
                        if idx >= 2:
                            t_.wait_ge(s_gd, idx - 1)
                        for k in range(KH):
                            inst = t_.matmul(
                                bankB(idx % 2),
                                wub[o % 2][:, k * 128:(k + 1) * 128],
                                x1T[k][:, tb * 512:(tb + 1) * 512],
                                start=(k == 0), stop=(k == KH - 1))
                        inst.then_inc(s_um, 1)

            @blk.scalar
            def _(sc):
                for o in range(NOF):
                    for tb in range(TB):
                        idx = o * TB + tb
                        sc.wait_ge(s_gm, idx + 1)
                        if idx >= 4:
                            sc.wait_ge(s_gd, idx - 3)
                        sc.activation(sgb[idx % 4], bankA(idx % 2),
                                      AF.Silu).then_inc(s_se, 1)


            @blk.vector
            def _(v):
                for o in range(NOF):
                    for tb in range(TB):
                        idx = o * TB + tb
                        v.wait_ge(s_um, idx + 1)
                        v.wait_ge(s_se, idx + 1)
                        if o >= 2 and tb == 0:
                            v.wait_ge(s_gs[o % 2], 16 * (o // 2))
                        v.tensor_mul(gtd[o % 2][:, tb * 512:(tb + 1) * 512],
                                     sgb[idx % 4], bankB(idx % 2)
                                     ).then_inc(s_gd, 1)


def build_p6(nc, sem, E):
    pA, pB = E["pA"], E["pB"]

    s_wl = sem("p6_wl"); s_mm = sem("p6_mm")
    s_gc = [sem("p6_gca"), sem("p6_gcb")]
    s_ev = sem("p6_ev"); s_ps = sem("p6_ps")
    s_pl = sem("p6_pl"); s_xl = sem("p6_xl")
    s_x0 = sem("p6_x0"); s_st = sem("p6_st"); s_xc = sem("p6_xc")
    s_rs = sem("p6_rs"); s_ap = sem("p6_ap")
    s_os = [sem("p6_osa"), sem("p6_osb")]

    NTILE = S // 128
    NHC = DFF // 128

    def gcol_ap(tt):
        return AP(E["d_gated"], tt * 128, [[S, 128], [128 * S, NHC], [1, 128]])

    with (
        nc.sbuf_tensor("wdh", [128, NHC * 1024], BF16) as wdhb,
        nc.sbuf_tensor("gcol", [128, 2 * DFF], BF16) as gcolb,
        nc.sbuf_tensor("prt", [128, 1024], F32) as prt,
        nc.sbuf_tensor("x1t6", [128, 2048], F32) as x1t6,
        nc.sbuf_tensor("dfl", [128, 2048], F32) as dfl,
        nc.sbuf_tensor("ob6", [128, 2 * 2048], F32) as ob6b,
        nc.sbuf_tensor("lncol6", [128, 16], F32) as lncol6,
    ):
        wdh = [wdhb[:, h * 1024:(h + 1) * 1024] for h in range(NHC)]
        gcol = [gcolb[:, 0:DFF], gcolb[:, DFF:2 * DFF]]
        ob6 = [ob6b[:, 0:2048], ob6b[:, 2048:4096]]

        def LC(par, i):
            return lncol6[:, par * 8 + i:par * 8 + i + 1]

        def LCbc(par, i):
            return AP(lncol6, par * 8 + i, [[16, 128], [0, H]])

        for dpass in range(2):
            with nc.Block() as blk:
                @blk.sync
                def _(sy):
                    if dpass == 0:
                        sy.wait_ge(E["s_wt5"], 32 * (DFF // 128))
                    for h in range(NHC):
                        sy.dma_start(
                            wdh[h],
                            E["d_wdT"].ap()[h * 128:(h + 1) * 128,
                                            dpass * 1024:(dpass + 1) * 1024]
                        ).then_inc(s_wl, 16)
                    for tt in range(NTILE):
                        gi = dpass * NTILE + tt
                        if tt >= 2:
                            sy.wait_ge(s_mm, gi - 1)
                        sy.dma_start(gcol[tt % 2], gcol_ap(tt)
                                     ).then_inc(s_gc[tt % 2], 16)
                        if dpass == 1:
                            if tt >= 1:
                                sy.wait_ge(s_x0, tt)
                            sy.dma_start(
                                x1t6[:, :],
                                E["d_x1f"].ap()[tt * 128:(tt + 1) * 128, :]
                            ).then_inc(s_xl, 16)
                            sy.dma_start(
                                prt[:, :],
                                E["d_part0"].ap()[tt * 128:(tt + 1) * 128, :]
                            ).then_inc(s_pl, 16)

                @blk.tensor
                def _(t_):
                    t_.wait_ge(s_wl, 16 * NHC * (dpass + 1))
                    for tt in range(NTILE):
                        gi = dpass * NTILE + tt
                        t_.wait_ge(s_gc[tt % 2],
                                   16 * (dpass * NTILE // 2 + tt // 2 + 1))
                        psum = pA[:, 0:1024] if tt % 2 == 0 else pB[:, 0:1024]
                        if tt >= 2:
                            if dpass == 0:
                                t_.wait_ge(s_ev, tt - 1)
                            else:
                                t_.wait_ge(s_x0, tt - 1)
                        for h in range(NHC):
                            for n in range(2):
                                inst = t_.matmul(
                                    psum[:, n * 512:(n + 1) * 512],
                                    gcol[tt % 2][:, h * 128:(h + 1) * 128],
                                    wdh[h][:, n * 512:(n + 1) * 512],
                                    start=(h == 0), stop=(h == NHC - 1))
                        inst.then_inc(s_mm, 1)

                if dpass == 0:
                    @blk.vector
                    def _(v):
                        for tt in range(NTILE):
                            v.wait_ge(s_mm, tt + 1)
                            if tt >= 1:
                                v.wait_ge(s_ps, 16 * tt)
                            psum = (pA[:, 0:1024] if tt % 2 == 0
                                    else pB[:, 0:1024])
                            v.tensor_copy(prt[:, :], psum).then_inc(s_ev, 1)

                    @blk.scalar
                    def _(sc):
                        for tt in range(NTILE):
                            sc.wait_ge(s_ev, tt + 1)
                            sc.dma_start(
                                E["d_part0"].ap()[tt * 128:(tt + 1) * 128, :],
                                prt[:, :]).then_inc(s_ps, 16)
                        sc.wait_ge(s_ps, 16 * NTILE)
                else:
                    @blk.vector
                    def _(v):
                        for tt in range(NTILE):
                            par = tt % 2
                            v.wait_ge(s_mm, NTILE + tt + 1)
                            v.wait_ge(s_xl, 16 * (tt + 1))
                            v.wait_ge(s_pl, 16 * (tt + 1))
                            if tt >= 1:
                                v.wait_ge(s_rs, tt)
                            psum = (pA[:, 0:1024] if par == 0
                                    else pB[:, 0:1024])
                            v.tensor_add(dfl[:, 0:1024], prt[:, :],
                                         x1t6[:, 0:1024])
                            v.tensor_add(dfl[:, 1024:2048], psum,
                                         x1t6[:, 1024:2048]).then_inc(s_x0, 1)
                            v.wait_ge(s_st, tt + 1)
                            v.scalar_tensor_tensor(ob6[par], LCbc(par, 0),
                                                   1.0 / H, dfl[:, :],
                                                   ALU.mult, ALU.subtract
                                                   ).then_inc(s_xc, 1)
                            v.wait_ge(s_rs, tt + 1)
                            v.tensor_mul(ob6[par], ob6[par], LCbc(par, 6)
                                         ).then_inc(s_ap, 1)

                if dpass == 1:
                    @blk.scalar
                    def _(sc):
                        for tt in range(NTILE):
                            par = tt % 2
                            sc.wait_ge(s_x0, tt + 1)
                            if tt >= 2:
                                sc.wait_ge(s_os[tt % 2], 16 * (tt // 2))
                            sc.activation(ob6[par], dfl[:, :], AF.Copy,
                                          accum_out=LC(par, 0))
                            sc.drain()
                            sc.nop().then_inc(s_st, 1)
                            sc.wait_ge(s_xc, tt + 1)
                            sc.activation(dfl[:, :], ob6[par], AF.Square,
                                          accum_out=LC(par, 1))
                            sc.drain()
                            sc.activation(LC(par, 4), LC(par, 1), AF.Ln,
                                          scale=1.0 / H, bias=EPS)
                            sc.drain()
                            sc.activation(LC(par, 5), LC(par, 4), AF.Exp,
                                          scale=-0.5)
                            sc.drain()
                            sc.activation(LC(par, 6), LC(par, 5), AF.Copy,
                                          scale=-1.0)
                            sc.drain()
                            sc.nop().then_inc(s_rs, 1)
                            sc.wait_ge(s_ap, tt + 1)
                            sc.dma_start(
                                E["d_out"].ap()[tt * 128:(tt + 1) * 128, :],
                                ob6[par]).then_inc(s_os[tt % 2], 16)
                        for i in range(2):
                            sc.wait_ge(s_os[i], 16 * NTILE // 2)


INPUT_MAP = [
    ("h0", "L0_attn_hidden_states"), ("wz", "L0_attn_Wz"), ("wx", "L0_attn_Wx"),
    ("wb", "L0_attn_WB"), ("wc", "L0_attn_WC"), ("wdt", "L0_attn_Wdt"),
    ("convw", "L0_attn_conv_w"), ("convb", "L0_attn_conv_b"),
    ("dtb", "L0_attn_dt_bias"), ("alog", "L0_attn_A_log"),
    ("dvec", "L0_attn_D"), ("nw", "L0_attn_norm_w"), ("wout", "L0_attn_Wout"),
    ("wgate", "L0_ffn_w_gate"), ("wup", "L0_ffn_w_up"), ("wdown", "L0_ffn_w_down"),
]


def make_in_maps(inputs, n_cores=8):
    hs = np.ascontiguousarray(np.asarray(inputs["L0_attn_hidden_states"],
                                         dtype=np.float32))
    shared = {}
    for dev, ref in INPUT_MAP:
        if dev == "h0":
            continue
        shared[dev] = np.ascontiguousarray(np.asarray(inputs[ref],
                                                      dtype=np.float32))
    maps = []
    for b in range(n_cores):
        m = dict(shared)
        m["h0"] = np.ascontiguousarray(hs[b])
        maps.append(m)
    return maps


def kernel(**inputs):
    nc = build_program()
    maps = make_in_maps(inputs)
    res = run_bass_kernel_spmd(nc, maps, core_ids=list(range(8)))
    out = np.stack([np.asarray(r["out"], dtype=np.float32)
                    for r in res.results], axis=0)
    return out


# version 06
# speedup vs baseline: 1.0903x; 1.0903x over previous
import os
from contextlib import ExitStack

import numpy as np
import concourse.bass as bass
import concourse.mybir as mybir
from concourse.bass_utils import run_bass_kernel_spmd

F32 = mybir.dt.float32
BF16 = mybir.dt.bfloat16
AF = mybir.ActivationFunctionType
ALU = mybir.AluOpType
AP = bass.AP

S = 4096
H = 2048
INTER = 4096
NH, HD, NS, NG = 64, 64, 128, 8
GN = NG * NS
CONV = INTER + 2 * GN
DFF = 8192
Q = 128
NCH = S // Q
KH = H // 128
KI = INTER // 128
NXT = CONV // 128
HALF = S // 2
NTB = HALF // 512
NTT = HALF // 128
EPS = 1e-5
CLAMP = 50.0


def build_program(dbg=(), phases=6, nch3=NCH):
    nc = bass.Bass("TRN2", target_bir_lowering=False,
                   detect_race_conditions=False)
    dbg = set(dbg)
    E = {}

    def sem(name):
        return nc.alloc_semaphore(name=name)

    def scratch(name, shape, dtype):
        kind = "ExternalOutput" if name in dbg else "Internal"
        return nc.dram_tensor(name, shape, dtype, kind=kind)

    E["d_h0"] = nc.dram_tensor("h0", [S, H], F32, kind="ExternalInput")
    E["d_wz"] = nc.dram_tensor("wz", [H, INTER], F32, kind="ExternalInput")
    E["d_wx"] = nc.dram_tensor("wx", [H, INTER], F32, kind="ExternalInput")
    E["d_wb"] = nc.dram_tensor("wb", [H, GN], F32, kind="ExternalInput")
    E["d_wc"] = nc.dram_tensor("wc", [H, GN], F32, kind="ExternalInput")
    E["d_wdt"] = nc.dram_tensor("wdt", [H, NH], F32, kind="ExternalInput")
    E["d_cw"] = nc.dram_tensor("convw", [CONV, 4], F32, kind="ExternalInput")
    E["d_cb"] = nc.dram_tensor("convb", [CONV], F32, kind="ExternalInput")
    E["d_dtb"] = nc.dram_tensor("dtb", [NH], F32, kind="ExternalInput")
    E["d_alog"] = nc.dram_tensor("alog", [NH], F32, kind="ExternalInput")
    E["d_dvec"] = nc.dram_tensor("dvec", [NH], F32, kind="ExternalInput")
    E["d_nw"] = nc.dram_tensor("nw", [INTER], F32, kind="ExternalInput")
    E["d_wout"] = nc.dram_tensor("wout", [INTER, H], F32, kind="ExternalInput")
    E["d_wg"] = nc.dram_tensor("wgate", [DFF, H], F32, kind="ExternalInput")
    E["d_wu"] = nc.dram_tensor("wup", [DFF, H], F32, kind="ExternalInput")
    E["d_wd"] = nc.dram_tensor("wdown", [H, DFF], F32, kind="ExternalInput")
    E["d_out"] = nc.dram_tensor("out", [S, H], F32, kind="ExternalOutput")

    E["d_hbf"] = scratch("h_bf", [S, H], BF16)
    E["d_woutb"] = scratch("wout_bf", [INTER, H], BF16)
    E["d_wgb"] = scratch("wg_bf", [DFF, H], BF16)
    E["d_wub"] = scratch("wu_bf", [DFF, H], BF16)
    E["d_wdb"] = scratch("wd_bf", [H, DFF], BF16)
    E["d_wgT"] = scratch("wgT", [H, DFF], BF16)
    E["d_wuT"] = scratch("wuT", [H, DFF], BF16)
    E["d_wdT"] = scratch("wdT", [DFF, H], BF16)
    E["d_xbc"] = scratch("xbc_bf", [CONV, S], BF16)
    E["d_sg"] = scratch("sg_h", [S, INTER], F32)
    E["d_tabs"] = scratch("tab_s", [NH, S], F32)
    E["d_tabsC"] = scratch("tab_sc", [NCH * NH, Q], F32)
    E["d_dec"] = scratch("dec_h", [NCH, NH], F32)
    E["d_norm"] = scratch("normed_h", [S, INTER], BF16)
    E["d_x1f"] = scratch("x1f_h", [S, H], F32)
    E["d_x1b"] = scratch("x1b_h", [S, H], BF16)
    E["d_gated"] = scratch("gated_h", [DFF, S], BF16)
    E["d_part0"] = scratch("part0_h", [S, H // 2], F32)

    pA = nc.alloc_psum_tensor("pA", [128, 2048], F32)
    pB = nc.alloc_psum_tensor("pB", [128, 2048], F32)
    E["pA"], E["pB"] = pA, pB
    E["bankA"] = lambda i: pA[:, i * 512:(i + 1) * 512]
    E["bankB"] = lambda i: pB[:, i * 512:(i + 1) * 512]

    sb = nc.alloc_sbuf_tensor
    E["cw_sb"] = sb("cw_sb", [128, 4 * NXT], F32)
    E["cb_sb"] = sb("cb_sb", [128, NXT], F32)
    E["dtb_sb"] = sb("dtb_sb", [64, 1], F32)
    E["alog_sb"] = sb("alog_sb", [64, 1], F32)
    E["negA_sb"] = sb("negA_sb", [64, 1], F32)
    E["dbcb_sb"] = sb("dbcb_sb", [128, 64], BF16)
    E["dbcf_sb"] = sb("dbcf_sb", [128, 64], F32)
    E["nw_sb"] = sb("nw_sb", [128, KI], F32)
    E["onesb_sb"] = sb("onesb_sb", [128, 128], BF16)
    E["onesf_sb"] = sb("onesf_sb", [64, 64], F32)
    E["onesf128"] = sb("onesf128", [128, 128], F32)
    E["id128f"] = sb("id128f", [128, 128], F32)
    E["caus_sb"] = sb("caus_sb", [128, 128], BF16)
    E["id64_sb"] = sb("id64_sb", [64, 64], F32)
    E["s_pc"] = sem("pc")

    CONSTS = [EPS, -0.5, -1.0, 1.0 / INTER, 1.0 / H, CLAMP]
    E["const_tiles"] = []
    for i, val in enumerate(CONSTS):
        t = sb(f"cst{i}", [128, 1], F32)
        nc.const_aps.register(F32, val, t[:, :])
        E["const_tiles"].append((t, val))

    with nc.named_scope("P1"):
        build_p1(nc, sem, E)
    if phases >= 2:
        with (
            nc.sbuf_tensor("pack1", [128, S], F32) as pack1,
            nc.sbuf_tensor("pack2", [128, S], F32) as pack2,
        ):
            E["pack1"], E["pack2"] = pack1, pack2
            with nc.named_scope("P2"):
                build_p2(nc, sem, E)
            if phases >= 3:
                with nc.named_scope("P3"):
                    build_p3(nc, sem, E, nch3)
    if phases >= 4:
        with nc.named_scope("P4"):
            build_p4(nc, sem, E)
    if phases >= 5:
        with nc.named_scope("P5"):
            build_p5(nc, sem, E)
    if phases >= 6:
        with nc.named_scope("P6"):
            build_p6(nc, sem, E)
    if phases >= 2:
        with nc.Block() as blk:
            @blk.gpsimd
            def _(g):
                g.wait_ge(E["s_pc"], 16 * E["NPC"])
    return nc


def build_p1(nc, sem, E):
    s_ld = sem("p1_ld")
    E["s_hc"] = s_hc = sem("p1_hc")

    with nc.Block() as blk:
        @blk.sync
        def _(sy):
            n = 0
            sy.dma_start(E["cw_sb"][:, :],
                         AP(E["d_cw"], 0, [[4, 128], [512, NXT], [1, 4]])
                         ).then_inc(s_ld, 16); n += 16
            sy.dma_start(E["cb_sb"][:, :],
                         AP(E["d_cb"], 0, [[1, 128], [128, NXT]]),
                         allow_slow_non_contiguous=True).then_inc(s_ld, 16); n += 16
            sy.dma_start(E["dtb_sb"][:, :], AP(E["d_dtb"], 0, [[1, 64], [1, 1]])
                         ).then_inc(s_ld, 16); n += 16
            sy.dma_start(E["alog_sb"][:, :], AP(E["d_alog"], 0, [[1, 64], [1, 1]])
                         ).then_inc(s_ld, 16); n += 16
            sy.dma_start(E["dbcf_sb"][:, :], AP(E["d_dvec"], 0, [[0, 128], [1, 64]])
                         ).then_inc(s_ld, 16); n += 16
            sy.dma_start(E["nw_sb"][:, :], AP(E["d_nw"], 0, [[1, 128], [128, KI]]),
                         allow_slow_non_contiguous=True).then_inc(s_ld, 16); n += 16
            sy.wait_ge(s_ld, n)

        @blk.gpsimd
        def _(g):
            for i in range(4):
                g.dma_start(E["d_hbf"].ap()[i * 1024:(i + 1) * 1024, :],
                            E["d_h0"].ap()[i * 1024:(i + 1) * 1024, :]
                            ).then_inc(s_hc, 16)
            g.memset(E["onesb_sb"][:, :], 1.0)
            g.memset(E["onesf_sb"][:, :], 1.0)
            g.memset(E["onesf128"][:, :], 1.0)
            for t, val in E["const_tiles"]:
                g.memset(t[:, :], val)

    with nc.Block() as blk:
        @blk.gpsimd
        def _(g):
            g.affine_select(E["caus_sb"][:, :], E["onesb_sb"][:, :], [[1, 128]],
                            ALU.is_ge, 0.0, channel_multiplier=-1)
            g.affine_select(E["id64_sb"][:, :], E["onesf_sb"][:, :], [[1, 64]],
                            ALU.is_equal, 0.0, channel_multiplier=-1)
            g.affine_select(E["id128f"][:, :], E["onesf128"][:, :], [[1, 128]],
                            ALU.is_equal, 0.0, channel_multiplier=-1)

        @blk.scalar
        def _(sc):
            sc.activation(E["negA_sb"][:, :], E["alog_sb"][:, :], AF.Exp)

        @blk.vector
        def _(v):
            v.tensor_copy(E["dbcb_sb"][:, :], E["dbcf_sb"][:, :])

    with nc.Block() as blk:
        @blk.vector
        def _(v):
            v.tensor_scalar_mul(E["negA_sb"][:, :], E["negA_sb"][:, :], -1.0)


def build_p2(nc, sem, E):
    bankA, bankB, pA = E["bankA"], E["bankB"], E["pA"]
    pack1, pack2 = E["pack1"], E["pack2"]
    s_pc, s_hc = E["s_pc"], E["s_hc"]

    s_ht = sem("p2_ht"); s_mm = sem("p2_mm")
    s_wld2 = [sem("p2_wld0"), sem("p2_wld1")]; s_wldt = sem("p2_wldt")
    s_ev = sem("p2_ev"); s_cv = sem("p2_cv"); s_si = sem("p2_si")
    s_sp = sem("p2_sp"); s_gld = sem("p2_gld"); s_gmm = sem("p2_gmm")
    s_gev = sem("p2_gev"); s_dmm = sem("p2_dmm")
    s_gst4 = [sem(f"p2_gst{i}") for i in range(4)]
    s_dev = sem("p2_dev"); s_tsp = sem("p2_tsp")

    W_SRCS = []
    for t in range(NXT):
        if t < 32:
            W_SRCS.append((E["d_wx"], t * 128, INTER))
        elif t < 40:
            W_SRCS.append((E["d_wb"], (t - 32) * 128, GN))
        else:
            W_SRCS.append((E["d_wc"], (t - 40) * 128, GN))

    PCH = ([(E["d_wg"], E["d_wgb"], i * 1024) for i in range(8)]
           + [(E["d_wu"], E["d_wub"], i * 1024) for i in range(8)])
    PCH_LATE = ([(E["d_wout"], E["d_woutb"], i * 1024) for i in range(4)]
                + [(E["d_wd"], E["d_wdb"], i * 1024) for i in range(2)])
    E["NPC"] = NPC = len(PCH) + len(PCH_LATE)
    E["NPC_EARLY"] = NPC_EARLY = len(PCH)

    def w_lhsT_ap(dw, of0, ncols):
        return AP(dw, of0, [[ncols, 128], [128 * ncols, KH], [1, 128]])

    def gw_rhs_ap(dw, ofb, ncols):
        return AP(dw, ofb * 512, [[ncols, 128], [128 * ncols, KH], [1, 512]])


    with ExitStack() as _st:
        hTb = _st.enter_context(nc.sbuf_tensor("hT", [128, KH * HALF], BF16))
        rb0 = _st.enter_context(nc.sbuf_tensor("rowb0", [128, HALF + 8], F32))
        rb1 = _st.enter_context(nc.sbuf_tensor("rowb1", [128, HALF + 8], F32))
        rowc = _st.enter_context(nc.sbuf_tensor("rowc", [128, HALF], F32))
        wxbb = _st.enter_context(nc.sbuf_tensor("wxb", [128, 2 * KH * 128], BF16))
        gwbb = _st.enter_context(nc.sbuf_tensor("gwb", [128, KH * 512], BF16))
        gevb = _st.enter_context(nc.sbuf_tensor("gev", [128, 4 * 512], F32))
        wdtb = _st.enter_context(nc.sbuf_tensor("wdtb", [128, KH * 64], BF16))
        carry = _st.enter_context(nc.sbuf_tensor("carry", [128, 3 * NXT], F32))
        slotA = _st.enter_context(nc.sbuf_tensor("slotA", [64, S], F32))
        slotB = _st.enter_context(nc.sbuf_tensor("slotB", [64, S], F32))
        slotC = _st.enter_context(nc.sbuf_tensor("slotC", [64, S], F32))
        zseg = _st.enter_context(nc.sbuf_tensor("zseg", [64, Q], F32))
        slast = _st.enter_context(nc.sbuf_tensor("slast", [64, NCH], F32))
        decT = _st.enter_context(nc.sbuf_tensor("decT", [NCH, 64], F32))
        hT = [hTb[:, k * HALF:(k + 1) * HALF] for k in range(KH)]
        rowbuf = [rb0, rb1]
        wxbuf = [wxbb[:, 0:KH * 128], wxbb[:, KH * 128:2 * KH * 128]]
        gwbuf = [gwbb, gwbb]
        gev = [gevb[:, i * 512:(i + 1) * 512] for i in range(4)]

        with nc.Block() as blk:
            @blk.gpsimd
            def _(g):
                g.memset(rb0[:, 0:3], 0.0)
                g.memset(rb1[:, 0:3], 0.0)
                g.memset(zseg[:, :], 0.0)
                g.memset(pack2[64:128, :], 0.0)

        pci = [0]

        for half in range(2):
            tok0 = half * HALF
            with nc.Block() as blk:
                @blk.sync
                def _(sy):
                    if half == 0:
                        sy.wait_ge(s_hc, 64)
                    for k in range(KH):
                        sy.dma_start_transpose(
                            hT[k],
                            E["d_hbf"].ap()[tok0:tok0 + HALF,
                                            k * 128:(k + 1) * 128]
                        ).then_inc(s_ht, 16)
                    for ofb in range(8):
                        for tt in range(NTT):
                            gidx = (half * 8 + ofb) * NTT + tt
                            sy.wait_ge(s_gev, gidx + 1)
                            sy.dma_start(
                                E["d_sg"].ap()[tok0 + tt * 128:
                                               tok0 + (tt + 1) * 128,
                                               ofb * 512:(ofb + 1) * 512],
                                gev[gidx % 4]).then_inc(s_gst4[gidx % 4], 16)
                    for i in range(4):
                        sy.wait_ge(s_gst4[i], 16 * (half + 1) * 32)

                @blk.gpsimd
                def _(g):
                    for t in range(NXT):
                        gi = half * NXT + t
                        if gi >= 2:
                            g.wait_ge(s_cv, gi - 1)
                        dw, of0, ncols = W_SRCS[t]
                        g.dma_start(wxbuf[gi % 2][:, :],
                                    w_lhsT_ap(dw, of0, ncols)
                                    ).then_inc(s_wld2[gi % 2], 16)
                        if half == 0 and t == 0:
                            g.dma_start(
                                wdtb[:, :],
                                AP(E["d_wdt"], 0,
                                   [[NH, 128], [128 * NH, KH], [1, 64]])
                            ).then_inc(s_wldt, 16)
                        target = (half * NXT + t) // 2
                        while pci[0] < min(target, NPC_EARLY):
                            src, dst, r0 = PCH[pci[0]]
                            g.dma_start(dst.ap()[r0:r0 + 1024, :],
                                        src.ap()[r0:r0 + 1024, :]
                                        ).then_inc(s_pc, 16)
                            pci[0] += 1
                        if gi >= 1:
                            g.wait_ge(s_si, gi)
                            tp, hp = (gi - 1) % NXT, (gi - 1) // NXT
                            g.dma_start(
                                E["d_xbc"].ap()[tp * 128:(tp + 1) * 128,
                                                hp * HALF:(hp + 1) * HALF],
                                rowc[:, :]).then_inc(s_sp, 16)
                    for ofb in range(8):
                        gio = half * 8 + ofb
                        if gio >= 1:
                            g.wait_ge(s_gmm, gio * NTT)
                        g.dma_start(gwbuf[gio % 2][:, :],
                                    gw_rhs_ap(E["d_wz"], ofb, INTER)
                                    ).then_inc(s_gld, 16)
                    gi_end = (half + 1) * NXT
                    g.wait_ge(s_si, gi_end)
                    g.dma_start(
                        E["d_xbc"].ap()[(NXT - 1) * 128:NXT * 128,
                                        half * HALF:(half + 1) * HALF],
                        rowc[:, :]).then_inc(s_sp, 16)
                    if half == 1:
                        while pci[0] < NPC_EARLY:
                            src, dst, r0 = PCH[pci[0]]
                            g.dma_start(dst.ap()[r0:r0 + 1024, :],
                                        src.ap()[r0:r0 + 1024, :]
                                        ).then_inc(s_pc, 16)
                            pci[0] += 1
                        for src, dst, r0 in PCH_LATE:
                            g.dma_start(dst.ap()[r0:r0 + 1024, :],
                                        src.ap()[r0:r0 + 1024, :]
                                        ).then_inc(s_pc, 16)
                    g.wait_ge(s_sp, 16 * gi_end)

                @blk.tensor
                def _(t_):
                    t_.wait_ge(s_ht, 16 * KH * (half + 1))
                    for t in range(NXT):
                        gi = half * NXT + t
                        t_.wait_ge(s_wld2[gi % 2], 16 * (gi // 2 + 1))
                        for tb in range(NTB):
                            idx = gi * NTB + tb
                            if idx >= 2:
                                t_.wait_ge(s_ev, idx - 1)
                            for k in range(KH):
                                inst = t_.matmul(
                                    bankA(idx % 2),
                                    wxbuf[gi % 2][:, k * 128:(k + 1) * 128],
                                    hT[k][:, tb * 512:(tb + 1) * 512],
                                    start=(k == 0), stop=(k == KH - 1))
                            inst.then_inc(s_mm, 1)
                    for ofb in range(8):
                        gio = half * 8 + ofb
                        t_.wait_ge(s_gld, 16 * (gio + 1))
                        for tt in range(NTT):
                            gidx = gio * NTT + tt
                            if gidx >= 2:
                                t_.wait_ge(s_gev, gidx - 1)
                            for k in range(KH):
                                inst = t_.matmul(
                                    bankB(gidx % 2),
                                    hT[k][:, tt * 128:(tt + 1) * 128],
                                    gwbuf[gio % 2][:, k * 512:(k + 1) * 512],
                                    start=(k == 0), stop=(k == KH - 1))
                            inst.then_inc(s_gmm, 1)
                    if half == 0:
                        t_.wait_ge(s_wldt, 16)
                    for tb in range(NTB):
                        didx = half * NTB + tb
                        if didx >= 2:
                            t_.wait_ge(s_dev, didx - 1)
                        for k in range(KH):
                            inst = t_.matmul(
                                bankB(2 + didx % 2)[0:64, :],
                                wdtb[:, k * 64:(k + 1) * 64],
                                hT[k][:, tb * 512:(tb + 1) * 512],
                                start=(k == 0), stop=(k == KH - 1))
                        inst.then_inc(s_dmm, 1)

                @blk.scalar
                def _(sc):
                    for t in range(NXT):
                        gi = half * NXT + t
                        for tb in range(NTB):
                            idx = gi * NTB + tb
                            sc.wait_ge(s_mm, idx + 1)
                            if gi >= 2 and tb == 0:
                                sc.wait_ge(s_cv, gi - 1)
                            sc.activation(
                                rowbuf[gi % 2][:, 3 + tb * 512:
                                               3 + (tb + 1) * 512],
                                bankA(idx % 2), AF.Copy).then_inc(s_ev, 1)
                        sc.wait_ge(s_cv, gi + 1)
                        sc.activation(rowc[:, :], rowc[:, :], AF.Silu)
                        sc.drain()
                        sc.nop().then_inc(s_si, 1)
                    for ofb in range(8):
                        for tt in range(NTT):
                            gidx = (half * 8 + ofb) * NTT + tt
                            sc.wait_ge(s_gmm, gidx + 1)
                            if gidx >= 4:
                                sc.wait_ge(s_gst4[gidx % 4], 16 * (gidx // 4))
                            sc.activation(gev[gidx % 4], bankB(gidx % 2),
                                          AF.Silu).then_inc(s_gev, 1)

                @blk.vector
                def _(v):
                    for t in range(NXT):
                        gi = half * NXT + t
                        v.wait_ge(s_ev, gi * NTB + NTB)
                        if gi >= 1:
                            v.wait_ge(s_sp, 16 * gi)
                        rb = rowbuf[gi % 2]
                        if half == 1:
                            v.tensor_copy(rb[:, 0:3], carry[:, 3 * t:3 * t + 3])
                        v.tensor_scalar(rowc[:, :], rb[:, 0:HALF],
                                        E["cw_sb"][:, 4 * t:4 * t + 1],
                                        E["cb_sb"][:, t:t + 1],
                                        ALU.mult, ALU.add)
                        for kk in range(1, 4):
                            v.scalar_tensor_tensor(
                                rowc[:, :], rb[:, kk:kk + HALF],
                                E["cw_sb"][:, 4 * t + kk:4 * t + kk + 1],
                                rowc[:, :], ALU.mult, ALU.add)
                        if half == 0:
                            v.tensor_copy(carry[:, 3 * t:3 * t + 3],
                                          rb[:, HALF:HALF + 3])
                        v.engine_nop().then_inc(s_cv, 1)
                    for tb in range(NTB):
                        didx = half * NTB + tb
                        v.wait_ge(s_dmm, didx + 1)
                        v.tensor_copy(
                            slotA[:, tok0 + tb * 512:tok0 + (tb + 1) * 512],
                            bankB(2 + didx % 2)[0:64, :]).then_inc(s_dev, 1)

        def vb(f):
            with nc.Block() as blk:
                blk.vector(f)

        def scb(f):
            with nc.Block() as blk:
                blk.scalar(f)

        vb(lambda v: v.tensor_scalar(slotA[:, :], slotA[:, :],
                                     E["dtb_sb"][:, :], None, ALU.add))
        scb(lambda sc: sc.activation(slotB[:, :], slotA[:, :], AF.Abs))
        scb(lambda sc: sc.activation(slotB[:, :], slotB[:, :], AF.Exp,
                                     scale=-1.0))
        scb(lambda sc: sc.activation(slotB[:, :], slotB[:, :], AF.Ln, bias=1.0))
        vb(lambda v: v.tensor_relu(slotC[:, :], slotA[:, :]))
        vb(lambda v: v.tensor_add(slotC[:, :], slotC[:, :], slotB[:, :]))
        vb(lambda v: v.tensor_scalar_mul(slotB[:, :], slotC[:, :],
                                         E["negA_sb"][:, :]))
        scb(lambda sc: sc.activation(slotC[:, :], slotC[:, :], AF.Ln))

        def scans(v):
            for ch in range(NCH):
                o = ch * Q
                v.tensor_tensor_scan(pack1[0:64, o:o + Q], slotB[:, o:o + Q],
                                     zseg[:, :], 0.0, ALU.add, ALU.add)
                rin = AP(slotB, o + Q - 1, [[S, 64], [-1, Q]])
                rz = AP(zseg, Q - 1, [[Q, 64], [-1, Q]])
                rout = AP(slotA, o + Q - 1, [[S, 64], [-1, Q]])
                v.tensor_tensor_scan(rout, rin, rz, 0.0, ALU.add, ALU.add)
        vb(scans)
        vb(lambda v: v.tensor_sub(slotA[:, :], slotA[:, :], slotB[:, :]))

        def finish_tabs(v):
            v.tensor_sub(slotB[:, :], slotC[:, :], pack1[0:64, :])
            v.tensor_add(pack2[0:64, :], slotA[:, :], slotC[:, :])
            v.tensor_copy(slast[:, :], AP(pack1, Q - 1, [[S, 64], [Q, NCH]]))
        vb(finish_tabs)
        scb(lambda sc: sc.activation(slast[:, :], slast[:, :], AF.Exp))
        s_nwm = sem("p2_nwm")
        with nc.Block() as blk:
            @blk.sync
            def _(sy):
                sy.dma_start(pack1[64:128, :], slotB[:, :]).then_inc(s_nwm, 16)
                sy.wait_ge(s_nwm, 16)

        with nc.Block() as blk:
            @blk.tensor
            def _(t_):
                t_.transpose(pA[0:NCH, 0:64], slast[:, :], E["id64_sb"][:, :])

        vb(lambda v: v.tensor_copy(decT[:, :], pA[0:NCH, 0:64]))

        with nc.Block() as blk:
            @blk.sync
            def _(sy):
                sy.dma_start(E["d_tabs"].ap()[:, :], pack1[0:64, :]
                             ).then_inc(s_tsp, 16)
                sy.dma_start(AP(E["d_tabsC"], 0,
                                [[Q, 64], [NH * Q, NCH], [1, Q]]),
                             pack1[0:64, :],
                             allow_slow_non_contiguous=True
                             ).then_inc(s_tsp, 16)
                sy.dma_start(E["d_dec"].ap()[:, :], decT[:, :]).then_inc(s_tsp, 16)
                sy.wait_ge(s_tsp, 48)


def build_p3(nc, sem, E, nch3=NCH):
    SKIP = set(os.environ.get("P3_SKIP", "").split(","))
    bankA, bankB, pA = E["bankA"], E["bankB"], E["pA"]
    pack1, pack2 = E["pack1"], E["pack2"]

    s_l1 = [sem("p3_l1a"), sem("p3_l1b")]
    s_l2 = [sem("p3_l2a"), sem("p3_l2b")]
    s_l3 = [sem(f"p3_l3{i}") for i in range(4)]
    s_l4 = sem("p3_l4")
    s_l5 = [sem("p3_l5a"), sem("p3_l5b")]
    s_ttr = sem("p3_ttr"); s_tev = sem("p3_tev"); s_ext = sem("p3_ext")
    s_gmm = sem("p3_gmm"); s_gme = sem("p3_gme")
    s_u4 = sem("p3_u4"); s_exp = sem("p3_exp"); s_m4 = sem("p3_m4")
    s_xs = sem("p3_xs")
    s_yim = sem("p3_yim"); s_yis = sem("p3_yis")
    s_yxm = sem("p3_yxm"); s_yfn = sem("p3_yfn")
    s_cmm = sem("p3_cmm"); s_stu = sem("p3_stu"); s_stc = sem("p3_stc")
    s_g = sem("p3_g"); s_rst = sem("p3_rst"); s_nrm = sem("p3_nrm")
    s_nst = sem("p3_nst"); s_cd = sem("p3_cd")
    s_wt3 = sem("p3_wt3")
    s_wtT = [sem("p3_wtT0"), sem("p3_wtT1")]
    s_wtS = [sem("p3_wtS0"), sem("p3_wtS1")]
    E["s_wtS"] = s_wtS
    WT_GU = []
    for mat_src, mat_dst in ((E["d_wgb"], E["d_wgT"]), (E["d_wub"], E["d_wuT"])):
        for k in range(KH):
            for hq in range(4):
                WT_GU.append((mat_src, mat_dst, k, hq))
    E["N_WT_GU"] = len(WT_GU)

    with ExitStack() as _st:
        DIall = _st.enter_context(nc.sbuf_tensor("DIall", [128, NH * 128], BF16))
        state = _st.enter_context(nc.sbuf_tensor("state", [128, NH * 64], F32))
        state_bf = _st.enter_context(nc.sbuf_tensor("state_bf", [128, NH * 64], BF16))
        xtokb = _st.enter_context(nc.sbuf_tensor("xtok", [128, 2 * 5120], BF16))
        bcFb = _st.enter_context(nc.sbuf_tensor("bcF", [128, 2 * 2048], BF16))
        sgch = _st.enter_context(nc.sbuf_tensor("sgch", [128, INTER], BF16))
        decbcb = _st.enter_context(nc.sbuf_tensor("decbc", [128, 2 * 64], F32))
        ubcb = _st.enter_context(nc.sbuf_tensor("ubc", [128, 4 * 512], F32))
        tabT = _st.enter_context(nc.sbuf_tensor("tabT", [128, 192], F32))
        expsT = _st.enter_context(nc.sbuf_tensor("expsT", [128, 64], F32))
        erdtT = _st.enter_context(nc.sbuf_tensor("erdtT", [128, 64], F32))
        Gm = _st.enter_context(nc.sbuf_tensor("Gm", [128, 8 * 128], BF16))
        u4b = _st.enter_context(nc.sbuf_tensor("u4", [128, 4 * 512], F32))
        e4b = _st.enter_context(nc.sbuf_tensor("e4", [128, 4 * 512], BF16))
        M4 = _st.enter_context(nc.sbuf_tensor("M4", [128, NH * 128], BF16))
        xs8 = _st.enter_context(nc.sbuf_tensor("xs8", [128, NH * 64], BF16))
        ybig = _st.enter_context(nc.sbuf_tensor("ybig", [128, INTER], F32))
        tmpg = _st.enter_context(nc.sbuf_tensor("tmpg", [128, 512], F32))
        normbb = _st.enter_context(nc.sbuf_tensor("normbb", [128, 2 * INTER], BF16))
        wtb3b = _st.enter_context(nc.sbuf_tensor("wtb3", [128, DFF // 2], BF16,
                                                 is_transpose_dest=True))
        wtb3 = [wtb3b[:, 0:DFF // 4], wtb3b[:, DFF // 4:DFF // 2]]
        sscol = _st.enter_context(nc.sbuf_tensor("sscol", [128, 1], F32))
        rstdc = _st.enter_context(nc.sbuf_tensor("rstdc", [128, 1], F32))
        xtok = [xtokb[:, 0:5120], xtokb[:, 5120:10240]]
        bcF = [bcFb[:, 0:2048], bcFb[:, 2048:4096]]
        decbc = [decbcb[:, 0:64], decbcb[:, 64:128]]
        ubc = [ubcb[:, i * 512:(i + 1) * 512] for i in range(4)]
        u4 = [u4b[:, i * 512:(i + 1) * 512] for i in range(4)]
        e4 = [e4b[:, i * 512:(i + 1) * 512] for i in range(4)]
        normb = [normbb[:, 0:INTER], normbb[:, INTER:2 * INTER]]

        with nc.Block() as blk:
            @blk.gpsimd
            def _(g):
                g.memset(state[:, :], 0.0)
                g.memset(state_bf[:, :], 0.0)
                g.affine_select(
                    AP(DIall, 0, [[NH * 128, 128], [128, 64], [1, 128]]),
                    AP(E["dbcb_sb"], 0, [[64, 128], [1, 64], [0, 128]]),
                    [[0, 64], [1, 128]], ALU.is_equal, 0.0,
                    channel_multiplier=-1)

        with nc.Block() as blk:
            @blk.sync
            def _(sy):
                for ch in range(nch3):
                    e = ch % 2
                    if ch >= 2:
                        sy.wait_ge(s_cd, ch - 1)
                    sy.dma_start(
                        bcF[e],
                        AP(E["d_xbc"], INTER * S + ch * Q,
                           [[S, 128], [128 * S, 16], [1, Q]])).then_inc(s_l1[e], 16)
                    sy.dma_start_transpose(
                        xtok[e],
                        E["d_xbc"].ap()[0:5120, ch * Q:(ch + 1) * Q]
                    ).then_inc(s_l2[e], 16)
                    sy.dma_start(decbc[e],
                                 AP(E["d_dec"], ch * 64, [[0, 128], [1, 64]])
                                 ).then_inc(s_l5[e], 16)
                    for q in range(16):
                        uq = ch * 16 + q
                        if uq >= 4:
                            sy.wait_ge(s_u4, uq - 3)
                        sy.dma_start(
                            ubc[q % 4],
                            AP(E["d_tabsC"], ch * NH * Q + (4 * q) * Q,
                               [[0, 128], [Q, 4], [1, Q]])
                        ).then_inc(s_l3[q % 4], 16)
                    if ch >= 1:
                        sy.wait_ge(s_nrm, ch)
                        sy.dma_start(E["d_norm"].ap()[(ch - 1) * Q:ch * Q, :],
                                     normb[(ch - 1) % 2][:, :]
                                     ).then_inc(s_nst, 16)
                        sy.wait_ge(s_nst, 16 * ch)
                        sy.sem_inc(s_cd, 1)
                    for j in range(4 * ch, min(4 * ch + 4, len(WT_GU))):
                        src_, dst, k, hq = WT_GU[j]
                        r0 = hq * (DFF // 4)
                        e2 = j % 2
                        if j == 0:
                            sy.wait_ge(E["s_pc"], 16 * E["NPC_EARLY"])
                        if j >= 2:
                            sy.wait_ge(s_wtS[e2], 16 * (j // 2))
                        sy.dma_start_transpose(
                            wtb3[e2],
                            src_.ap()[r0:r0 + DFF // 4, k * 128:(k + 1) * 128]
                        ).then_inc(s_wtT[e2], 16)
                        sy.wait_ge(s_wtT[e2], 16 * (j // 2 + 1))
                        sy.dma_start(
                            dst.ap()[k * 128:(k + 1) * 128, r0:r0 + DFF // 4],
                            wtb3[e2]).then_inc(s_wtS[e2], 16)
                sy.wait_ge(s_nrm, nch3)
                sy.dma_start(E["d_norm"].ap()[(nch3 - 1) * Q:nch3 * Q, :],
                             normb[(nch3 - 1) % 2][:, :]).then_inc(s_nst, 16)
                sy.wait_ge(s_nst, 16 * nch3)
                sy.sem_inc(s_cd, 1)

            @blk.tensor
            def _(t_):
                for ch in range(nch3):
                    e = ch % 2
                    t_.wait_ge(s_tev, 3 * ch)
                    t_.wait_ge(s_gme, 8 * ch)
                    if "tr" not in SKIP:
                        t_.transpose(pA[:, 0:128], pack1[:, ch * Q:(ch + 1) * Q],
                                     E["id128f"][:, :])
                        t_.transpose(pA[:, 128:256],
                                     pack2[:, ch * Q:(ch + 1) * Q],
                                     E["id128f"][:, :]).then_inc(s_ttr, 1)
                    else:
                        t_.nop().then_inc(s_ttr, 1)
                    t_.wait_ge(s_l1[e], 16 * (ch // 2 + 1))
                    t_.wait_ge(s_tev, 3 * ch + 3)
                    for g in range(8):
                        if "g" not in SKIP:
                            t_.matmul(
                                bankA(g // 4)[:, (g % 4) * 128:(g % 4 + 1) * 128],
                                bcF[e][:, g * 128:(g + 1) * 128],
                                bcF[e][:, 1024 + g * 128:1024 + (g + 1) * 128],
                                start=True, stop=True).then_inc(s_gmm, 1)
                        else:
                            t_.nop().then_inc(s_gmm, 1)
                    t_.wait_ge(s_stc, ch)
                    for g in range(8):
                        idx = ch * 8 + g
                        if idx >= 2:
                            t_.wait_ge(s_yis, idx - 1)
                        if "y" not in SKIP:
                            for hh in range(8):
                                h = g * 8 + hh
                                inst = t_.matmul(
                                    bankA(2 + g % 2)[:, hh * 64:(hh + 1) * 64],
                                    bcF[e][:, 1024 + g * 128:1024 + (g + 1) * 128],
                                    state_bf[:, h * 64:(h + 1) * 64],
                                    start=True, stop=True)
                            inst.then_inc(s_yim, 1)
                        else:
                            t_.nop().then_inc(s_yim, 1)
                    t_.wait_ge(s_l2[e], 16 * (ch // 2 + 1))
                    for g in range(8):
                        idx = ch * 8 + g
                        t_.wait_ge(s_m4, ch * 16 + 2 * (g + 1))
                        if idx >= 2:
                            t_.wait_ge(s_yfn, idx - 1)
                        if "y" not in SKIP:
                            for hh in range(8):
                                h = g * 8 + hh
                                inst = t_.matmul(
                                    bankB(g % 2)[:, hh * 64:(hh + 1) * 64],
                                    M4[:, h * 128:(h + 1) * 128],
                                    xtok[e][:, h * 64:(h + 1) * 64],
                                    start=True, stop=True)
                            inst.then_inc(s_yxm, 1)
                        else:
                            t_.nop().then_inc(s_yxm, 1)
                    for g in range(8):
                        idx = ch * 8 + g
                        t_.wait_ge(s_xs, idx + 1)
                        if idx >= 2:
                            t_.wait_ge(s_stu, idx - 1)
                        if "c" not in SKIP:
                            for hh in range(8):
                                h = g * 8 + hh
                                inst = t_.matmul(
                                    bankB(2 + g % 2)[:, hh * 64:(hh + 1) * 64],
                                    xtok[e][:, 4096 + g * 128:4096 + (g + 1) * 128],
                                    xs8[:, h * 64:(h + 1) * 64],
                                    start=True, stop=True)
                            inst.then_inc(s_cmm, 1)
                        else:
                            t_.nop().then_inc(s_cmm, 1)

            @blk.vector
            def _(v):
                for ch in range(nch3):
                    e = ch % 2
                    if ch >= 2:
                        v.wait_ge(s_cd, ch - 1)
                    v.wait_ge(s_ttr, ch + 1)
                    if ch >= 1:
                        v.wait_ge(s_ext, ch)
                    if "tr" not in SKIP:
                        v.tensor_copy(tabT[:, :], pA[:, 0:192])
                    v.engine_nop().then_inc(s_tev, 3)
                    for g in range(8):
                        v.wait_ge(s_gmm, ch * 8 + (4 if g < 4 else 8))
                        if "g" not in SKIP:
                            v.tensor_mul(
                                Gm[:, g * 128:(g + 1) * 128],
                                bankA(g // 4)[:, (g % 4) * 128:(g % 4 + 1) * 128],
                                E["caus_sb"][:, :]).then_inc(s_gme, 1)
                        else:
                            v.engine_nop().then_inc(s_gme, 1)
                    if ch >= 1:
                        v.wait_ge(s_yxm, 8 * ch)
                    for qq in range(18):
                        if qq < 16:
                            q = qq
                            v.wait_ge(s_l3[q % 4],
                                      16 * ((ch * 16 + q) // 4 + 1))
                            if ch * 16 + q >= 4:
                                v.wait_ge(s_exp, ch * 16 + q - 3)
                            if "u" not in SKIP:
                                for hh in range(4):
                                    h = q * 4 + hh
                                    v.tensor_scalar(
                                        u4[q % 4][:, hh * 128:(hh + 1) * 128],
                                        ubc[q % 4][:, hh * 128:(hh + 1) * 128],
                                        tabT[:, 64 + h:64 + h + 1], CLAMP,
                                        ALU.add, ALU.min)
                            v.engine_nop().then_inc(s_u4, 1)
                        if qq >= 2:
                            q = qq - 2
                            v.wait_ge(s_exp, ch * 16 + q + 1)
                            g = q // 2
                            v.wait_ge(s_gme, ch * 8 + g + 1)
                            if "m" not in SKIP:
                                gm_ap = AP(Gm, g * 128,
                                           [[1024, 128], [0, 4], [1, 128]])
                                v.tensor_tensor(M4[:, q * 512:(q + 1) * 512],
                                                e4[q % 4], gm_ap, ALU.mult)
                                v.tensor_add(M4[:, q * 512:(q + 1) * 512],
                                             M4[:, q * 512:(q + 1) * 512],
                                             DIall[:, q * 512:(q + 1) * 512]
                                             ).then_inc(s_m4, 1)
                            else:
                                v.engine_nop().then_inc(s_m4, 1)
                    v.wait_ge(s_l2[e], 16 * (ch // 2 + 1))
                    v.wait_ge(s_ext, ch + 1)
                    if ch >= 1:
                        v.wait_ge(s_cmm, 8 * ch)
                    for g in range(8):
                        if "c" not in SKIP:
                            erd_ap = AP(erdtT, 8 * g,
                                        [[64, 128], [1, 8], [0, 64]])
                            v.tensor_mul(xs8[:, g * 512:(g + 1) * 512],
                                         xtok[e][:, g * 512:(g + 1) * 512],
                                         erd_ap).then_inc(s_xs, 1)
                        else:
                            v.engine_nop().then_inc(s_xs, 1)
                    for g in range(8):
                        v.wait_ge(s_yim, ch * 8 + g + 1)
                        if "y" not in SKIP:
                            ex_ap = AP(expsT, 8 * g,
                                       [[64, 128], [1, 8], [0, 64]])
                            v.tensor_mul(ybig[:, g * 512:(g + 1) * 512],
                                         bankA(2 + g % 2), ex_ap
                                         ).then_inc(s_yis, 1)
                        else:
                            v.engine_nop().then_inc(s_yis, 1)
                    for g in range(8):
                        v.wait_ge(s_yxm, ch * 8 + g + 1)
                        if "y" not in SKIP:
                            v.tensor_add(ybig[:, g * 512:(g + 1) * 512],
                                         bankB(g % 2),
                                         ybig[:, g * 512:(g + 1) * 512]
                                         ).then_inc(s_yfn, 1)
                        else:
                            v.engine_nop().then_inc(s_yfn, 1)
                    v.wait_ge(s_yim, ch * 8 + 8)
                    v.wait_ge(s_l5[e], 16 * (ch // 2 + 1))
                    for g in range(8):
                        v.wait_ge(s_cmm, ch * 8 + g + 1)
                        if "c" not in SKIP:
                            dec_ap = AP(decbcb, e * 64 + 8 * g,
                                        [[128, 128], [1, 8], [0, 64]])
                            v.tensor_mul(tmpg[:, :],
                                         state[:, g * 512:(g + 1) * 512],
                                         dec_ap)
                            v.tensor_add(state[:, g * 512:(g + 1) * 512],
                                         tmpg[:, :], bankB(2 + g % 2)
                                         ).then_inc(s_stu, 1)
                        else:
                            v.engine_nop().then_inc(s_stu, 1)
                    v.tensor_copy(state_bf[:, :], state[:, :]).then_inc(s_stc, 1)
                    v.wait_ge(s_l4, 16 * (ch + 1))
                    if "n" not in SKIP:
                        v.tensor_mul(ybig[:, :], ybig[:, :], sgch[:, :])
                    v.engine_nop().then_inc(s_g, 1)
                    v.wait_ge(s_rst, ch + 1)
                    if "n" not in SKIP:
                        v.tensor_mul(normb[e], ybig[:, :],
                                     AP(rstdc, 0, [[1, 128], [0, INTER]]))
                    v.engine_nop().then_inc(s_nrm, 1)

            @blk.gpsimd
            def _(gp):
                for ch in range(nch3):
                    if ch >= 1:
                        gp.wait_ge(s_g, ch)
                    gp.dma_start(sgch[:, :],
                                 E["d_sg"].ap()[ch * Q:(ch + 1) * Q, :]
                                 ).then_inc(s_l4, 16)

            @blk.scalar
            def _(sc):
                for ch in range(nch3):
                    e = ch % 2
                    if ch >= 1:
                        sc.wait_ge(s_yis, 8 * ch)
                        sc.wait_ge(s_xs, 8 * ch)
                    sc.wait_ge(s_tev, 3 * ch + 3)
                    if "tr" not in SKIP:
                        sc.activation(expsT[:, :], tabT[:, 0:64], AF.Exp)
                        sc.activation(erdtT[:, :], tabT[:, 128:192],
                                      AF.Exp).then_inc(s_ext, 1)
                    else:
                        sc.nop().then_inc(s_ext, 1)
                    for q in range(16):
                        sc.wait_ge(s_u4, ch * 16 + q + 1)
                        if ch * 16 + q >= 4:
                            sc.wait_ge(s_m4, ch * 16 + q - 3)
                        if "e" not in SKIP:
                            sc.activation(e4[q % 4], u4[q % 4],
                                          AF.Exp).then_inc(s_exp, 1)
                        else:
                            sc.nop().then_inc(s_exp, 1)
                    sc.wait_ge(s_g, ch + 1)
                    if ch >= 2:
                        sc.wait_ge(s_cd, ch - 1)
                    if "n" not in SKIP:
                        sc.activation(normb[e], ybig[:, :], AF.Square,
                                      accum_out=sscol[:, :])
                        sc.drain()
                        sc.activation(rstdc[:, :], sscol[:, :], AF.Ln,
                                      scale=1.0 / INTER, bias=EPS)
                        sc.drain()
                        sc.activation(rstdc[:, :], rstdc[:, :], AF.Exp,
                                      scale=-0.5)
                        sc.drain()
                        sc.nop().then_inc(s_rst, 1)
                    else:
                        sc.nop().then_inc(s_rst, 1)


def build_p4(nc, sem, E):
    SKIP = set(os.environ.get("P4_SKIP", "").split(","))
    pA, pB = E["pA"], E["pB"]
    s_pc, NPC = E["s_pc"], E["NPC"]

    s_wl = sem("p4_wl"); s_ws = sem("p4_ws"); s_nt = sem("p4_nt")
    s_h0 = sem("p4_h0"); s_mm = sem("p4_mm"); s_x0 = sem("p4_x0")
    s_st = sem("p4_st"); s_xc = sem("p4_xc"); s_rs = sem("p4_rs")
    s_ap = sem("p4_ap"); s_dbg = sem("p4_dbg")
    E["d_dbgln"] = nc.dram_tensor("dbgln", [S // 128 * 128, 16], F32,
                                  kind="ExternalOutput")
    s_sf = sem("p4_sf"); s_sb2 = sem("p4_sb")

    NSUP = 16
    TPS = S // NSUP
    TT = TPS // 128
    NTILE = S // 128


    with (
        nc.sbuf_tensor("wouts", [128, KI * 2048], BF16) as woutsb,
        nc.sbuf_tensor("nT", [128, 2 * KI * TPS], BF16) as nTb,
        nc.sbuf_tensor("h0t", [128, 2048], F32) as h0t,
        nc.sbuf_tensor("x0b", [128, 2 * 2048], F32) as x0bb,
        nc.sbuf_tensor("x1b", [128, 2 * 2048], F32) as x1bb,
        nc.sbuf_tensor("lncol", [128, 16], F32) as lncol,
    ):
        wouts = [woutsb[:, k * 2048:(k + 1) * 2048] for k in range(KI)]
        nT = [[nTb[:, (e * KI + k) * TPS:(e * KI + k + 1) * TPS]
               for k in range(KI)] for e in range(2)]
        x0b = [x0bb[:, 0:2048], x0bb[:, 2048:4096]]
        x1b = [x1bb[:, 0:2048], x1bb[:, 2048:4096]]

        def LC(par, i):
            return lncol[:, par * 8 + i:par * 8 + i + 1]

        def LCbc(par, i):
            return AP(lncol, par * 8 + i, [[16, 128], [0, H]])

        with nc.Block() as blk:
            @blk.sync
            def _(sy):
                sy.wait_ge(s_pc, 16 * NPC)
                for k in range(KI):
                    sy.dma_start(wouts[k],
                                 E["d_woutb"].ap()[k * 128:(k + 1) * 128, :]
                                 ).then_inc(s_wl, 16)
                for sp in range(NSUP):
                    if sp >= 2:
                        sy.wait_ge(s_mm, (sp - 1) * TT)
                    for k in range(KI):
                        sy.dma_start_transpose(
                            nT[sp % 2][k],
                            E["d_norm"].ap()[sp * TPS:(sp + 1) * TPS,
                                             k * 128:(k + 1) * 128]
                        ).then_inc(s_nt, 16)
                    for tt in range(TT):
                        gi = sp * TT + tt
                        if gi >= 1:
                            sy.wait_ge(s_x0, gi)
                        if gi >= 2:
                            sy.wait_ge(s_rs, gi - 1)
                        sy.dma_start(
                            h0t[:, :],
                            E["d_h0"].ap()[sp * TPS + tt * 128:
                                           sp * TPS + (tt + 1) * 128, :]
                        ).then_inc(s_h0, 16)
                        if gi >= 2:
                            sy.wait_ge(s_ap, gi - 1)
                            sy.dma_start(
                                E["d_x1f"].ap()[(gi - 2) * 128:
                                                (gi - 1) * 128, :],
                                x1b[gi % 2]).then_inc(s_sf, 16)
                for gi in range(NTILE - 2, NTILE):
                    sy.wait_ge(s_ap, gi + 1)
                    sy.dma_start(E["d_x1f"].ap()[gi * 128:(gi + 1) * 128, :],
                                 x1b[gi % 2]).then_inc(s_sf, 16)
                sy.wait_ge(s_sf, 16 * NTILE)

            @blk.tensor
            def _(t_):
                t_.wait_ge(s_ws, KI)
                for sp in range(NSUP):
                    t_.wait_ge(s_nt, 16 * KI * (sp + 1))
                    for tt in range(TT):
                        gi = sp * TT + tt
                        psum = pA if gi % 2 == 0 else pB
                        if gi >= 2:
                            t_.wait_ge(s_x0, gi - 1)
                        for db in range(4):
                            for k in range(KI):
                                inst = t_.matmul(
                                    psum[:, db * 512:(db + 1) * 512],
                                    nT[sp % 2][k][:, tt * 128:(tt + 1) * 128],
                                    wouts[k][:, db * 512:(db + 1) * 512],
                                    start=(k == 0), stop=(k == KI - 1))
                        inst.then_inc(s_mm, 1)

            @blk.vector
            def _(v):
                v.wait_ge(s_wl, 16 * KI)
                for k in range(KI):
                    v.tensor_scalar_mul(wouts[k], wouts[k],
                                        E["nw_sb"][:, k:k + 1]).then_inc(s_ws, 1)
                for gi in range(NTILE):
                    par = gi % 2
                    v.wait_ge(s_mm, gi + 1)
                    v.wait_ge(s_h0, 16 * (gi + 1))
                    if gi >= 2:
                        v.wait_ge(s_ap, gi - 1)
                    psum = pA if par == 0 else pB
                    v.tensor_add(x0b[par], psum[:, :], h0t[:, :]).then_inc(s_x0, 1)
                    v.wait_ge(s_st, gi + 1)
                    v.scalar_tensor_tensor(x1b[par],
                                           LCbc(par, 0), 1.0 / H, x0b[par],
                                           ALU.mult, ALU.subtract
                                           ).then_inc(s_xc, 1)
                    v.wait_ge(s_rs, gi + 1)
                    v.tensor_mul(x1b[par], x1b[par], LCbc(par, 6)
                                 ).then_inc(s_ap, 1)

            @blk.scalar
            def _(sc):
                for gi in range(NTILE):
                    par = gi % 2
                    sc.wait_ge(s_x0, gi + 1)
                    if gi >= 2:
                        sc.wait_ge(s_sf, 16 * (gi - 1))
                        sc.wait_ge(s_sb2, 16 * (gi - 1))
                    sc.activation(x1b[par], x0b[par], AF.Copy,
                                  accum_out=LC(par, 0))
                    sc.drain()
                    sc.nop().then_inc(s_st, 1)
                    sc.wait_ge(s_xc, gi + 1)
                    sc.activation(x0b[par], x1b[par], AF.Square,
                                  accum_out=LC(par, 1))
                    sc.drain()
                    sc.activation(LC(par, 4), LC(par, 1), AF.Ln,
                                  scale=1.0 / H, bias=EPS)
                    sc.drain()
                    sc.activation(LC(par, 5), LC(par, 4), AF.Exp, scale=-0.5)
                    sc.drain()
                    sc.activation(LC(par, 6), LC(par, 5), AF.Copy, scale=-1.0)
                    sc.drain()
                    sc.nop().then_inc(s_rs, 1)
                    if "dbgln" in SKIP:
                        sc.dma_start(E["d_dbgln"].ap()[gi * 128:(gi + 1) * 128, :],
                                     lncol[:, :]).then_inc(s_dbg, 16)

            @blk.gpsimd
            def _(g):
                for gi in range(NTILE):
                    g.wait_ge(s_ap, gi + 1)
                    g.dma_start(E["d_x1b"].ap()[gi * 128:(gi + 1) * 128, :],
                                x1b[gi % 2]).then_inc(s_sb2, 16)
                g.wait_ge(s_sb2, 16 * NTILE)


def build_p5(nc, sem, E):
    bankA, bankB = E["bankA"], E["bankB"]

    s_xt = sem("p5_xt"); s_gm = sem("p5_gm")
    s_wl = [sem("p5_wla"), sem("p5_wlb")]
    s_um = sem("p5_um"); s_se = sem("p5_se"); s_gd = sem("p5_gd")
    s_gs = [sem("p5_gsa"), sem("p5_gsb")]; s_wt = sem("p5_wt")

    NOF = DFF // 128
    TB = S // 512

    WT_JOBS = [(E["d_wdb"], E["d_wdT"], k) for k in range(NOF)]
    E["s_wt5"] = s_wt

    def wT_lhsT_ap(dw, o):
        return AP(dw, o * 128, [[DFF, 128], [128 * DFF, KH], [1, 128]])

    with (
        nc.sbuf_tensor("x1T", [128, KH * S], BF16) as x1Tb,
        nc.sbuf_tensor("wgb5", [128, 2 * KH * 128], BF16) as wgb5,
        nc.sbuf_tensor("wub5", [128, 2 * KH * 128], BF16) as wub5,
        nc.sbuf_tensor("sgb", [128, 4 * 512], BF16) as sgbb,
        nc.sbuf_tensor("gtd", [128, 2 * S], BF16) as gtdb,
        nc.sbuf_tensor("wtb5", [128, 2 * H], BF16, is_transpose_dest=True
                       ) as wtb5,
    ):
        x1T = [x1Tb[:, k * S:(k + 1) * S] for k in range(KH)]
        wgb = [wgb5[:, 0:KH * 128], wgb5[:, KH * 128:2 * KH * 128]]
        wub = [wub5[:, 0:KH * 128], wub5[:, KH * 128:2 * KH * 128]]
        sgb = [sgbb[:, i * 512:(i + 1) * 512] for i in range(4)]
        gtd = [gtdb[:, 0:S], gtdb[:, S:2 * S]]
        wtb = [wtb5[:, 0:H], wtb5[:, H:2 * H]]

        with nc.Block() as blk:
            @blk.sync
            def _(sy):
                for k in range(KH):
                    sy.dma_start_transpose(
                        x1T[k],
                        E["d_x1b"].ap()[0:S, k * 128:(k + 1) * 128]
                    ).then_inc(s_xt, 16)
                sy.wait_ge(E["s_wtS"][0], 16 * (E["N_WT_GU"] // 2))
                sy.wait_ge(E["s_wtS"][1], 16 * (E["N_WT_GU"] // 2))
                for o in range(NOF):
                    if o >= 2:
                        sy.wait_ge(s_gm, (o - 1) * TB)
                    sy.dma_start(wgb[o % 2][:, :],
                                 wT_lhsT_ap(E["d_wgT"], o)
                                 ).then_inc(s_wl[o % 2], 16)
                    sy.dma_start(wub[o % 2][:, :],
                                 wT_lhsT_ap(E["d_wuT"], o)
                                 ).then_inc(s_wl[o % 2], 16)
                    if o >= 2:
                        sy.wait_ge(s_gd, (o - 1) * TB)
                        sy.dma_start(
                            E["d_gated"].ap()[(o - 2) * 128:(o - 1) * 128, :],
                            gtd[o % 2]).then_inc(s_gs[o % 2], 16)
                    src_, dst, k = WT_JOBS[o]
                    sy.dma_start_transpose(
                        wtb[0],
                        src_.ap()[0:H, k * 128:(k + 1) * 128]).then_inc(s_wt, 16)
                    sy.wait_ge(s_wt, 32 * o + 16)
                    sy.dma_start(dst.ap()[k * 128:(k + 1) * 128, :],
                                 wtb[0]).then_inc(s_wt, 16)
                    sy.wait_ge(s_wt, 32 * (o + 1))
                for o in range(NOF - 2, NOF):
                    sy.wait_ge(s_gd, o * TB + TB)
                    sy.dma_start(E["d_gated"].ap()[o * 128:(o + 1) * 128, :],
                                 gtd[o % 2]).then_inc(s_gs[o % 2], 16)
                for i in range(2):
                    sy.wait_ge(s_gs[i], 16 * NOF // 2)

            @blk.tensor
            def _(t_):
                t_.wait_ge(s_xt, 16 * KH)
                for o in range(NOF):
                    t_.wait_ge(s_wl[o % 2], 32 * (o // 2 + 1))
                    for tb in range(TB):
                        idx = o * TB + tb
                        if idx >= 2:
                            t_.wait_ge(s_se, idx - 1)
                        for k in range(KH):
                            inst = t_.matmul(
                                bankA(idx % 2),
                                wgb[o % 2][:, k * 128:(k + 1) * 128],
                                x1T[k][:, tb * 512:(tb + 1) * 512],
                                start=(k == 0), stop=(k == KH - 1))
                        inst.then_inc(s_gm, 1)
                        if idx >= 2:
                            t_.wait_ge(s_gd, idx - 1)
                        for k in range(KH):
                            inst = t_.matmul(
                                bankB(idx % 2),
                                wub[o % 2][:, k * 128:(k + 1) * 128],
                                x1T[k][:, tb * 512:(tb + 1) * 512],
                                start=(k == 0), stop=(k == KH - 1))
                        inst.then_inc(s_um, 1)

            @blk.scalar
            def _(sc):
                for o in range(NOF):
                    for tb in range(TB):
                        idx = o * TB + tb
                        sc.wait_ge(s_gm, idx + 1)
                        if idx >= 4:
                            sc.wait_ge(s_gd, idx - 3)
                        sc.activation(sgb[idx % 4], bankA(idx % 2),
                                      AF.Silu).then_inc(s_se, 1)


            @blk.vector
            def _(v):
                for o in range(NOF):
                    for tb in range(TB):
                        idx = o * TB + tb
                        v.wait_ge(s_um, idx + 1)
                        v.wait_ge(s_se, idx + 1)
                        if o >= 2 and tb == 0:
                            v.wait_ge(s_gs[o % 2], 16 * (o // 2))
                        v.tensor_mul(gtd[o % 2][:, tb * 512:(tb + 1) * 512],
                                     sgb[idx % 4], bankB(idx % 2)
                                     ).then_inc(s_gd, 1)


def build_p6(nc, sem, E):
    pA, pB = E["pA"], E["pB"]

    s_wl = sem("p6_wl"); s_mm = sem("p6_mm")
    s_gc = [sem("p6_gca"), sem("p6_gcb")]
    s_ev = sem("p6_ev"); s_ps = sem("p6_ps")
    s_pl = sem("p6_pl"); s_xl = sem("p6_xl")
    s_x0 = sem("p6_x0"); s_st = sem("p6_st"); s_xc = sem("p6_xc")
    s_rs = sem("p6_rs"); s_ap = sem("p6_ap")
    s_os = [sem("p6_osa"), sem("p6_osb")]

    NTILE = S // 128
    NHC = DFF // 128

    def gcol_ap(tt):
        return AP(E["d_gated"], tt * 128, [[S, 128], [128 * S, NHC], [1, 128]])

    with (
        nc.sbuf_tensor("wdh", [128, NHC * 1024], BF16) as wdhb,
        nc.sbuf_tensor("gcol", [128, 2 * DFF], BF16) as gcolb,
        nc.sbuf_tensor("prt", [128, 1024], F32) as prt,
        nc.sbuf_tensor("x1t6", [128, 2048], F32) as x1t6,
        nc.sbuf_tensor("dfl", [128, 2048], F32) as dfl,
        nc.sbuf_tensor("ob6", [128, 2 * 2048], F32) as ob6b,
        nc.sbuf_tensor("lncol6", [128, 16], F32) as lncol6,
    ):
        wdh = [wdhb[:, h * 1024:(h + 1) * 1024] for h in range(NHC)]
        gcol = [gcolb[:, 0:DFF], gcolb[:, DFF:2 * DFF]]
        ob6 = [ob6b[:, 0:2048], ob6b[:, 2048:4096]]

        def LC(par, i):
            return lncol6[:, par * 8 + i:par * 8 + i + 1]

        def LCbc(par, i):
            return AP(lncol6, par * 8 + i, [[16, 128], [0, H]])

        for dpass in range(2):
            with nc.Block() as blk:
                @blk.sync
                def _(sy):
                    if dpass == 0:
                        sy.wait_ge(E["s_wt5"], 32 * (DFF // 128))
                    for h in range(NHC):
                        sy.dma_start(
                            wdh[h],
                            E["d_wdT"].ap()[h * 128:(h + 1) * 128,
                                            dpass * 1024:(dpass + 1) * 1024]
                        ).then_inc(s_wl, 16)
                    for tt in range(NTILE):
                        gi = dpass * NTILE + tt
                        if tt >= 2:
                            sy.wait_ge(s_mm, gi - 1)
                        sy.dma_start(gcol[tt % 2], gcol_ap(tt)
                                     ).then_inc(s_gc[tt % 2], 16)
                        if dpass == 1:
                            if tt >= 1:
                                sy.wait_ge(s_x0, tt)
                            sy.dma_start(
                                x1t6[:, :],
                                E["d_x1f"].ap()[tt * 128:(tt + 1) * 128, :]
                            ).then_inc(s_xl, 16)
                            sy.dma_start(
                                prt[:, :],
                                E["d_part0"].ap()[tt * 128:(tt + 1) * 128, :]
                            ).then_inc(s_pl, 16)

                @blk.tensor
                def _(t_):
                    t_.wait_ge(s_wl, 16 * NHC * (dpass + 1))
                    for tt in range(NTILE):
                        gi = dpass * NTILE + tt
                        t_.wait_ge(s_gc[tt % 2],
                                   16 * (dpass * NTILE // 2 + tt // 2 + 1))
                        psum = pA[:, 0:1024] if tt % 2 == 0 else pB[:, 0:1024]
                        if tt >= 2:
                            if dpass == 0:
                                t_.wait_ge(s_ev, tt - 1)
                            else:
                                t_.wait_ge(s_x0, tt - 1)
                        for h in range(NHC):
                            for n in range(2):
                                inst = t_.matmul(
                                    psum[:, n * 512:(n + 1) * 512],
                                    gcol[tt % 2][:, h * 128:(h + 1) * 128],
                                    wdh[h][:, n * 512:(n + 1) * 512],
                                    start=(h == 0), stop=(h == NHC - 1))
                        inst.then_inc(s_mm, 1)

                if dpass == 0:
                    @blk.vector
                    def _(v):
                        for tt in range(NTILE):
                            v.wait_ge(s_mm, tt + 1)
                            if tt >= 1:
                                v.wait_ge(s_ps, 16 * tt)
                            psum = (pA[:, 0:1024] if tt % 2 == 0
                                    else pB[:, 0:1024])
                            v.tensor_copy(prt[:, :], psum).then_inc(s_ev, 1)

                    @blk.scalar
                    def _(sc):
                        for tt in range(NTILE):
                            sc.wait_ge(s_ev, tt + 1)
                            sc.dma_start(
                                E["d_part0"].ap()[tt * 128:(tt + 1) * 128, :],
                                prt[:, :]).then_inc(s_ps, 16)
                        sc.wait_ge(s_ps, 16 * NTILE)
                else:
                    @blk.vector
                    def _(v):
                        for tt in range(NTILE):
                            par = tt % 2
                            v.wait_ge(s_mm, NTILE + tt + 1)
                            v.wait_ge(s_xl, 16 * (tt + 1))
                            v.wait_ge(s_pl, 16 * (tt + 1))
                            if tt >= 1:
                                v.wait_ge(s_rs, tt)
                            psum = (pA[:, 0:1024] if par == 0
                                    else pB[:, 0:1024])
                            v.tensor_add(dfl[:, 0:1024], prt[:, :],
                                         x1t6[:, 0:1024])
                            v.tensor_add(dfl[:, 1024:2048], psum,
                                         x1t6[:, 1024:2048]).then_inc(s_x0, 1)
                            v.wait_ge(s_st, tt + 1)
                            v.scalar_tensor_tensor(ob6[par], LCbc(par, 0),
                                                   1.0 / H, dfl[:, :],
                                                   ALU.mult, ALU.subtract
                                                   ).then_inc(s_xc, 1)
                            v.wait_ge(s_rs, tt + 1)
                            v.tensor_mul(ob6[par], ob6[par], LCbc(par, 6)
                                         ).then_inc(s_ap, 1)

                if dpass == 1:
                    @blk.scalar
                    def _(sc):
                        for tt in range(NTILE):
                            par = tt % 2
                            sc.wait_ge(s_x0, tt + 1)
                            if tt >= 2:
                                sc.wait_ge(s_os[tt % 2], 16 * (tt // 2))
                            sc.activation(ob6[par], dfl[:, :], AF.Copy,
                                          accum_out=LC(par, 0))
                            sc.drain()
                            sc.nop().then_inc(s_st, 1)
                            sc.wait_ge(s_xc, tt + 1)
                            sc.activation(dfl[:, :], ob6[par], AF.Square,
                                          accum_out=LC(par, 1))
                            sc.drain()
                            sc.activation(LC(par, 4), LC(par, 1), AF.Ln,
                                          scale=1.0 / H, bias=EPS)
                            sc.drain()
                            sc.activation(LC(par, 5), LC(par, 4), AF.Exp,
                                          scale=-0.5)
                            sc.drain()
                            sc.activation(LC(par, 6), LC(par, 5), AF.Copy,
                                          scale=-1.0)
                            sc.drain()
                            sc.nop().then_inc(s_rs, 1)
                            sc.wait_ge(s_ap, tt + 1)
                            sc.dma_start(
                                E["d_out"].ap()[tt * 128:(tt + 1) * 128, :],
                                ob6[par]).then_inc(s_os[tt % 2], 16)
                        for i in range(2):
                            sc.wait_ge(s_os[i], 16 * NTILE // 2)


INPUT_MAP = [
    ("h0", "L0_attn_hidden_states"), ("wz", "L0_attn_Wz"), ("wx", "L0_attn_Wx"),
    ("wb", "L0_attn_WB"), ("wc", "L0_attn_WC"), ("wdt", "L0_attn_Wdt"),
    ("convw", "L0_attn_conv_w"), ("convb", "L0_attn_conv_b"),
    ("dtb", "L0_attn_dt_bias"), ("alog", "L0_attn_A_log"),
    ("dvec", "L0_attn_D"), ("nw", "L0_attn_norm_w"), ("wout", "L0_attn_Wout"),
    ("wgate", "L0_ffn_w_gate"), ("wup", "L0_ffn_w_up"), ("wdown", "L0_ffn_w_down"),
]


def make_in_maps(inputs, n_cores=8):
    hs = np.ascontiguousarray(np.asarray(inputs["L0_attn_hidden_states"],
                                         dtype=np.float32))
    shared = {}
    for dev, ref in INPUT_MAP:
        if dev == "h0":
            continue
        shared[dev] = np.ascontiguousarray(np.asarray(inputs[ref],
                                                      dtype=np.float32))
    maps = []
    for b in range(n_cores):
        m = dict(shared)
        m["h0"] = np.ascontiguousarray(hs[b])
        maps.append(m)
    return maps


def kernel(**inputs):
    nc = build_program()
    maps = make_in_maps(inputs)
    res = run_bass_kernel_spmd(nc, maps, core_ids=list(range(8)))
    out = np.stack([np.asarray(r["out"], dtype=np.float32)
                    for r in res.results], axis=0)
    return out


# version 07
# speedup vs baseline: 1.1884x; 1.0901x over previous
import os
from contextlib import ExitStack

import numpy as np
import concourse.bass as bass
import concourse.mybir as mybir
from concourse.bass_utils import run_bass_kernel_spmd

F32 = mybir.dt.float32
BF16 = mybir.dt.bfloat16
AF = mybir.ActivationFunctionType
ALU = mybir.AluOpType
AP = bass.AP

S = 4096
H = 2048
INTER = 4096
NH, HD, NS, NG = 64, 64, 128, 8
GN = NG * NS
CONV = INTER + 2 * GN
DFF = 8192
Q = 128
NCH = S // Q
KH = H // 128
KI = INTER // 128
NXT = CONV // 128
HALF = S // 2
NTB = HALF // 512
NTT = HALF // 128
EPS = 1e-5
CLAMP = 50.0


def build_program(dbg=(), phases=6, nch3=NCH):
    nc = bass.Bass("TRN2", target_bir_lowering=False,
                   detect_race_conditions=False)
    dbg = set(dbg)
    E = {}

    def sem(name):
        return nc.alloc_semaphore(name=name)

    def scratch(name, shape, dtype):
        kind = "ExternalOutput" if name in dbg else "Internal"
        return nc.dram_tensor(name, shape, dtype, kind=kind)

    E["d_h0"] = nc.dram_tensor("h0", [S, H], F32, kind="ExternalInput")
    E["d_wz"] = nc.dram_tensor("wz", [H, INTER], F32, kind="ExternalInput")
    E["d_wx"] = nc.dram_tensor("wx", [H, INTER], F32, kind="ExternalInput")
    E["d_wb"] = nc.dram_tensor("wb", [H, GN], F32, kind="ExternalInput")
    E["d_wc"] = nc.dram_tensor("wc", [H, GN], F32, kind="ExternalInput")
    E["d_wdt"] = nc.dram_tensor("wdt", [H, NH], F32, kind="ExternalInput")
    E["d_cw"] = nc.dram_tensor("convw", [CONV, 4], F32, kind="ExternalInput")
    E["d_cb"] = nc.dram_tensor("convb", [CONV], F32, kind="ExternalInput")
    E["d_dtb"] = nc.dram_tensor("dtb", [NH], F32, kind="ExternalInput")
    E["d_alog"] = nc.dram_tensor("alog", [NH], F32, kind="ExternalInput")
    E["d_dvec"] = nc.dram_tensor("dvec", [NH], F32, kind="ExternalInput")
    E["d_nw"] = nc.dram_tensor("nw", [INTER], F32, kind="ExternalInput")
    E["d_wout"] = nc.dram_tensor("wout", [INTER, H], F32, kind="ExternalInput")
    E["d_wg"] = nc.dram_tensor("wgate", [DFF, H], F32, kind="ExternalInput")
    E["d_wu"] = nc.dram_tensor("wup", [DFF, H], F32, kind="ExternalInput")
    E["d_wd"] = nc.dram_tensor("wdown", [H, DFF], F32, kind="ExternalInput")
    E["d_out"] = nc.dram_tensor("out", [S, H], F32, kind="ExternalOutput")

    E["d_hbf"] = scratch("h_bf", [S, H], BF16)
    E["d_woutb"] = scratch("wout_bf", [INTER, H], BF16)
    E["d_wgb"] = scratch("wg_bf", [DFF, H], BF16)
    E["d_wub"] = scratch("wu_bf", [DFF, H], BF16)
    E["d_wdb"] = scratch("wd_bf", [H, DFF], BF16)
    E["d_wgT"] = scratch("wgT", [H, DFF], BF16)
    E["d_wuT"] = scratch("wuT", [H, DFF], BF16)
    E["d_wdT"] = scratch("wdT", [DFF, H], BF16)
    E["d_xbc"] = scratch("xbc_bf", [CONV, S], BF16)
    E["d_sg"] = scratch("sg_h", [S, INTER], F32)
    E["d_tabs"] = scratch("tab_s", [NH, S], F32)
    E["d_tabsC"] = scratch("tab_sc", [NCH * NH, Q], F32)
    E["d_dec"] = scratch("dec_h", [NCH, NH], F32)
    E["d_norm"] = scratch("normed_h", [S, INTER], BF16)
    E["d_x1f"] = scratch("x1f_h", [S, H], F32)
    E["d_x1b"] = scratch("x1b_h", [S, H], BF16)
    E["d_gated"] = scratch("gated_h", [DFF, S], BF16)
    E["d_part0"] = scratch("part0_h", [S, H // 2], F32)

    pA = nc.alloc_psum_tensor("pA", [128, 2048], F32)
    pB = nc.alloc_psum_tensor("pB", [128, 2048], F32)
    E["pA"], E["pB"] = pA, pB
    E["bankA"] = lambda i: pA[:, i * 512:(i + 1) * 512]
    E["bankB"] = lambda i: pB[:, i * 512:(i + 1) * 512]

    sb = nc.alloc_sbuf_tensor
    E["cw_sb"] = sb("cw_sb", [128, 4 * NXT], F32)
    E["cb_sb"] = sb("cb_sb", [128, NXT], F32)
    E["dtb_sb"] = sb("dtb_sb", [64, 1], F32)
    E["alog_sb"] = sb("alog_sb", [64, 1], F32)
    E["negA_sb"] = sb("negA_sb", [64, 1], F32)
    E["dbcb_sb"] = sb("dbcb_sb", [128, 64], BF16)
    E["dbcf_sb"] = sb("dbcf_sb", [128, 64], F32)
    E["nw_sb"] = sb("nw_sb", [128, KI], F32)
    E["onesb_sb"] = sb("onesb_sb", [128, 128], BF16)
    E["onesf_sb"] = sb("onesf_sb", [64, 64], F32)
    E["onesf128"] = sb("onesf128", [128, 128], F32)
    E["id128f"] = sb("id128f", [128, 128], F32)
    E["caus_sb"] = sb("caus_sb", [128, 128], BF16)
    E["id64_sb"] = sb("id64_sb", [64, 64], F32)
    E["s_pc"] = sem("pc")

    CONSTS = [EPS, -0.5, -1.0, 1.0 / INTER, 1.0 / H, CLAMP]
    E["const_tiles"] = []
    for i, val in enumerate(CONSTS):
        t = sb(f"cst{i}", [128, 1], F32)
        nc.const_aps.register(F32, val, t[:, :])
        E["const_tiles"].append((t, val))

    with nc.named_scope("P1"):
        build_p1(nc, sem, E)
    if phases >= 2:
        with (
            nc.sbuf_tensor("pack1", [128, S], F32) as pack1,
            nc.sbuf_tensor("pack2", [128, S], F32) as pack2,
        ):
            E["pack1"], E["pack2"] = pack1, pack2
            with nc.named_scope("P2"):
                build_p2(nc, sem, E)
            if phases >= 3:
                with nc.named_scope("P3"):
                    build_p3(nc, sem, E, nch3)
    if phases >= 4:
        with nc.named_scope("P4"):
            build_p4(nc, sem, E)
    if phases >= 5:
        with nc.named_scope("P5"):
            build_p5(nc, sem, E)
    if phases >= 6:
        with nc.named_scope("P6"):
            build_p6(nc, sem, E)
    if phases >= 2:
        with nc.Block() as blk:
            @blk.gpsimd
            def _(g):
                g.wait_ge(E["s_pc"], 16 * E["NPC"])
    return nc


def build_p1(nc, sem, E):
    s_ld = sem("p1_ld")
    E["s_hc"] = s_hc = sem("p1_hc")

    with nc.Block() as blk:
        @blk.sync
        def _(sy):
            n = 0
            sy.dma_start(E["cw_sb"][:, :],
                         AP(E["d_cw"], 0, [[4, 128], [512, NXT], [1, 4]])
                         ).then_inc(s_ld, 16); n += 16
            sy.dma_start(E["cb_sb"][:, :],
                         AP(E["d_cb"], 0, [[1, 128], [128, NXT]]),
                         allow_slow_non_contiguous=True).then_inc(s_ld, 16); n += 16
            sy.dma_start(E["dtb_sb"][:, :], AP(E["d_dtb"], 0, [[1, 64], [1, 1]])
                         ).then_inc(s_ld, 16); n += 16
            sy.dma_start(E["alog_sb"][:, :], AP(E["d_alog"], 0, [[1, 64], [1, 1]])
                         ).then_inc(s_ld, 16); n += 16
            sy.dma_start(E["dbcf_sb"][:, :], AP(E["d_dvec"], 0, [[0, 128], [1, 64]])
                         ).then_inc(s_ld, 16); n += 16
            sy.dma_start(E["nw_sb"][:, :], AP(E["d_nw"], 0, [[1, 128], [128, KI]]),
                         allow_slow_non_contiguous=True).then_inc(s_ld, 16); n += 16
            sy.wait_ge(s_ld, n)

        @blk.gpsimd
        def _(g):
            for i in range(4):
                g.dma_start(E["d_hbf"].ap()[i * 1024:(i + 1) * 1024, :],
                            E["d_h0"].ap()[i * 1024:(i + 1) * 1024, :]
                            ).then_inc(s_hc, 16)
            g.memset(E["onesb_sb"][:, :], 1.0)
            g.memset(E["onesf_sb"][:, :], 1.0)
            g.memset(E["onesf128"][:, :], 1.0)
            for t, val in E["const_tiles"]:
                g.memset(t[:, :], val)

    with nc.Block() as blk:
        @blk.gpsimd
        def _(g):
            g.affine_select(E["caus_sb"][:, :], E["onesb_sb"][:, :], [[1, 128]],
                            ALU.is_ge, 0.0, channel_multiplier=-1)
            g.affine_select(E["id64_sb"][:, :], E["onesf_sb"][:, :], [[1, 64]],
                            ALU.is_equal, 0.0, channel_multiplier=-1)
            g.affine_select(E["id128f"][:, :], E["onesf128"][:, :], [[1, 128]],
                            ALU.is_equal, 0.0, channel_multiplier=-1)

        @blk.scalar
        def _(sc):
            sc.activation(E["negA_sb"][:, :], E["alog_sb"][:, :], AF.Exp)

        @blk.vector
        def _(v):
            v.tensor_copy(E["dbcb_sb"][:, :], E["dbcf_sb"][:, :])

    with nc.Block() as blk:
        @blk.vector
        def _(v):
            v.tensor_scalar_mul(E["negA_sb"][:, :], E["negA_sb"][:, :], -1.0)


def build_p2(nc, sem, E):
    bankA, bankB, pA = E["bankA"], E["bankB"], E["pA"]
    pack1, pack2 = E["pack1"], E["pack2"]
    s_pc, s_hc = E["s_pc"], E["s_hc"]

    s_ht = sem("p2_ht"); s_mm = sem("p2_mm")
    s_wld2 = [sem("p2_wld0"), sem("p2_wld1")]; s_wldt = sem("p2_wldt")
    s_ev = sem("p2_ev"); s_cv = sem("p2_cv"); s_si = sem("p2_si")
    s_evP = [sem("p2_evA"), sem("p2_evB")]
    s_sp = sem("p2_sp"); s_gld = sem("p2_gld"); s_gmm = sem("p2_gmm")
    s_gev = sem("p2_gev"); s_dmm = sem("p2_dmm")
    s_gst4 = [sem(f"p2_gst{i}") for i in range(4)]
    s_dev = sem("p2_dev"); s_tsp = sem("p2_tsp")

    W_SRCS = []
    for t in range(NXT):
        if t < 32:
            W_SRCS.append((E["d_wx"], t * 128, INTER))
        elif t < 40:
            W_SRCS.append((E["d_wb"], (t - 32) * 128, GN))
        else:
            W_SRCS.append((E["d_wc"], (t - 40) * 128, GN))

    PCH = ([(E["d_wg"], E["d_wgb"], i * 1024) for i in range(8)]
           + [(E["d_wu"], E["d_wub"], i * 1024) for i in range(8)])
    PCH_LATE = ([(E["d_wout"], E["d_woutb"], i * 1024) for i in range(4)]
                + [(E["d_wd"], E["d_wdb"], i * 1024) for i in range(2)])
    E["NPC"] = NPC = len(PCH) + len(PCH_LATE)
    E["NPC_EARLY"] = NPC_EARLY = len(PCH)

    def w_lhsT_ap(dw, of0, ncols):
        return AP(dw, of0, [[ncols, 128], [128 * ncols, KH], [1, 128]])

    def gw_rhs_ap(dw, ofb, ncols):
        return AP(dw, ofb * 512, [[ncols, 128], [128 * ncols, KH], [1, 512]])


    with ExitStack() as _st:
        hTb = _st.enter_context(nc.sbuf_tensor("hT", [128, KH * HALF], BF16))
        rb0 = _st.enter_context(nc.sbuf_tensor("rowb0", [128, HALF + 8], F32))
        rb1 = _st.enter_context(nc.sbuf_tensor("rowb1", [128, HALF + 8], F32))
        rowc = _st.enter_context(nc.sbuf_tensor("rowc", [128, HALF], F32))
        wxbb = _st.enter_context(nc.sbuf_tensor("wxb", [128, 2 * KH * 128], BF16))
        gwbb = _st.enter_context(nc.sbuf_tensor("gwb", [128, KH * 512], BF16))
        gevb = _st.enter_context(nc.sbuf_tensor("gev", [128, 4 * 512], F32))
        wdtb = _st.enter_context(nc.sbuf_tensor("wdtb", [128, KH * 64], BF16))
        carry = _st.enter_context(nc.sbuf_tensor("carry", [128, 3 * NXT], F32))
        slotA = _st.enter_context(nc.sbuf_tensor("slotA", [64, S], F32))
        slotB = _st.enter_context(nc.sbuf_tensor("slotB", [64, S], F32))
        slotC = _st.enter_context(nc.sbuf_tensor("slotC", [64, S], F32))
        zseg = _st.enter_context(nc.sbuf_tensor("zseg", [64, Q], F32))
        slast = _st.enter_context(nc.sbuf_tensor("slast", [64, NCH], F32))
        decT = _st.enter_context(nc.sbuf_tensor("decT", [NCH, 64], F32))
        hT = [hTb[:, k * HALF:(k + 1) * HALF] for k in range(KH)]
        rowbuf = [rb0, rb1]
        wxbuf = [wxbb[:, 0:KH * 128], wxbb[:, KH * 128:2 * KH * 128]]
        gwbuf = [gwbb, gwbb]
        gev = [gevb[:, i * 512:(i + 1) * 512] for i in range(4)]

        with nc.Block() as blk:
            @blk.gpsimd
            def _(g):
                g.memset(rb0[:, 0:3], 0.0)
                g.memset(rb1[:, 0:3], 0.0)
                g.memset(zseg[:, :], 0.0)
                g.memset(pack2[64:128, :], 0.0)

        pci = [0]

        for half in range(2):
            tok0 = half * HALF
            with nc.Block() as blk:
                @blk.sync
                def _(sy):
                    if half == 0:
                        sy.wait_ge(s_hc, 64)
                    for k in range(KH):
                        sy.dma_start_transpose(
                            hT[k],
                            E["d_hbf"].ap()[tok0:tok0 + HALF,
                                            k * 128:(k + 1) * 128]
                        ).then_inc(s_ht, 16)
                    for ofb in range(8):
                        for tt in range(NTT):
                            gidx = (half * 8 + ofb) * NTT + tt
                            sy.wait_ge(s_gev, gidx + 1)
                            sy.dma_start(
                                E["d_sg"].ap()[tok0 + tt * 128:
                                               tok0 + (tt + 1) * 128,
                                               ofb * 512:(ofb + 1) * 512],
                                gev[gidx % 4]).then_inc(s_gst4[gidx % 4], 16)
                    for i in range(4):
                        sy.wait_ge(s_gst4[i], 16 * (half + 1) * 32)

                @blk.gpsimd
                def _(g):
                    for t in range(NXT):
                        gi = half * NXT + t
                        if gi >= 2:
                            g.wait_ge(s_cv, gi - 1)
                        dw, of0, ncols = W_SRCS[t]
                        g.dma_start(wxbuf[gi % 2][:, :],
                                    w_lhsT_ap(dw, of0, ncols)
                                    ).then_inc(s_wld2[gi % 2], 16)
                        if half == 0 and t == 0:
                            g.dma_start(
                                wdtb[:, :],
                                AP(E["d_wdt"], 0,
                                   [[NH, 128], [128 * NH, KH], [1, 64]])
                            ).then_inc(s_wldt, 16)
                        target = (half * NXT + t) // 2
                        while pci[0] < min(target, NPC_EARLY):
                            src, dst, r0 = PCH[pci[0]]
                            g.dma_start(dst.ap()[r0:r0 + 1024, :],
                                        src.ap()[r0:r0 + 1024, :]
                                        ).then_inc(s_pc, 16)
                            pci[0] += 1
                        if gi >= 1:
                            g.wait_ge(s_si, gi)
                            tp, hp = (gi - 1) % NXT, (gi - 1) // NXT
                            g.dma_start(
                                E["d_xbc"].ap()[tp * 128:(tp + 1) * 128,
                                                hp * HALF:(hp + 1) * HALF],
                                rowc[:, :]).then_inc(s_sp, 16)
                    for ofb in range(8):
                        gio = half * 8 + ofb
                        if gio >= 1:
                            g.wait_ge(s_gmm, gio * NTT)
                        g.dma_start(gwbuf[gio % 2][:, :],
                                    gw_rhs_ap(E["d_wz"], ofb, INTER)
                                    ).then_inc(s_gld, 16)
                    gi_end = (half + 1) * NXT
                    g.wait_ge(s_si, gi_end)
                    g.dma_start(
                        E["d_xbc"].ap()[(NXT - 1) * 128:NXT * 128,
                                        half * HALF:(half + 1) * HALF],
                        rowc[:, :]).then_inc(s_sp, 16)
                    if half == 1:
                        while pci[0] < NPC_EARLY:
                            src, dst, r0 = PCH[pci[0]]
                            g.dma_start(dst.ap()[r0:r0 + 1024, :],
                                        src.ap()[r0:r0 + 1024, :]
                                        ).then_inc(s_pc, 16)
                            pci[0] += 1
                        for src, dst, r0 in PCH_LATE:
                            g.dma_start(dst.ap()[r0:r0 + 1024, :],
                                        src.ap()[r0:r0 + 1024, :]
                                        ).then_inc(s_pc, 16)
                    g.wait_ge(s_sp, 16 * gi_end)

                @blk.tensor
                def _(t_):
                    t_.wait_ge(s_ht, 16 * KH * (half + 1))
                    for t in range(NXT):
                        gi = half * NXT + t
                        t_.wait_ge(s_wld2[gi % 2], 16 * (gi // 2 + 1))
                        for tb in range(NTB):
                            idx = gi * NTB + tb
                            if idx >= 2:
                                t_.wait_ge(s_evP[idx % 2], idx // 2)
                            for k in range(KH):
                                inst = t_.matmul(
                                    bankA(idx % 2),
                                    wxbuf[gi % 2][:, k * 128:(k + 1) * 128],
                                    hT[k][:, tb * 512:(tb + 1) * 512],
                                    start=(k == 0), stop=(k == KH - 1))
                            inst.then_inc(s_mm, 1)
                    for ofb in range(8):
                        gio = half * 8 + ofb
                        t_.wait_ge(s_gld, 16 * (gio + 1))
                        for tt in range(NTT):
                            gidx = gio * NTT + tt
                            if gidx >= 2:
                                t_.wait_ge(s_gev, gidx - 1)
                            for k in range(KH):
                                inst = t_.matmul(
                                    bankB(gidx % 2),
                                    hT[k][:, tt * 128:(tt + 1) * 128],
                                    gwbuf[gio % 2][:, k * 512:(k + 1) * 512],
                                    start=(k == 0), stop=(k == KH - 1))
                            inst.then_inc(s_gmm, 1)
                    if half == 0:
                        t_.wait_ge(s_wldt, 16)
                    for tb in range(NTB):
                        didx = half * NTB + tb
                        if didx >= 2:
                            t_.wait_ge(s_dev, didx - 1)
                        for k in range(KH):
                            inst = t_.matmul(
                                bankB(2 + didx % 2)[0:64, :],
                                wdtb[:, k * 64:(k + 1) * 64],
                                hT[k][:, tb * 512:(tb + 1) * 512],
                                start=(k == 0), stop=(k == KH - 1))
                        inst.then_inc(s_dmm, 1)

                @blk.scalar
                def _(sc):
                    for t in range(NXT):
                        gi = half * NXT + t
                        for tb in (0, 2):
                            idx = gi * NTB + tb
                            sc.wait_ge(s_mm, idx + 1)
                            if gi >= 2 and tb == 0:
                                sc.wait_ge(s_cv, gi - 1)
                            sc.activation(
                                rowbuf[gi % 2][:, 3 + tb * 512:
                                               3 + (tb + 1) * 512],
                                bankA(idx % 2), AF.Copy).then_inc(s_evP[0], 1)
                        sc.wait_ge(s_cv, gi + 1)
                        sc.activation(rowc[:, :], rowc[:, :], AF.Silu)
                        sc.drain()
                        sc.nop().then_inc(s_si, 1)
                    for ofb in range(8):
                        for tt in range(NTT):
                            gidx = (half * 8 + ofb) * NTT + tt
                            sc.wait_ge(s_gmm, gidx + 1)
                            if gidx >= 4:
                                sc.wait_ge(s_gst4[gidx % 4], 16 * (gidx // 4))
                            sc.activation(gev[gidx % 4], bankB(gidx % 2),
                                          AF.Silu).then_inc(s_gev, 1)

                @blk.vector
                def _(v):
                    for t in range(NXT):
                        gi = half * NXT + t
                        for tb in (1, 3):
                            idx = gi * NTB + tb
                            v.wait_ge(s_mm, idx + 1)
                            v.tensor_copy(
                                rowbuf[gi % 2][:, 3 + tb * 512:
                                               3 + (tb + 1) * 512],
                                bankA(idx % 2)).then_inc(s_evP[1], 1)
                        v.wait_ge(s_evP[0], 2 * (gi + 1))
                        if gi >= 1:
                            v.wait_ge(s_sp, 16 * gi)
                        rb = rowbuf[gi % 2]
                        if half == 1:
                            v.tensor_copy(rb[:, 0:3], carry[:, 3 * t:3 * t + 3])
                        v.tensor_scalar(rowc[:, :], rb[:, 0:HALF],
                                        E["cw_sb"][:, 4 * t:4 * t + 1],
                                        E["cb_sb"][:, t:t + 1],
                                        ALU.mult, ALU.add)
                        for kk in range(1, 4):
                            v.scalar_tensor_tensor(
                                rowc[:, :], rb[:, kk:kk + HALF],
                                E["cw_sb"][:, 4 * t + kk:4 * t + kk + 1],
                                rowc[:, :], ALU.mult, ALU.add)
                        if half == 0:
                            v.tensor_copy(carry[:, 3 * t:3 * t + 3],
                                          rb[:, HALF:HALF + 3])
                        v.engine_nop().then_inc(s_cv, 1)
                    for tb in range(NTB):
                        didx = half * NTB + tb
                        v.wait_ge(s_dmm, didx + 1)
                        v.tensor_copy(
                            slotA[:, tok0 + tb * 512:tok0 + (tb + 1) * 512],
                            bankB(2 + didx % 2)[0:64, :]).then_inc(s_dev, 1)

        def vb(f):
            with nc.Block() as blk:
                blk.vector(f)

        def scb(f):
            with nc.Block() as blk:
                blk.scalar(f)

        vb(lambda v: v.tensor_scalar(slotA[:, :], slotA[:, :],
                                     E["dtb_sb"][:, :], None, ALU.add))
        scb(lambda sc: sc.activation(slotB[:, :], slotA[:, :], AF.Abs))
        scb(lambda sc: sc.activation(slotB[:, :], slotB[:, :], AF.Exp,
                                     scale=-1.0))
        scb(lambda sc: sc.activation(slotB[:, :], slotB[:, :], AF.Ln, bias=1.0))
        vb(lambda v: v.tensor_relu(slotC[:, :], slotA[:, :]))
        vb(lambda v: v.tensor_add(slotC[:, :], slotC[:, :], slotB[:, :]))
        vb(lambda v: v.tensor_scalar_mul(slotB[:, :], slotC[:, :],
                                         E["negA_sb"][:, :]))
        scb(lambda sc: sc.activation(slotC[:, :], slotC[:, :], AF.Ln))

        def scans(v):
            for ch in range(NCH):
                o = ch * Q
                v.tensor_tensor_scan(pack1[0:64, o:o + Q], slotB[:, o:o + Q],
                                     zseg[:, :], 0.0, ALU.add, ALU.add)
                rin = AP(slotB, o + Q - 1, [[S, 64], [-1, Q]])
                rz = AP(zseg, Q - 1, [[Q, 64], [-1, Q]])
                rout = AP(slotA, o + Q - 1, [[S, 64], [-1, Q]])
                v.tensor_tensor_scan(rout, rin, rz, 0.0, ALU.add, ALU.add)
        vb(scans)
        vb(lambda v: v.tensor_sub(slotA[:, :], slotA[:, :], slotB[:, :]))

        def finish_tabs(v):
            v.tensor_sub(slotB[:, :], slotC[:, :], pack1[0:64, :])
            v.tensor_add(pack2[0:64, :], slotA[:, :], slotC[:, :])
            v.tensor_copy(slast[:, :], AP(pack1, Q - 1, [[S, 64], [Q, NCH]]))
        vb(finish_tabs)
        scb(lambda sc: sc.activation(slast[:, :], slast[:, :], AF.Exp))
        s_nwm = sem("p2_nwm")
        with nc.Block() as blk:
            @blk.sync
            def _(sy):
                sy.dma_start(pack1[64:128, :], slotB[:, :]).then_inc(s_nwm, 16)
                sy.wait_ge(s_nwm, 16)

        with nc.Block() as blk:
            @blk.tensor
            def _(t_):
                t_.transpose(pA[0:NCH, 0:64], slast[:, :], E["id64_sb"][:, :])

        vb(lambda v: v.tensor_copy(decT[:, :], pA[0:NCH, 0:64]))

        with nc.Block() as blk:
            @blk.sync
            def _(sy):
                sy.dma_start(E["d_tabs"].ap()[:, :], pack1[0:64, :]
                             ).then_inc(s_tsp, 16)
                sy.dma_start(AP(E["d_tabsC"], 0,
                                [[Q, 64], [NH * Q, NCH], [1, Q]]),
                             pack1[0:64, :],
                             allow_slow_non_contiguous=True
                             ).then_inc(s_tsp, 16)
                sy.dma_start(E["d_dec"].ap()[:, :], decT[:, :]).then_inc(s_tsp, 16)
                sy.wait_ge(s_tsp, 48)


def build_p3(nc, sem, E, nch3=NCH):
    SKIP = set(os.environ.get("P3_SKIP", "").split(","))
    bankA, bankB, pA = E["bankA"], E["bankB"], E["pA"]
    pack1, pack2 = E["pack1"], E["pack2"]

    s_l1 = [sem("p3_l1a"), sem("p3_l1b")]
    s_l2 = [sem("p3_l2a"), sem("p3_l2b")]
    s_l3 = [sem(f"p3_l3{i}") for i in range(4)]
    s_l4 = sem("p3_l4")
    s_l5 = [sem("p3_l5a"), sem("p3_l5b")]
    s_ttr = sem("p3_ttr"); s_tev = sem("p3_tev"); s_ext = sem("p3_ext")
    s_gmm = sem("p3_gmm"); s_gme = sem("p3_gme")
    s_u4 = sem("p3_u4"); s_exp = sem("p3_exp"); s_m4 = sem("p3_m4")
    s_xs = sem("p3_xs")
    s_yim = sem("p3_yim"); s_yis = sem("p3_yis")
    s_yxm = sem("p3_yxm"); s_yfn = sem("p3_yfn")
    s_cmm = sem("p3_cmm"); s_stu = sem("p3_stu"); s_stc = sem("p3_stc")
    s_g = sem("p3_g"); s_rst = sem("p3_rst"); s_nrm = sem("p3_nrm")
    s_nst = sem("p3_nst"); s_cd = sem("p3_cd")
    s_wt3 = sem("p3_wt3")
    s_wtT = [sem("p3_wtT0"), sem("p3_wtT1")]
    s_wtS = [sem("p3_wtS0"), sem("p3_wtS1")]
    E["s_wtS"] = s_wtS
    WT_GU = []
    for mat_src, mat_dst in ((E["d_wgb"], E["d_wgT"]), (E["d_wub"], E["d_wuT"])):
        for k in range(KH):
            for hq in range(4):
                WT_GU.append((mat_src, mat_dst, k, hq))
    E["N_WT_GU"] = len(WT_GU)

    with ExitStack() as _st:
        DIall = _st.enter_context(nc.sbuf_tensor("DIall", [128, NH * 128], BF16))
        state = _st.enter_context(nc.sbuf_tensor("state", [128, NH * 64], F32))
        state_bf = _st.enter_context(nc.sbuf_tensor("state_bf", [128, NH * 64], BF16))
        xtokb = _st.enter_context(nc.sbuf_tensor("xtok", [128, 2 * 5120], BF16))
        bcFb = _st.enter_context(nc.sbuf_tensor("bcF", [128, 2 * 2048], BF16))
        sgch = _st.enter_context(nc.sbuf_tensor("sgch", [128, INTER], BF16))
        decbcb = _st.enter_context(nc.sbuf_tensor("decbc", [128, 2 * 64], F32))
        ubcb = _st.enter_context(nc.sbuf_tensor("ubc", [128, 4 * 512], F32))
        tabT = _st.enter_context(nc.sbuf_tensor("tabT", [128, 192], F32))
        expsT = _st.enter_context(nc.sbuf_tensor("expsT", [128, 64], F32))
        erdtT = _st.enter_context(nc.sbuf_tensor("erdtT", [128, 64], F32))
        Gm = _st.enter_context(nc.sbuf_tensor("Gm", [128, 8 * 128], BF16))
        u4b = _st.enter_context(nc.sbuf_tensor("u4", [128, 4 * 512], F32))
        e4b = _st.enter_context(nc.sbuf_tensor("e4", [128, 4 * 512], BF16))
        M4 = _st.enter_context(nc.sbuf_tensor("M4", [128, NH * 128], BF16))
        xs8 = _st.enter_context(nc.sbuf_tensor("xs8", [128, NH * 64], BF16))
        ybig = _st.enter_context(nc.sbuf_tensor("ybig", [128, INTER], F32))
        tmpg = _st.enter_context(nc.sbuf_tensor("tmpg", [128, 512], F32))
        normbb = _st.enter_context(nc.sbuf_tensor("normbb", [128, 2 * INTER], BF16))
        wtb3b = _st.enter_context(nc.sbuf_tensor("wtb3", [128, DFF // 2], BF16,
                                                 is_transpose_dest=True))
        wtb3 = [wtb3b[:, 0:DFF // 4], wtb3b[:, DFF // 4:DFF // 2]]
        sscol = _st.enter_context(nc.sbuf_tensor("sscol", [128, 1], F32))
        rstdc = _st.enter_context(nc.sbuf_tensor("rstdc", [128, 1], F32))
        xtok = [xtokb[:, 0:5120], xtokb[:, 5120:10240]]
        bcF = [bcFb[:, 0:2048], bcFb[:, 2048:4096]]
        decbc = [decbcb[:, 0:64], decbcb[:, 64:128]]
        ubc = [ubcb[:, i * 512:(i + 1) * 512] for i in range(4)]
        u4 = [u4b[:, i * 512:(i + 1) * 512] for i in range(4)]
        e4 = [e4b[:, i * 512:(i + 1) * 512] for i in range(4)]
        normb = [normbb[:, 0:INTER], normbb[:, INTER:2 * INTER]]

        with nc.Block() as blk:
            @blk.gpsimd
            def _(g):
                g.memset(state[:, :], 0.0)
                g.memset(state_bf[:, :], 0.0)
                g.affine_select(
                    AP(DIall, 0, [[NH * 128, 128], [128, 64], [1, 128]]),
                    AP(E["dbcb_sb"], 0, [[64, 128], [1, 64], [0, 128]]),
                    [[0, 64], [1, 128]], ALU.is_equal, 0.0,
                    channel_multiplier=-1)

        with nc.Block() as blk:
            @blk.sync
            def _(sy):
                for ch in range(nch3):
                    e = ch % 2
                    if ch >= 2:
                        sy.wait_ge(s_cd, ch - 1)
                    sy.dma_start(
                        bcF[e],
                        AP(E["d_xbc"], INTER * S + ch * Q,
                           [[S, 128], [128 * S, 16], [1, Q]])).then_inc(s_l1[e], 16)
                    sy.dma_start_transpose(
                        xtok[e],
                        E["d_xbc"].ap()[0:5120, ch * Q:(ch + 1) * Q]
                    ).then_inc(s_l2[e], 16)
                    sy.dma_start(decbc[e],
                                 AP(E["d_dec"], ch * 64, [[0, 128], [1, 64]])
                                 ).then_inc(s_l5[e], 16)
                    for q in range(16):
                        uq = ch * 16 + q
                        if uq >= 4:
                            sy.wait_ge(s_u4, uq - 3)
                        sy.dma_start(
                            ubc[q % 4],
                            AP(E["d_tabsC"], ch * NH * Q + (4 * q) * Q,
                               [[0, 128], [Q, 4], [1, Q]])
                        ).then_inc(s_l3[q % 4], 16)
                    if ch >= 1:
                        sy.wait_ge(s_nrm, ch)
                        sy.dma_start(E["d_norm"].ap()[(ch - 1) * Q:ch * Q, :],
                                     normb[(ch - 1) % 2][:, :]
                                     ).then_inc(s_nst, 16)
                        sy.wait_ge(s_nst, 16 * ch)
                        sy.sem_inc(s_cd, 1)
                    for j in range(4 * ch, min(4 * ch + 4, len(WT_GU))):
                        src_, dst, k, hq = WT_GU[j]
                        r0 = hq * (DFF // 4)
                        e2 = j % 2
                        if j == 0:
                            sy.wait_ge(E["s_pc"], 16 * E["NPC_EARLY"])
                        if j >= 2:
                            sy.wait_ge(s_wtS[e2], 16 * (j // 2))
                        sy.dma_start_transpose(
                            wtb3[e2],
                            src_.ap()[r0:r0 + DFF // 4, k * 128:(k + 1) * 128]
                        ).then_inc(s_wtT[e2], 16)
                        sy.wait_ge(s_wtT[e2], 16 * (j // 2 + 1))
                        sy.dma_start(
                            dst.ap()[k * 128:(k + 1) * 128, r0:r0 + DFF // 4],
                            wtb3[e2]).then_inc(s_wtS[e2], 16)
                sy.wait_ge(s_nrm, nch3)
                sy.dma_start(E["d_norm"].ap()[(nch3 - 1) * Q:nch3 * Q, :],
                             normb[(nch3 - 1) % 2][:, :]).then_inc(s_nst, 16)
                sy.wait_ge(s_nst, 16 * nch3)
                sy.sem_inc(s_cd, 1)

            @blk.tensor
            def _(t_):
                for ch in range(nch3):
                    e = ch % 2
                    t_.wait_ge(s_tev, 3 * ch)
                    t_.wait_ge(s_gme, 8 * ch)
                    if "tr" not in SKIP:
                        t_.transpose(pA[:, 0:128], pack1[:, ch * Q:(ch + 1) * Q],
                                     E["id128f"][:, :])
                        t_.transpose(pA[:, 128:256],
                                     pack2[:, ch * Q:(ch + 1) * Q],
                                     E["id128f"][:, :]).then_inc(s_ttr, 1)
                    else:
                        t_.nop().then_inc(s_ttr, 1)
                    t_.wait_ge(s_l1[e], 16 * (ch // 2 + 1))
                    t_.wait_ge(s_tev, 3 * ch + 3)
                    for g in range(8):
                        if "g" not in SKIP:
                            t_.matmul(
                                bankA(g // 4)[:, (g % 4) * 128:(g % 4 + 1) * 128],
                                bcF[e][:, g * 128:(g + 1) * 128],
                                bcF[e][:, 1024 + g * 128:1024 + (g + 1) * 128],
                                start=True, stop=True).then_inc(s_gmm, 1)
                        else:
                            t_.nop().then_inc(s_gmm, 1)
                    t_.wait_ge(s_stc, ch)
                    for g in range(8):
                        idx = ch * 8 + g
                        if idx >= 2:
                            t_.wait_ge(s_yis, idx - 1)
                        if "y" not in SKIP:
                            for hh in range(8):
                                h = g * 8 + hh
                                inst = t_.matmul(
                                    bankA(2 + g % 2)[:, hh * 64:(hh + 1) * 64],
                                    bcF[e][:, 1024 + g * 128:1024 + (g + 1) * 128],
                                    state_bf[:, h * 64:(h + 1) * 64],
                                    start=True, stop=True)
                            inst.then_inc(s_yim, 1)
                        else:
                            t_.nop().then_inc(s_yim, 1)
                    t_.wait_ge(s_l2[e], 16 * (ch // 2 + 1))
                    for g in range(8):
                        idx = ch * 8 + g
                        t_.wait_ge(s_m4, ch * 16 + 2 * (g + 1))
                        if idx >= 2:
                            t_.wait_ge(s_yfn, idx - 1)
                        if "y" not in SKIP:
                            for hh in range(8):
                                h = g * 8 + hh
                                inst = t_.matmul(
                                    bankB(g % 2)[:, hh * 64:(hh + 1) * 64],
                                    M4[:, h * 128:(h + 1) * 128],
                                    xtok[e][:, h * 64:(h + 1) * 64],
                                    start=True, stop=True)
                            inst.then_inc(s_yxm, 1)
                        else:
                            t_.nop().then_inc(s_yxm, 1)
                    for g in range(8):
                        idx = ch * 8 + g
                        t_.wait_ge(s_xs, idx + 1)
                        if idx >= 2:
                            t_.wait_ge(s_stu, idx - 1)
                        if "c" not in SKIP:
                            for hh in range(8):
                                h = g * 8 + hh
                                inst = t_.matmul(
                                    bankB(2 + g % 2)[:, hh * 64:(hh + 1) * 64],
                                    xtok[e][:, 4096 + g * 128:4096 + (g + 1) * 128],
                                    xs8[:, h * 64:(h + 1) * 64],
                                    start=True, stop=True)
                            inst.then_inc(s_cmm, 1)
                        else:
                            t_.nop().then_inc(s_cmm, 1)

            @blk.vector
            def _(v):
                for ch in range(nch3):
                    e = ch % 2
                    if ch >= 2:
                        v.wait_ge(s_cd, ch - 1)
                    v.wait_ge(s_ttr, ch + 1)
                    if ch >= 1:
                        v.wait_ge(s_ext, ch)
                    if "tr" not in SKIP:
                        v.tensor_copy(tabT[:, :], pA[:, 0:192])
                    v.engine_nop().then_inc(s_tev, 3)
                    for g in range(8):
                        v.wait_ge(s_gmm, ch * 8 + (4 if g < 4 else 8))
                        if "g" not in SKIP:
                            v.tensor_mul(
                                Gm[:, g * 128:(g + 1) * 128],
                                bankA(g // 4)[:, (g % 4) * 128:(g % 4 + 1) * 128],
                                E["caus_sb"][:, :]).then_inc(s_gme, 1)
                        else:
                            v.engine_nop().then_inc(s_gme, 1)
                    if ch >= 1:
                        v.wait_ge(s_yxm, 8 * ch)
                    for qq in range(18):
                        if qq < 16:
                            q = qq
                            v.wait_ge(s_l3[q % 4],
                                      16 * ((ch * 16 + q) // 4 + 1))
                            if ch * 16 + q >= 4:
                                v.wait_ge(s_exp, ch * 16 + q - 3)
                            if "u" not in SKIP:
                                for hh in range(4):
                                    h = q * 4 + hh
                                    v.tensor_scalar(
                                        u4[q % 4][:, hh * 128:(hh + 1) * 128],
                                        ubc[q % 4][:, hh * 128:(hh + 1) * 128],
                                        tabT[:, 64 + h:64 + h + 1], CLAMP,
                                        ALU.add, ALU.min)
                            v.engine_nop().then_inc(s_u4, 1)
                        if qq >= 2:
                            q = qq - 2
                            v.wait_ge(s_exp, ch * 16 + q + 1)
                            g = q // 2
                            v.wait_ge(s_gme, ch * 8 + g + 1)
                            if "m" not in SKIP:
                                gm_ap = AP(Gm, g * 128,
                                           [[1024, 128], [0, 4], [1, 128]])
                                v.tensor_tensor(M4[:, q * 512:(q + 1) * 512],
                                                e4[q % 4], gm_ap, ALU.mult)
                                v.tensor_add(M4[:, q * 512:(q + 1) * 512],
                                             M4[:, q * 512:(q + 1) * 512],
                                             DIall[:, q * 512:(q + 1) * 512]
                                             ).then_inc(s_m4, 1)
                            else:
                                v.engine_nop().then_inc(s_m4, 1)
                    v.wait_ge(s_l2[e], 16 * (ch // 2 + 1))
                    v.wait_ge(s_ext, ch + 1)
                    if ch >= 1:
                        v.wait_ge(s_cmm, 8 * ch)
                    for g in range(8):
                        if "c" not in SKIP:
                            erd_ap = AP(erdtT, 8 * g,
                                        [[64, 128], [1, 8], [0, 64]])
                            v.tensor_mul(xs8[:, g * 512:(g + 1) * 512],
                                         xtok[e][:, g * 512:(g + 1) * 512],
                                         erd_ap).then_inc(s_xs, 1)
                        else:
                            v.engine_nop().then_inc(s_xs, 1)
                    for g in range(8):
                        v.wait_ge(s_yim, ch * 8 + g + 1)
                        if "y" not in SKIP:
                            ex_ap = AP(expsT, 8 * g,
                                       [[64, 128], [1, 8], [0, 64]])
                            v.tensor_mul(ybig[:, g * 512:(g + 1) * 512],
                                         bankA(2 + g % 2), ex_ap
                                         ).then_inc(s_yis, 1)
                        else:
                            v.engine_nop().then_inc(s_yis, 1)
                    for g in range(8):
                        v.wait_ge(s_yxm, ch * 8 + g + 1)
                        if "y" not in SKIP:
                            v.tensor_add(ybig[:, g * 512:(g + 1) * 512],
                                         bankB(g % 2),
                                         ybig[:, g * 512:(g + 1) * 512]
                                         ).then_inc(s_yfn, 1)
                        else:
                            v.engine_nop().then_inc(s_yfn, 1)
                    v.wait_ge(s_yim, ch * 8 + 8)
                    v.wait_ge(s_l5[e], 16 * (ch // 2 + 1))
                    for g in range(8):
                        v.wait_ge(s_cmm, ch * 8 + g + 1)
                        if "c" not in SKIP:
                            dec_ap = AP(decbcb, e * 64 + 8 * g,
                                        [[128, 128], [1, 8], [0, 64]])
                            v.tensor_mul(tmpg[:, :],
                                         state[:, g * 512:(g + 1) * 512],
                                         dec_ap)
                            v.tensor_add(state[:, g * 512:(g + 1) * 512],
                                         tmpg[:, :], bankB(2 + g % 2)
                                         ).then_inc(s_stu, 1)
                        else:
                            v.engine_nop().then_inc(s_stu, 1)
                    v.tensor_copy(state_bf[:, :], state[:, :]).then_inc(s_stc, 1)
                    v.wait_ge(s_l4, 16 * (ch + 1))
                    if "n" not in SKIP:
                        v.tensor_mul(ybig[:, :], ybig[:, :], sgch[:, :])
                    v.engine_nop().then_inc(s_g, 1)
                    v.wait_ge(s_rst, ch + 1)
                    if "n" not in SKIP:
                        v.tensor_mul(normb[e], ybig[:, :],
                                     AP(rstdc, 0, [[1, 128], [0, INTER]]))
                    v.engine_nop().then_inc(s_nrm, 1)

            @blk.gpsimd
            def _(gp):
                for ch in range(nch3):
                    if ch >= 1:
                        gp.wait_ge(s_g, ch)
                    gp.dma_start(sgch[:, :],
                                 E["d_sg"].ap()[ch * Q:(ch + 1) * Q, :]
                                 ).then_inc(s_l4, 16)

            @blk.scalar
            def _(sc):
                for ch in range(nch3):
                    e = ch % 2
                    if ch >= 1:
                        sc.wait_ge(s_yis, 8 * ch)
                        sc.wait_ge(s_xs, 8 * ch)
                    sc.wait_ge(s_tev, 3 * ch + 3)
                    if "tr" not in SKIP:
                        sc.activation(expsT[:, :], tabT[:, 0:64], AF.Exp)
                        sc.activation(erdtT[:, :], tabT[:, 128:192],
                                      AF.Exp).then_inc(s_ext, 1)
                    else:
                        sc.nop().then_inc(s_ext, 1)
                    for q in range(16):
                        sc.wait_ge(s_u4, ch * 16 + q + 1)
                        if ch * 16 + q >= 4:
                            sc.wait_ge(s_m4, ch * 16 + q - 3)
                        if "e" not in SKIP:
                            sc.activation(e4[q % 4], u4[q % 4],
                                          AF.Exp).then_inc(s_exp, 1)
                        else:
                            sc.nop().then_inc(s_exp, 1)
                    sc.wait_ge(s_g, ch + 1)
                    if ch >= 2:
                        sc.wait_ge(s_cd, ch - 1)
                    if "n" not in SKIP:
                        sc.activation(normb[e], ybig[:, :], AF.Square,
                                      accum_out=sscol[:, :])
                        sc.drain()
                        sc.activation(rstdc[:, :], sscol[:, :], AF.Ln,
                                      scale=1.0 / INTER, bias=EPS)
                        sc.drain()
                        sc.activation(rstdc[:, :], rstdc[:, :], AF.Exp,
                                      scale=-0.5)
                        sc.drain()
                        sc.nop().then_inc(s_rst, 1)
                    else:
                        sc.nop().then_inc(s_rst, 1)


def build_p4(nc, sem, E):
    SKIP = set(os.environ.get("P4_SKIP", "").split(","))
    pA, pB = E["pA"], E["pB"]
    s_pc, NPC = E["s_pc"], E["NPC"]

    s_wl = sem("p4_wl"); s_ws = sem("p4_ws"); s_nt = sem("p4_nt")
    s_h0 = sem("p4_h0"); s_mm = sem("p4_mm"); s_x0 = sem("p4_x0")
    s_st = sem("p4_st"); s_xc = sem("p4_xc"); s_rs = sem("p4_rs")
    s_ap = sem("p4_ap"); s_dbg = sem("p4_dbg")
    E["d_dbgln"] = nc.dram_tensor("dbgln", [S // 128 * 128, 16], F32,
                                  kind="ExternalOutput")
    s_sf = sem("p4_sf"); s_sb2 = sem("p4_sb")

    NSUP = 16
    TPS = S // NSUP
    TT = TPS // 128
    NTILE = S // 128


    with (
        nc.sbuf_tensor("wouts", [128, KI * 2048], BF16) as woutsb,
        nc.sbuf_tensor("nT", [128, 2 * KI * TPS], BF16) as nTb,
        nc.sbuf_tensor("h0t", [128, 2048], F32) as h0t,
        nc.sbuf_tensor("x0b", [128, 2 * 2048], F32) as x0bb,
        nc.sbuf_tensor("x1b", [128, 2 * 2048], F32) as x1bb,
        nc.sbuf_tensor("lncol", [128, 16], F32) as lncol,
    ):
        wouts = [woutsb[:, k * 2048:(k + 1) * 2048] for k in range(KI)]
        nT = [[nTb[:, (e * KI + k) * TPS:(e * KI + k + 1) * TPS]
               for k in range(KI)] for e in range(2)]
        x0b = [x0bb[:, 0:2048], x0bb[:, 2048:4096]]
        x1b = [x1bb[:, 0:2048], x1bb[:, 2048:4096]]

        def LC(par, i):
            return lncol[:, par * 8 + i:par * 8 + i + 1]

        def LCbc(par, i):
            return AP(lncol, par * 8 + i, [[16, 128], [0, H]])

        with nc.Block() as blk:
            @blk.sync
            def _(sy):
                sy.wait_ge(s_pc, 16 * NPC)
                for k in range(KI):
                    sy.dma_start(wouts[k],
                                 E["d_woutb"].ap()[k * 128:(k + 1) * 128, :]
                                 ).then_inc(s_wl, 16)
                for sp in range(NSUP):
                    if sp >= 2:
                        sy.wait_ge(s_mm, (sp - 1) * TT)
                    for k in range(KI):
                        sy.dma_start_transpose(
                            nT[sp % 2][k],
                            E["d_norm"].ap()[sp * TPS:(sp + 1) * TPS,
                                             k * 128:(k + 1) * 128]
                        ).then_inc(s_nt, 16)
                    for tt in range(TT):
                        gi = sp * TT + tt
                        if gi >= 1:
                            sy.wait_ge(s_x0, gi)
                        if gi >= 2:
                            sy.wait_ge(s_rs, gi - 1)
                        sy.dma_start(
                            h0t[:, :],
                            E["d_h0"].ap()[sp * TPS + tt * 128:
                                           sp * TPS + (tt + 1) * 128, :]
                        ).then_inc(s_h0, 16)
                        if gi >= 2:
                            sy.wait_ge(s_ap, gi - 1)
                            sy.dma_start(
                                E["d_x1f"].ap()[(gi - 2) * 128:
                                                (gi - 1) * 128, :],
                                x1b[gi % 2]).then_inc(s_sf, 16)
                for gi in range(NTILE - 2, NTILE):
                    sy.wait_ge(s_ap, gi + 1)
                    sy.dma_start(E["d_x1f"].ap()[gi * 128:(gi + 1) * 128, :],
                                 x1b[gi % 2]).then_inc(s_sf, 16)
                sy.wait_ge(s_sf, 16 * NTILE)

            @blk.tensor
            def _(t_):
                t_.wait_ge(s_ws, KI)
                for sp in range(NSUP):
                    t_.wait_ge(s_nt, 16 * KI * (sp + 1))
                    for tt in range(TT):
                        gi = sp * TT + tt
                        psum = pA if gi % 2 == 0 else pB
                        if gi >= 2:
                            t_.wait_ge(s_x0, gi - 1)
                        for db in range(4):
                            for k in range(KI):
                                inst = t_.matmul(
                                    psum[:, db * 512:(db + 1) * 512],
                                    nT[sp % 2][k][:, tt * 128:(tt + 1) * 128],
                                    wouts[k][:, db * 512:(db + 1) * 512],
                                    start=(k == 0), stop=(k == KI - 1))
                        inst.then_inc(s_mm, 1)

            @blk.vector
            def _(v):
                v.wait_ge(s_wl, 16 * KI)
                for k in range(KI):
                    v.tensor_scalar_mul(wouts[k], wouts[k],
                                        E["nw_sb"][:, k:k + 1]).then_inc(s_ws, 1)
                for gi in range(NTILE):
                    par = gi % 2
                    v.wait_ge(s_mm, gi + 1)
                    v.wait_ge(s_h0, 16 * (gi + 1))
                    if gi >= 2:
                        v.wait_ge(s_ap, gi - 1)
                    psum = pA if par == 0 else pB
                    v.tensor_add(x0b[par], psum[:, :], h0t[:, :]).then_inc(s_x0, 1)
                    v.wait_ge(s_st, gi + 1)
                    v.scalar_tensor_tensor(x1b[par],
                                           LCbc(par, 0), 1.0 / H, x0b[par],
                                           ALU.mult, ALU.subtract
                                           ).then_inc(s_xc, 1)
                    v.wait_ge(s_rs, gi + 1)
                    v.tensor_mul(x1b[par], x1b[par], LCbc(par, 6)
                                 ).then_inc(s_ap, 1)

            @blk.scalar
            def _(sc):
                for gi in range(NTILE):
                    par = gi % 2
                    sc.wait_ge(s_x0, gi + 1)
                    if gi >= 2:
                        sc.wait_ge(s_sf, 16 * (gi - 1))
                        sc.wait_ge(s_sb2, 16 * (gi - 1))
                    sc.activation(x1b[par], x0b[par], AF.Copy,
                                  accum_out=LC(par, 0))
                    sc.drain()
                    sc.nop().then_inc(s_st, 1)
                    sc.wait_ge(s_xc, gi + 1)
                    sc.activation(x0b[par], x1b[par], AF.Square,
                                  accum_out=LC(par, 1))
                    sc.drain()
                    sc.activation(LC(par, 4), LC(par, 1), AF.Ln,
                                  scale=1.0 / H, bias=EPS)
                    sc.drain()
                    sc.activation(LC(par, 5), LC(par, 4), AF.Exp, scale=-0.5)
                    sc.drain()
                    sc.activation(LC(par, 6), LC(par, 5), AF.Copy, scale=-1.0)
                    sc.drain()
                    sc.nop().then_inc(s_rs, 1)
                    if "dbgln" in SKIP:
                        sc.dma_start(E["d_dbgln"].ap()[gi * 128:(gi + 1) * 128, :],
                                     lncol[:, :]).then_inc(s_dbg, 16)

            @blk.gpsimd
            def _(g):
                for gi in range(NTILE):
                    g.wait_ge(s_ap, gi + 1)
                    g.dma_start(E["d_x1b"].ap()[gi * 128:(gi + 1) * 128, :],
                                x1b[gi % 2]).then_inc(s_sb2, 16)
                g.wait_ge(s_sb2, 16 * NTILE)


def build_p5(nc, sem, E):
    bankA, bankB = E["bankA"], E["bankB"]

    s_xt = sem("p5_xt"); s_gm = sem("p5_gm")
    s_wl = [sem("p5_wla"), sem("p5_wlb")]
    s_um = sem("p5_um"); s_se = sem("p5_se"); s_gd = sem("p5_gd")
    s_gs = [sem("p5_gsa"), sem("p5_gsb")]; s_wt = sem("p5_wt")

    NOF = DFF // 128
    TB = S // 512

    WT_JOBS = [(E["d_wdb"], E["d_wdT"], k) for k in range(NOF)]
    E["s_wt5"] = s_wt

    def wT_lhsT_ap(dw, o):
        return AP(dw, o * 128, [[DFF, 128], [128 * DFF, KH], [1, 128]])

    with (
        nc.sbuf_tensor("x1T", [128, KH * S], BF16) as x1Tb,
        nc.sbuf_tensor("wgb5", [128, 2 * KH * 128], BF16) as wgb5,
        nc.sbuf_tensor("wub5", [128, 2 * KH * 128], BF16) as wub5,
        nc.sbuf_tensor("sgb", [128, 4 * 512], BF16) as sgbb,
        nc.sbuf_tensor("gtd", [128, 2 * S], BF16) as gtdb,
        nc.sbuf_tensor("wtb5", [128, 2 * H], BF16, is_transpose_dest=True
                       ) as wtb5,
    ):
        x1T = [x1Tb[:, k * S:(k + 1) * S] for k in range(KH)]
        wgb = [wgb5[:, 0:KH * 128], wgb5[:, KH * 128:2 * KH * 128]]
        wub = [wub5[:, 0:KH * 128], wub5[:, KH * 128:2 * KH * 128]]
        sgb = [sgbb[:, i * 512:(i + 1) * 512] for i in range(4)]
        gtd = [gtdb[:, 0:S], gtdb[:, S:2 * S]]
        wtb = [wtb5[:, 0:H], wtb5[:, H:2 * H]]

        with nc.Block() as blk:
            @blk.sync
            def _(sy):
                for k in range(KH):
                    sy.dma_start_transpose(
                        x1T[k],
                        E["d_x1b"].ap()[0:S, k * 128:(k + 1) * 128]
                    ).then_inc(s_xt, 16)
                sy.wait_ge(E["s_wtS"][0], 16 * (E["N_WT_GU"] // 2))
                sy.wait_ge(E["s_wtS"][1], 16 * (E["N_WT_GU"] // 2))
                for o in range(NOF):
                    if o >= 2:
                        sy.wait_ge(s_gm, (o - 1) * TB)
                    sy.dma_start(wgb[o % 2][:, :],
                                 wT_lhsT_ap(E["d_wgT"], o)
                                 ).then_inc(s_wl[o % 2], 16)
                    sy.dma_start(wub[o % 2][:, :],
                                 wT_lhsT_ap(E["d_wuT"], o)
                                 ).then_inc(s_wl[o % 2], 16)
                    if o >= 2:
                        sy.wait_ge(s_gd, (o - 1) * TB)
                        sy.dma_start(
                            E["d_gated"].ap()[(o - 2) * 128:(o - 1) * 128, :],
                            gtd[o % 2]).then_inc(s_gs[o % 2], 16)
                    src_, dst, k = WT_JOBS[o]
                    sy.dma_start_transpose(
                        wtb[0],
                        src_.ap()[0:H, k * 128:(k + 1) * 128]).then_inc(s_wt, 16)
                    sy.wait_ge(s_wt, 32 * o + 16)
                    sy.dma_start(dst.ap()[k * 128:(k + 1) * 128, :],
                                 wtb[0]).then_inc(s_wt, 16)
                    sy.wait_ge(s_wt, 32 * (o + 1))
                for o in range(NOF - 2, NOF):
                    sy.wait_ge(s_gd, o * TB + TB)
                    sy.dma_start(E["d_gated"].ap()[o * 128:(o + 1) * 128, :],
                                 gtd[o % 2]).then_inc(s_gs[o % 2], 16)
                for i in range(2):
                    sy.wait_ge(s_gs[i], 16 * NOF // 2)

            @blk.tensor
            def _(t_):
                t_.wait_ge(s_xt, 16 * KH)
                for o in range(NOF):
                    t_.wait_ge(s_wl[o % 2], 32 * (o // 2 + 1))
                    for tb in range(TB):
                        idx = o * TB + tb
                        if idx >= 2:
                            t_.wait_ge(s_se, idx - 1)
                        for k in range(KH):
                            inst = t_.matmul(
                                bankA(idx % 2),
                                wgb[o % 2][:, k * 128:(k + 1) * 128],
                                x1T[k][:, tb * 512:(tb + 1) * 512],
                                start=(k == 0), stop=(k == KH - 1))
                        inst.then_inc(s_gm, 1)
                        if idx >= 2:
                            t_.wait_ge(s_gd, idx - 1)
                        for k in range(KH):
                            inst = t_.matmul(
                                bankB(idx % 2),
                                wub[o % 2][:, k * 128:(k + 1) * 128],
                                x1T[k][:, tb * 512:(tb + 1) * 512],
                                start=(k == 0), stop=(k == KH - 1))
                        inst.then_inc(s_um, 1)

            @blk.scalar
            def _(sc):
                for o in range(NOF):
                    for tb in range(TB):
                        idx = o * TB + tb
                        sc.wait_ge(s_gm, idx + 1)
                        if idx >= 4:
                            sc.wait_ge(s_gd, idx - 3)
                        sc.activation(sgb[idx % 4], bankA(idx % 2),
                                      AF.Silu).then_inc(s_se, 1)


            @blk.vector
            def _(v):
                for o in range(NOF):
                    for tb in range(TB):
                        idx = o * TB + tb
                        v.wait_ge(s_um, idx + 1)
                        v.wait_ge(s_se, idx + 1)
                        if o >= 2 and tb == 0:
                            v.wait_ge(s_gs[o % 2], 16 * (o // 2))
                        v.tensor_mul(gtd[o % 2][:, tb * 512:(tb + 1) * 512],
                                     sgb[idx % 4], bankB(idx % 2)
                                     ).then_inc(s_gd, 1)


def build_p6(nc, sem, E):
    pA, pB = E["pA"], E["pB"]

    s_wl = sem("p6_wl"); s_mm = sem("p6_mm")
    s_gc = [sem("p6_gca"), sem("p6_gcb")]
    s_ev = sem("p6_ev"); s_ps = sem("p6_ps")
    s_pl = sem("p6_pl"); s_xl = sem("p6_xl")
    s_x0 = sem("p6_x0"); s_st = sem("p6_st"); s_xc = sem("p6_xc")
    s_rs = sem("p6_rs"); s_ap = sem("p6_ap")
    s_os = [sem("p6_osa"), sem("p6_osb")]

    NTILE = S // 128
    NHC = DFF // 128

    def gcol_ap(tt):
        return AP(E["d_gated"], tt * 128, [[S, 128], [128 * S, NHC], [1, 128]])

    with (
        nc.sbuf_tensor("wdh", [128, NHC * 1024], BF16) as wdhb,
        nc.sbuf_tensor("gcol", [128, 2 * DFF], BF16) as gcolb,
        nc.sbuf_tensor("prt", [128, 1024], F32) as prt,
        nc.sbuf_tensor("x1t6", [128, 2048], F32) as x1t6,
        nc.sbuf_tensor("dfl", [128, 2048], F32) as dfl,
        nc.sbuf_tensor("ob6", [128, 2 * 2048], F32) as ob6b,
        nc.sbuf_tensor("lncol6", [128, 16], F32) as lncol6,
    ):
        wdh = [wdhb[:, h * 1024:(h + 1) * 1024] for h in range(NHC)]
        gcol = [gcolb[:, 0:DFF], gcolb[:, DFF:2 * DFF]]
        ob6 = [ob6b[:, 0:2048], ob6b[:, 2048:4096]]

        def LC(par, i):
            return lncol6[:, par * 8 + i:par * 8 + i + 1]

        def LCbc(par, i):
            return AP(lncol6, par * 8 + i, [[16, 128], [0, H]])

        for dpass in range(2):
            with nc.Block() as blk:
                @blk.sync
                def _(sy):
                    if dpass == 0:
                        sy.wait_ge(E["s_wt5"], 32 * (DFF // 128))
                    for h in range(NHC):
                        sy.dma_start(
                            wdh[h],
                            E["d_wdT"].ap()[h * 128:(h + 1) * 128,
                                            dpass * 1024:(dpass + 1) * 1024]
                        ).then_inc(s_wl, 16)
                    for tt in range(NTILE):
                        gi = dpass * NTILE + tt
                        if tt >= 2:
                            sy.wait_ge(s_mm, gi - 1)
                        sy.dma_start(gcol[tt % 2], gcol_ap(tt)
                                     ).then_inc(s_gc[tt % 2], 16)
                        if dpass == 1:
                            if tt >= 1:
                                sy.wait_ge(s_x0, tt)
                            sy.dma_start(
                                x1t6[:, :],
                                E["d_x1f"].ap()[tt * 128:(tt + 1) * 128, :]
                            ).then_inc(s_xl, 16)
                            sy.dma_start(
                                prt[:, :],
                                E["d_part0"].ap()[tt * 128:(tt + 1) * 128, :]
                            ).then_inc(s_pl, 16)

                @blk.tensor
                def _(t_):
                    t_.wait_ge(s_wl, 16 * NHC * (dpass + 1))
                    for tt in range(NTILE):
                        gi = dpass * NTILE + tt
                        t_.wait_ge(s_gc[tt % 2],
                                   16 * (dpass * NTILE // 2 + tt // 2 + 1))
                        psum = pA[:, 0:1024] if tt % 2 == 0 else pB[:, 0:1024]
                        if tt >= 2:
                            if dpass == 0:
                                t_.wait_ge(s_ev, tt - 1)
                            else:
                                t_.wait_ge(s_x0, tt - 1)
                        for h in range(NHC):
                            for n in range(2):
                                inst = t_.matmul(
                                    psum[:, n * 512:(n + 1) * 512],
                                    gcol[tt % 2][:, h * 128:(h + 1) * 128],
                                    wdh[h][:, n * 512:(n + 1) * 512],
                                    start=(h == 0), stop=(h == NHC - 1))
                        inst.then_inc(s_mm, 1)

                if dpass == 0:
                    @blk.vector
                    def _(v):
                        for tt in range(NTILE):
                            v.wait_ge(s_mm, tt + 1)
                            if tt >= 1:
                                v.wait_ge(s_ps, 16 * tt)
                            psum = (pA[:, 0:1024] if tt % 2 == 0
                                    else pB[:, 0:1024])
                            v.tensor_copy(prt[:, :], psum).then_inc(s_ev, 1)

                    @blk.scalar
                    def _(sc):
                        for tt in range(NTILE):
                            sc.wait_ge(s_ev, tt + 1)
                            sc.dma_start(
                                E["d_part0"].ap()[tt * 128:(tt + 1) * 128, :],
                                prt[:, :]).then_inc(s_ps, 16)
                        sc.wait_ge(s_ps, 16 * NTILE)
                else:
                    @blk.vector
                    def _(v):
                        for tt in range(NTILE):
                            par = tt % 2
                            v.wait_ge(s_mm, NTILE + tt + 1)
                            v.wait_ge(s_xl, 16 * (tt + 1))
                            v.wait_ge(s_pl, 16 * (tt + 1))
                            if tt >= 1:
                                v.wait_ge(s_rs, tt)
                            psum = (pA[:, 0:1024] if par == 0
                                    else pB[:, 0:1024])
                            v.tensor_add(dfl[:, 0:1024], prt[:, :],
                                         x1t6[:, 0:1024])
                            v.tensor_add(dfl[:, 1024:2048], psum,
                                         x1t6[:, 1024:2048]).then_inc(s_x0, 1)
                            v.wait_ge(s_st, tt + 1)
                            v.scalar_tensor_tensor(ob6[par], LCbc(par, 0),
                                                   1.0 / H, dfl[:, :],
                                                   ALU.mult, ALU.subtract
                                                   ).then_inc(s_xc, 1)
                            v.wait_ge(s_rs, tt + 1)
                            v.tensor_mul(ob6[par], ob6[par], LCbc(par, 6)
                                         ).then_inc(s_ap, 1)

                if dpass == 1:
                    @blk.scalar
                    def _(sc):
                        for tt in range(NTILE):
                            par = tt % 2
                            sc.wait_ge(s_x0, tt + 1)
                            if tt >= 2:
                                sc.wait_ge(s_os[tt % 2], 16 * (tt // 2))
                            sc.activation(ob6[par], dfl[:, :], AF.Copy,
                                          accum_out=LC(par, 0))
                            sc.drain()
                            sc.nop().then_inc(s_st, 1)
                            sc.wait_ge(s_xc, tt + 1)
                            sc.activation(dfl[:, :], ob6[par], AF.Square,
                                          accum_out=LC(par, 1))
                            sc.drain()
                            sc.activation(LC(par, 4), LC(par, 1), AF.Ln,
                                          scale=1.0 / H, bias=EPS)
                            sc.drain()
                            sc.activation(LC(par, 5), LC(par, 4), AF.Exp,
                                          scale=-0.5)
                            sc.drain()
                            sc.activation(LC(par, 6), LC(par, 5), AF.Copy,
                                          scale=-1.0)
                            sc.drain()
                            sc.nop().then_inc(s_rs, 1)
                            sc.wait_ge(s_ap, tt + 1)
                            sc.dma_start(
                                E["d_out"].ap()[tt * 128:(tt + 1) * 128, :],
                                ob6[par]).then_inc(s_os[tt % 2], 16)
                        for i in range(2):
                            sc.wait_ge(s_os[i], 16 * NTILE // 2)


INPUT_MAP = [
    ("h0", "L0_attn_hidden_states"), ("wz", "L0_attn_Wz"), ("wx", "L0_attn_Wx"),
    ("wb", "L0_attn_WB"), ("wc", "L0_attn_WC"), ("wdt", "L0_attn_Wdt"),
    ("convw", "L0_attn_conv_w"), ("convb", "L0_attn_conv_b"),
    ("dtb", "L0_attn_dt_bias"), ("alog", "L0_attn_A_log"),
    ("dvec", "L0_attn_D"), ("nw", "L0_attn_norm_w"), ("wout", "L0_attn_Wout"),
    ("wgate", "L0_ffn_w_gate"), ("wup", "L0_ffn_w_up"), ("wdown", "L0_ffn_w_down"),
]


def make_in_maps(inputs, n_cores=8):
    hs = np.ascontiguousarray(np.asarray(inputs["L0_attn_hidden_states"],
                                         dtype=np.float32))
    shared = {}
    for dev, ref in INPUT_MAP:
        if dev == "h0":
            continue
        shared[dev] = np.ascontiguousarray(np.asarray(inputs[ref],
                                                      dtype=np.float32))
    maps = []
    for b in range(n_cores):
        m = dict(shared)
        m["h0"] = np.ascontiguousarray(hs[b])
        maps.append(m)
    return maps


def kernel(**inputs):
    nc = build_program()
    maps = make_in_maps(inputs)
    res = run_bass_kernel_spmd(nc, maps, core_ids=list(range(8)))
    out = np.stack([np.asarray(r["out"], dtype=np.float32)
                    for r in res.results], axis=0)
    return out


# version 08
# speedup vs baseline: 1.2901x; 1.0856x over previous
import os
from contextlib import ExitStack

import numpy as np
import concourse.bass as bass
import concourse.mybir as mybir
from concourse.bass_utils import run_bass_kernel_spmd

F32 = mybir.dt.float32
BF16 = mybir.dt.bfloat16
AF = mybir.ActivationFunctionType
ALU = mybir.AluOpType
AP = bass.AP

S = 4096
H = 2048
INTER = 4096
NH, HD, NS, NG = 64, 64, 128, 8
GN = NG * NS
CONV = INTER + 2 * GN
DFF = 8192
Q = 128
NCH = S // Q
KH = H // 128
KI = INTER // 128
NXT = CONV // 128
HALF = S // 2
NTB = HALF // 512
NTT = HALF // 128
EPS = 1e-5
CLAMP = 50.0


def build_program(dbg=(), phases=6, nch3=NCH):
    nc = bass.Bass("TRN2", target_bir_lowering=False,
                   detect_race_conditions=False)
    dbg = set(dbg)
    E = {}

    def sem(name):
        return nc.alloc_semaphore(name=name)

    def scratch(name, shape, dtype):
        kind = "ExternalOutput" if name in dbg else "Internal"
        return nc.dram_tensor(name, shape, dtype, kind=kind)

    E["d_h0"] = nc.dram_tensor("h0", [S, H], F32, kind="ExternalInput")
    E["d_wz"] = nc.dram_tensor("wz", [H, INTER], F32, kind="ExternalInput")
    E["d_wx"] = nc.dram_tensor("wx", [H, INTER], F32, kind="ExternalInput")
    E["d_wb"] = nc.dram_tensor("wb", [H, GN], F32, kind="ExternalInput")
    E["d_wc"] = nc.dram_tensor("wc", [H, GN], F32, kind="ExternalInput")
    E["d_wdt"] = nc.dram_tensor("wdt", [H, NH], F32, kind="ExternalInput")
    E["d_cw"] = nc.dram_tensor("convw", [CONV, 4], F32, kind="ExternalInput")
    E["d_cb"] = nc.dram_tensor("convb", [CONV], F32, kind="ExternalInput")
    E["d_dtb"] = nc.dram_tensor("dtb", [NH], F32, kind="ExternalInput")
    E["d_alog"] = nc.dram_tensor("alog", [NH], F32, kind="ExternalInput")
    E["d_dvec"] = nc.dram_tensor("dvec", [NH], F32, kind="ExternalInput")
    E["d_nw"] = nc.dram_tensor("nw", [INTER], F32, kind="ExternalInput")
    E["d_wout"] = nc.dram_tensor("wout", [INTER, H], F32, kind="ExternalInput")
    E["d_wg"] = nc.dram_tensor("wgate", [DFF, H], F32, kind="ExternalInput")
    E["d_wu"] = nc.dram_tensor("wup", [DFF, H], F32, kind="ExternalInput")
    E["d_wd"] = nc.dram_tensor("wdown", [H, DFF], F32, kind="ExternalInput")
    E["d_out"] = nc.dram_tensor("out", [S, H], F32, kind="ExternalOutput")

    E["d_hbf"] = scratch("h_bf", [S, H], BF16)
    E["d_woutb"] = scratch("wout_bf", [INTER, H], BF16)
    E["d_wgb"] = scratch("wg_bf", [DFF, H], BF16)
    E["d_wub"] = scratch("wu_bf", [DFF, H], BF16)
    E["d_wdb"] = scratch("wd_bf", [H, DFF], BF16)
    E["d_wgT"] = scratch("wgT", [H, DFF], BF16)
    E["d_wuT"] = scratch("wuT", [H, DFF], BF16)
    E["d_wdT"] = scratch("wdT", [DFF, H], BF16)
    E["d_xbc"] = scratch("xbc_bf", [CONV, S], BF16)
    E["d_sg"] = scratch("sg_h", [S, INTER], F32)
    E["d_tabs"] = scratch("tab_s", [NH, S], F32)
    E["d_tabsC"] = scratch("tab_sc", [NCH * NH, Q], F32)
    E["d_dec"] = scratch("dec_h", [NCH, NH], F32)
    E["d_norm"] = scratch("normed_h", [S, INTER], BF16)
    E["d_x1f"] = scratch("x1f_h", [S, H], F32)
    E["d_x1b"] = scratch("x1b_h", [S, H], BF16)
    E["d_gated"] = scratch("gated_h", [DFF, S], BF16)
    E["d_part0"] = scratch("part0_h", [S, H // 2], F32)

    pA = nc.alloc_psum_tensor("pA", [128, 2048], F32)
    pB = nc.alloc_psum_tensor("pB", [128, 2048], F32)
    E["pA"], E["pB"] = pA, pB
    E["bankA"] = lambda i: pA[:, i * 512:(i + 1) * 512]
    E["bankB"] = lambda i: pB[:, i * 512:(i + 1) * 512]

    sb = nc.alloc_sbuf_tensor
    E["cw_sb"] = sb("cw_sb", [128, 4 * NXT], F32)
    E["cb_sb"] = sb("cb_sb", [128, NXT], F32)
    E["dtb_sb"] = sb("dtb_sb", [64, 1], F32)
    E["alog_sb"] = sb("alog_sb", [64, 1], F32)
    E["negA_sb"] = sb("negA_sb", [64, 1], F32)
    E["dbcb_sb"] = sb("dbcb_sb", [128, 64], BF16)
    E["dbcf_sb"] = sb("dbcf_sb", [128, 64], F32)
    E["nw_sb"] = sb("nw_sb", [128, KI], F32)
    E["onesb_sb"] = sb("onesb_sb", [128, 128], BF16)
    E["onesf_sb"] = sb("onesf_sb", [64, 64], F32)
    E["onesf128"] = sb("onesf128", [128, 128], F32)
    E["id128f"] = sb("id128f", [128, 128], F32)
    E["caus_sb"] = sb("caus_sb", [128, 128], BF16)
    E["id64_sb"] = sb("id64_sb", [64, 64], F32)
    E["s_pc"] = sem("pc")

    CONSTS = [EPS, -0.5, -1.0, 1.0 / INTER, 1.0 / H, CLAMP]
    E["const_tiles"] = []
    for i, val in enumerate(CONSTS):
        t = sb(f"cst{i}", [128, 1], F32)
        nc.const_aps.register(F32, val, t[:, :])
        E["const_tiles"].append((t, val))

    with nc.named_scope("P1"):
        build_p1(nc, sem, E)
    if phases >= 2:
        with (
            nc.sbuf_tensor("pack1", [128, S], F32) as pack1,
            nc.sbuf_tensor("pack2", [128, S], F32) as pack2,
        ):
            E["pack1"], E["pack2"] = pack1, pack2
            with nc.named_scope("P2"):
                build_p2(nc, sem, E)
            if phases >= 3:
                with nc.named_scope("P3"):
                    build_p3(nc, sem, E, nch3)
    if phases >= 4:
        with nc.named_scope("P4"):
            build_p4(nc, sem, E)
    if phases >= 5:
        with nc.named_scope("P5"):
            build_p5(nc, sem, E)
    if phases >= 6:
        with nc.named_scope("P6"):
            build_p6(nc, sem, E)
    if phases >= 2:
        with nc.Block() as blk:
            @blk.gpsimd
            def _(g):
                g.wait_ge(E["s_pc"], 16 * E["NPC"])
    return nc


def build_p1(nc, sem, E):
    s_ld = sem("p1_ld")
    E["s_hc"] = s_hc = sem("p1_hc")

    with nc.Block() as blk:
        @blk.sync
        def _(sy):
            n = 0
            sy.dma_start(E["cw_sb"][:, :],
                         AP(E["d_cw"], 0, [[4, 128], [512, NXT], [1, 4]])
                         ).then_inc(s_ld, 16); n += 16
            sy.dma_start(E["cb_sb"][:, :],
                         AP(E["d_cb"], 0, [[1, 128], [128, NXT]]),
                         allow_slow_non_contiguous=True).then_inc(s_ld, 16); n += 16
            sy.dma_start(E["dtb_sb"][:, :], AP(E["d_dtb"], 0, [[1, 64], [1, 1]])
                         ).then_inc(s_ld, 16); n += 16
            sy.dma_start(E["alog_sb"][:, :], AP(E["d_alog"], 0, [[1, 64], [1, 1]])
                         ).then_inc(s_ld, 16); n += 16
            sy.dma_start(E["dbcf_sb"][:, :], AP(E["d_dvec"], 0, [[0, 128], [1, 64]])
                         ).then_inc(s_ld, 16); n += 16
            sy.dma_start(E["nw_sb"][:, :], AP(E["d_nw"], 0, [[1, 128], [128, KI]]),
                         allow_slow_non_contiguous=True).then_inc(s_ld, 16); n += 16
            sy.wait_ge(s_ld, n)

        @blk.gpsimd
        def _(g):
            for i in range(4):
                g.dma_start(E["d_hbf"].ap()[i * 1024:(i + 1) * 1024, :],
                            E["d_h0"].ap()[i * 1024:(i + 1) * 1024, :]
                            ).then_inc(s_hc, 16)
            g.memset(E["onesb_sb"][:, :], 1.0)
            g.memset(E["onesf_sb"][:, :], 1.0)
            g.memset(E["onesf128"][:, :], 1.0)
            for t, val in E["const_tiles"]:
                g.memset(t[:, :], val)

    with nc.Block() as blk:
        @blk.gpsimd
        def _(g):
            g.affine_select(E["caus_sb"][:, :], E["onesb_sb"][:, :], [[1, 128]],
                            ALU.is_ge, 0.0, channel_multiplier=-1)
            g.affine_select(E["id64_sb"][:, :], E["onesf_sb"][:, :], [[1, 64]],
                            ALU.is_equal, 0.0, channel_multiplier=-1)
            g.affine_select(E["id128f"][:, :], E["onesf128"][:, :], [[1, 128]],
                            ALU.is_equal, 0.0, channel_multiplier=-1)

        @blk.scalar
        def _(sc):
            sc.activation(E["negA_sb"][:, :], E["alog_sb"][:, :], AF.Exp)

        @blk.vector
        def _(v):
            v.tensor_copy(E["dbcb_sb"][:, :], E["dbcf_sb"][:, :])

    with nc.Block() as blk:
        @blk.vector
        def _(v):
            v.tensor_scalar_mul(E["negA_sb"][:, :], E["negA_sb"][:, :], -1.0)


def build_p2(nc, sem, E):
    bankA, bankB, pA = E["bankA"], E["bankB"], E["pA"]
    pack1, pack2 = E["pack1"], E["pack2"]
    s_pc, s_hc = E["s_pc"], E["s_hc"]

    s_ht = sem("p2_ht"); s_mm = sem("p2_mm")
    s_wld2 = [sem("p2_wld0"), sem("p2_wld1")]; s_wldt = sem("p2_wldt")
    s_ev = sem("p2_ev"); s_cv = sem("p2_cv"); s_si = sem("p2_si")
    s_sp = sem("p2_sp"); s_gld = sem("p2_gld"); s_gmm = sem("p2_gmm")
    s_gev = sem("p2_gev"); s_dmm = sem("p2_dmm")
    s_gst4 = [sem(f"p2_gst{i}") for i in range(4)]
    s_dev = sem("p2_dev"); s_tsp = sem("p2_tsp")

    W_SRCS = []
    for t in range(NXT):
        if t < 32:
            W_SRCS.append((E["d_wx"], t * 128, INTER))
        elif t < 40:
            W_SRCS.append((E["d_wb"], (t - 32) * 128, GN))
        else:
            W_SRCS.append((E["d_wc"], (t - 40) * 128, GN))

    PCH = ([(E["d_wg"], E["d_wgb"], i * 1024) for i in range(8)]
           + [(E["d_wu"], E["d_wub"], i * 1024) for i in range(8)])
    PCH_LATE = ([(E["d_wout"], E["d_woutb"], i * 1024) for i in range(4)]
                + [(E["d_wd"], E["d_wdb"], i * 1024) for i in range(2)])
    E["NPC"] = NPC = len(PCH) + len(PCH_LATE)
    E["NPC_EARLY"] = NPC_EARLY = len(PCH)

    def w_lhsT_ap(dw, of0, ncols):
        return AP(dw, of0, [[ncols, 128], [128 * ncols, KH], [1, 128]])

    def gw_rhs_ap(dw, ofb, ncols):
        return AP(dw, ofb * 512, [[ncols, 128], [128 * ncols, KH], [1, 512]])


    with ExitStack() as _st:
        hTb = _st.enter_context(nc.sbuf_tensor("hT", [128, KH * HALF], BF16))
        rb0 = _st.enter_context(nc.sbuf_tensor("rowb0", [128, HALF + 8], F32))
        rb1 = _st.enter_context(nc.sbuf_tensor("rowb1", [128, HALF + 8], F32))
        rowc = _st.enter_context(nc.sbuf_tensor("rowc", [128, HALF], F32))
        wxbb = _st.enter_context(nc.sbuf_tensor("wxb", [128, 2 * KH * 128], BF16))
        gwbb = _st.enter_context(nc.sbuf_tensor("gwb", [128, KH * 512], BF16))
        gevb = _st.enter_context(nc.sbuf_tensor("gev", [128, 4 * 512], F32))
        wdtb = _st.enter_context(nc.sbuf_tensor("wdtb", [128, KH * 64], BF16))
        carry = _st.enter_context(nc.sbuf_tensor("carry", [128, 3 * NXT], F32))
        slotA = _st.enter_context(nc.sbuf_tensor("slotA", [64, S], F32))
        slotB = _st.enter_context(nc.sbuf_tensor("slotB", [64, S], F32))
        slotC = _st.enter_context(nc.sbuf_tensor("slotC", [64, S], F32))
        zseg = _st.enter_context(nc.sbuf_tensor("zseg", [64, Q], F32))
        slast = _st.enter_context(nc.sbuf_tensor("slast", [64, NCH], F32))
        decT = _st.enter_context(nc.sbuf_tensor("decT", [NCH, 64], F32))
        hT = [hTb[:, k * HALF:(k + 1) * HALF] for k in range(KH)]
        rowbuf = [rb0, rb1]
        wxbuf = [wxbb[:, 0:KH * 128], wxbb[:, KH * 128:2 * KH * 128]]
        gwbuf = [gwbb, gwbb]
        gev = [gevb[:, i * 512:(i + 1) * 512] for i in range(4)]

        with nc.Block() as blk:
            @blk.gpsimd
            def _(g):
                g.memset(rb0[:, 0:3], 0.0)
                g.memset(rb1[:, 0:3], 0.0)
                g.memset(zseg[:, :], 0.0)
                g.memset(pack2[64:128, :], 0.0)

        pci = [0]

        for half in range(2):
            tok0 = half * HALF
            with nc.Block() as blk:
                @blk.sync
                def _(sy):
                    if half == 0:
                        sy.wait_ge(s_hc, 64)
                    for k in range(KH):
                        sy.dma_start_transpose(
                            hT[k],
                            E["d_hbf"].ap()[tok0:tok0 + HALF,
                                            k * 128:(k + 1) * 128]
                        ).then_inc(s_ht, 16)
                    for ofb in range(8):
                        for tt in range(NTT):
                            gidx = (half * 8 + ofb) * NTT + tt
                            sy.wait_ge(s_gev, gidx + 1)
                            sy.dma_start(
                                E["d_sg"].ap()[tok0 + tt * 128:
                                               tok0 + (tt + 1) * 128,
                                               ofb * 512:(ofb + 1) * 512],
                                gev[gidx % 4]).then_inc(s_gst4[gidx % 4], 16)
                    for i in range(4):
                        sy.wait_ge(s_gst4[i], 16 * (half + 1) * 32)

                @blk.gpsimd
                def _(g):
                    for t in range(NXT):
                        gi = half * NXT + t
                        if gi >= 2:
                            g.wait_ge(s_cv, gi - 1)
                        dw, of0, ncols = W_SRCS[t]
                        g.dma_start(wxbuf[gi % 2][:, :],
                                    w_lhsT_ap(dw, of0, ncols)
                                    ).then_inc(s_wld2[gi % 2], 16)
                        if half == 0 and t == 0:
                            g.dma_start(
                                wdtb[:, :],
                                AP(E["d_wdt"], 0,
                                   [[NH, 128], [128 * NH, KH], [1, 64]])
                            ).then_inc(s_wldt, 16)
                        target = (half * NXT + t) // 2
                        while pci[0] < min(target, NPC_EARLY):
                            src, dst, r0 = PCH[pci[0]]
                            g.dma_start(dst.ap()[r0:r0 + 1024, :],
                                        src.ap()[r0:r0 + 1024, :]
                                        ).then_inc(s_pc, 16)
                            pci[0] += 1
                        if gi >= 1:
                            g.wait_ge(s_si, gi)
                            tp, hp = (gi - 1) % NXT, (gi - 1) // NXT
                            g.dma_start(
                                E["d_xbc"].ap()[tp * 128:(tp + 1) * 128,
                                                hp * HALF:(hp + 1) * HALF],
                                rowc[:, :]).then_inc(s_sp, 16)
                    for ofb in range(8):
                        gio = half * 8 + ofb
                        if gio >= 1:
                            g.wait_ge(s_gmm, gio * NTT)
                        g.dma_start(gwbuf[gio % 2][:, :],
                                    gw_rhs_ap(E["d_wz"], ofb, INTER)
                                    ).then_inc(s_gld, 16)
                    gi_end = (half + 1) * NXT
                    g.wait_ge(s_si, gi_end)
                    g.dma_start(
                        E["d_xbc"].ap()[(NXT - 1) * 128:NXT * 128,
                                        half * HALF:(half + 1) * HALF],
                        rowc[:, :]).then_inc(s_sp, 16)
                    if half == 1:
                        while pci[0] < NPC_EARLY:
                            src, dst, r0 = PCH[pci[0]]
                            g.dma_start(dst.ap()[r0:r0 + 1024, :],
                                        src.ap()[r0:r0 + 1024, :]
                                        ).then_inc(s_pc, 16)
                            pci[0] += 1
                        for src, dst, r0 in PCH_LATE:
                            g.dma_start(dst.ap()[r0:r0 + 1024, :],
                                        src.ap()[r0:r0 + 1024, :]
                                        ).then_inc(s_pc, 16)
                    g.wait_ge(s_sp, 16 * gi_end)

                @blk.tensor
                def _(t_):
                    t_.wait_ge(s_ht, 16 * KH * (half + 1))
                    for t in range(NXT):
                        gi = half * NXT + t
                        t_.wait_ge(s_wld2[gi % 2], 16 * (gi // 2 + 1))
                        for tb in range(NTB):
                            idx = gi * NTB + tb
                            if idx >= 2:
                                t_.wait_ge(s_ev, idx - 1)
                            for k in range(KH):
                                inst = t_.matmul(
                                    bankA(idx % 2),
                                    wxbuf[gi % 2][:, k * 128:(k + 1) * 128],
                                    hT[k][:, tb * 512:(tb + 1) * 512],
                                    start=(k == 0), stop=(k == KH - 1))
                            inst.then_inc(s_mm, 1)
                    for ofb in range(8):
                        gio = half * 8 + ofb
                        t_.wait_ge(s_gld, 16 * (gio + 1))
                        for tt in range(NTT):
                            gidx = gio * NTT + tt
                            if gidx >= 2:
                                t_.wait_ge(s_gev, gidx - 1)
                            for k in range(KH):
                                inst = t_.matmul(
                                    bankB(gidx % 2),
                                    hT[k][:, tt * 128:(tt + 1) * 128],
                                    gwbuf[gio % 2][:, k * 512:(k + 1) * 512],
                                    start=(k == 0), stop=(k == KH - 1))
                            inst.then_inc(s_gmm, 1)
                    if half == 0:
                        t_.wait_ge(s_wldt, 16)
                    for tb in range(NTB):
                        didx = half * NTB + tb
                        if didx >= 2:
                            t_.wait_ge(s_dev, didx - 1)
                        for k in range(KH):
                            inst = t_.matmul(
                                bankB(2 + didx % 2)[0:64, :],
                                wdtb[:, k * 64:(k + 1) * 64],
                                hT[k][:, tb * 512:(tb + 1) * 512],
                                start=(k == 0), stop=(k == KH - 1))
                        inst.then_inc(s_dmm, 1)

                @blk.scalar
                def _(sc):
                    for t in range(NXT):
                        gi = half * NXT + t
                        for tb in range(NTB):
                            idx = gi * NTB + tb
                            sc.wait_ge(s_mm, idx + 1)
                            if gi >= 2 and tb == 0:
                                sc.wait_ge(s_cv, gi - 1)
                            sc.activation(
                                rowbuf[gi % 2][:, 3 + tb * 512:
                                               3 + (tb + 1) * 512],
                                bankA(idx % 2), AF.Copy).then_inc(s_ev, 1)
                        sc.wait_ge(s_cv, gi + 1)
                        sc.activation(rowc[:, :], rowc[:, :], AF.Silu)
                        sc.drain()
                        sc.nop().then_inc(s_si, 1)
                    for ofb in range(8):
                        for tt in range(NTT):
                            gidx = (half * 8 + ofb) * NTT + tt
                            sc.wait_ge(s_gmm, gidx + 1)
                            if gidx >= 4:
                                sc.wait_ge(s_gst4[gidx % 4], 16 * (gidx // 4))
                            sc.activation(gev[gidx % 4], bankB(gidx % 2),
                                          AF.Silu).then_inc(s_gev, 1)

                @blk.vector
                def _(v):
                    for t in range(NXT):
                        gi = half * NXT + t
                        v.wait_ge(s_ev, gi * NTB + NTB)
                        if gi >= 1:
                            v.wait_ge(s_sp, 16 * gi)
                        rb = rowbuf[gi % 2]
                        if half == 1:
                            v.tensor_copy(rb[:, 0:3], carry[:, 3 * t:3 * t + 3])
                        v.tensor_scalar(rowc[:, :], rb[:, 0:HALF],
                                        E["cw_sb"][:, 4 * t:4 * t + 1],
                                        E["cb_sb"][:, t:t + 1],
                                        ALU.mult, ALU.add)
                        for kk in range(1, 4):
                            v.scalar_tensor_tensor(
                                rowc[:, :], rb[:, kk:kk + HALF],
                                E["cw_sb"][:, 4 * t + kk:4 * t + kk + 1],
                                rowc[:, :], ALU.mult, ALU.add)
                        if half == 0:
                            v.tensor_copy(carry[:, 3 * t:3 * t + 3],
                                          rb[:, HALF:HALF + 3])
                        v.engine_nop().then_inc(s_cv, 1)
                    for tb in range(NTB):
                        didx = half * NTB + tb
                        v.wait_ge(s_dmm, didx + 1)
                        v.tensor_copy(
                            slotA[:, tok0 + tb * 512:tok0 + (tb + 1) * 512],
                            bankB(2 + didx % 2)[0:64, :]).then_inc(s_dev, 1)

        def vb(f):
            with nc.Block() as blk:
                blk.vector(f)

        def scb(f):
            with nc.Block() as blk:
                blk.scalar(f)

        vb(lambda v: v.tensor_scalar(slotA[:, :], slotA[:, :],
                                     E["dtb_sb"][:, :], None, ALU.add))
        scb(lambda sc: sc.activation(slotB[:, :], slotA[:, :], AF.Abs))
        scb(lambda sc: sc.activation(slotB[:, :], slotB[:, :], AF.Exp,
                                     scale=-1.0))
        scb(lambda sc: sc.activation(slotB[:, :], slotB[:, :], AF.Ln, bias=1.0))
        vb(lambda v: v.tensor_relu(slotC[:, :], slotA[:, :]))
        vb(lambda v: v.tensor_add(slotC[:, :], slotC[:, :], slotB[:, :]))
        vb(lambda v: v.tensor_scalar_mul(slotB[:, :], slotC[:, :],
                                         E["negA_sb"][:, :]))
        scb(lambda sc: sc.activation(slotC[:, :], slotC[:, :], AF.Ln))

        def scans(v):
            for ch in range(NCH):
                o = ch * Q
                v.tensor_tensor_scan(pack1[0:64, o:o + Q], slotB[:, o:o + Q],
                                     zseg[:, :], 0.0, ALU.add, ALU.add)
                rin = AP(slotB, o + Q - 1, [[S, 64], [-1, Q]])
                rz = AP(zseg, Q - 1, [[Q, 64], [-1, Q]])
                rout = AP(slotA, o + Q - 1, [[S, 64], [-1, Q]])
                v.tensor_tensor_scan(rout, rin, rz, 0.0, ALU.add, ALU.add)
        vb(scans)
        vb(lambda v: v.tensor_sub(slotA[:, :], slotA[:, :], slotB[:, :]))

        def finish_tabs(v):
            v.tensor_sub(slotB[:, :], slotC[:, :], pack1[0:64, :])
            v.tensor_add(pack2[0:64, :], slotA[:, :], slotC[:, :])
            v.tensor_copy(slast[:, :], AP(pack1, Q - 1, [[S, 64], [Q, NCH]]))
        vb(finish_tabs)
        scb(lambda sc: sc.activation(slast[:, :], slast[:, :], AF.Exp))
        s_nwm = sem("p2_nwm")
        with nc.Block() as blk:
            @blk.sync
            def _(sy):
                sy.dma_start(pack1[64:128, :], slotB[:, :]).then_inc(s_nwm, 16)
                sy.wait_ge(s_nwm, 16)

        with nc.Block() as blk:
            @blk.tensor
            def _(t_):
                t_.transpose(pA[0:NCH, 0:64], slast[:, :], E["id64_sb"][:, :])

        vb(lambda v: v.tensor_copy(decT[:, :], pA[0:NCH, 0:64]))

        with nc.Block() as blk:
            @blk.sync
            def _(sy):
                sy.dma_start(E["d_tabs"].ap()[:, :], pack1[0:64, :]
                             ).then_inc(s_tsp, 16)
                sy.dma_start(AP(E["d_tabsC"], 0,
                                [[Q, 64], [NH * Q, NCH], [1, Q]]),
                             pack1[0:64, :],
                             allow_slow_non_contiguous=True
                             ).then_inc(s_tsp, 16)
                sy.dma_start(E["d_dec"].ap()[:, :], decT[:, :]).then_inc(s_tsp, 16)
                sy.wait_ge(s_tsp, 48)


def build_p3(nc, sem, E, nch3=NCH):
    SKIP = set(os.environ.get("P3_SKIP", "").split(","))
    bankA, bankB, pA = E["bankA"], E["bankB"], E["pA"]
    pack1, pack2 = E["pack1"], E["pack2"]

    s_l1 = [sem("p3_l1a"), sem("p3_l1b")]
    s_l2 = [sem("p3_l2a"), sem("p3_l2b")]
    s_l3 = [sem(f"p3_l3{i}") for i in range(4)]
    s_l4 = sem("p3_l4")
    s_l5 = [sem("p3_l5a"), sem("p3_l5b")]
    s_ttr = sem("p3_ttr"); s_tev = sem("p3_tev"); s_ext = sem("p3_ext")
    s_gmm = sem("p3_gmm"); s_gme = sem("p3_gme")
    s_u4 = sem("p3_u4"); s_exp = sem("p3_exp"); s_m4 = sem("p3_m4")
    s_xs = sem("p3_xs")
    s_yim = sem("p3_yim"); s_yis = sem("p3_yis")
    s_yxm = sem("p3_yxm"); s_yfn = sem("p3_yfn")
    s_cmm = sem("p3_cmm"); s_stu = sem("p3_stu"); s_stc = sem("p3_stc")
    s_g = sem("p3_g"); s_rst = sem("p3_rst"); s_nrm = sem("p3_nrm")
    s_nst = sem("p3_nst"); s_cd = sem("p3_cd")
    s_wt3 = sem("p3_wt3")
    s_wtT = [sem("p3_wtT0"), sem("p3_wtT1")]
    s_wtS = [sem("p3_wtS0"), sem("p3_wtS1")]
    E["s_wtS"] = s_wtS
    WT_GU = []
    for mat_src, mat_dst in ((E["d_wgb"], E["d_wgT"]), (E["d_wub"], E["d_wuT"])):
        for k in range(KH):
            for hq in range(4):
                WT_GU.append((mat_src, mat_dst, k, hq))
    E["N_WT_GU"] = len(WT_GU)

    with ExitStack() as _st:
        DIall = _st.enter_context(nc.sbuf_tensor("DIall", [128, NH * 128], BF16))
        state = _st.enter_context(nc.sbuf_tensor("state", [128, NH * 64], F32))
        state_bf = _st.enter_context(nc.sbuf_tensor("state_bf", [128, NH * 64], BF16))
        xtokb = _st.enter_context(nc.sbuf_tensor("xtok", [128, 2 * 5120], BF16))
        bcFb = _st.enter_context(nc.sbuf_tensor("bcF", [128, 2 * 2048], BF16))
        sgch = _st.enter_context(nc.sbuf_tensor("sgch", [128, INTER], BF16))
        decbcb = _st.enter_context(nc.sbuf_tensor("decbc", [128, 2 * 64], F32))
        ubcb = _st.enter_context(nc.sbuf_tensor("ubc", [128, 4 * 512], F32))
        tabT = _st.enter_context(nc.sbuf_tensor("tabT", [128, 192], F32))
        expsT = _st.enter_context(nc.sbuf_tensor("expsT", [128, 64], F32))
        erdtT = _st.enter_context(nc.sbuf_tensor("erdtT", [128, 64], F32))
        Gm = _st.enter_context(nc.sbuf_tensor("Gm", [128, 8 * 128], BF16))
        u4b = _st.enter_context(nc.sbuf_tensor("u4", [128, 4 * 512], F32))
        e4b = _st.enter_context(nc.sbuf_tensor("e4", [128, 4 * 512], BF16))
        M4 = _st.enter_context(nc.sbuf_tensor("M4", [128, NH * 128], BF16))
        xs8 = _st.enter_context(nc.sbuf_tensor("xs8", [128, NH * 64], BF16))
        ybig = _st.enter_context(nc.sbuf_tensor("ybig", [128, INTER], F32))
        tmpg = _st.enter_context(nc.sbuf_tensor("tmpg", [128, 512], F32))
        normbb = _st.enter_context(nc.sbuf_tensor("normbb", [128, 2 * INTER], BF16))
        wtb3b = _st.enter_context(nc.sbuf_tensor("wtb3", [128, DFF // 2], BF16,
                                                 is_transpose_dest=True))
        wtb3 = [wtb3b[:, 0:DFF // 4], wtb3b[:, DFF // 4:DFF // 2]]
        sscol = _st.enter_context(nc.sbuf_tensor("sscol", [128, 1], F32))
        rstdc = _st.enter_context(nc.sbuf_tensor("rstdc", [128, 1], F32))
        xtok = [xtokb[:, 0:5120], xtokb[:, 5120:10240]]
        bcF = [bcFb[:, 0:2048], bcFb[:, 2048:4096]]
        decbc = [decbcb[:, 0:64], decbcb[:, 64:128]]
        ubc = [ubcb[:, i * 512:(i + 1) * 512] for i in range(4)]
        u4 = [u4b[:, i * 512:(i + 1) * 512] for i in range(4)]
        e4 = [e4b[:, i * 512:(i + 1) * 512] for i in range(4)]
        normb = [normbb[:, 0:INTER], normbb[:, INTER:2 * INTER]]

        with nc.Block() as blk:
            @blk.gpsimd
            def _(g):
                g.memset(state[:, :], 0.0)
                g.memset(state_bf[:, :], 0.0)
                g.affine_select(
                    AP(DIall, 0, [[NH * 128, 128], [128, 64], [1, 128]]),
                    AP(E["dbcb_sb"], 0, [[64, 128], [1, 64], [0, 128]]),
                    [[0, 64], [1, 128]], ALU.is_equal, 0.0,
                    channel_multiplier=-1)

        with nc.Block() as blk:
            @blk.sync
            def _(sy):
                for ch in range(nch3):
                    e = ch % 2
                    if ch >= 2:
                        sy.wait_ge(s_cd, ch - 1)
                    sy.dma_start(
                        bcF[e],
                        AP(E["d_xbc"], INTER * S + ch * Q,
                           [[S, 128], [128 * S, 16], [1, Q]])).then_inc(s_l1[e], 16)
                    sy.dma_start_transpose(
                        xtok[e],
                        E["d_xbc"].ap()[0:5120, ch * Q:(ch + 1) * Q]
                    ).then_inc(s_l2[e], 16)
                    sy.dma_start(decbc[e],
                                 AP(E["d_dec"], ch * 64, [[0, 128], [1, 64]])
                                 ).then_inc(s_l5[e], 16)
                    for q in range(16):
                        uq = ch * 16 + q
                        if uq >= 4:
                            sy.wait_ge(s_u4, uq - 3)
                        sy.dma_start(
                            ubc[q % 4],
                            AP(E["d_tabsC"], ch * NH * Q + (4 * q) * Q,
                               [[0, 128], [Q, 4], [1, Q]])
                        ).then_inc(s_l3[q % 4], 16)
                    if ch >= 1:
                        sy.wait_ge(s_nrm, ch)
                        sy.dma_start(E["d_norm"].ap()[(ch - 1) * Q:ch * Q, :],
                                     normb[(ch - 1) % 2][:, :]
                                     ).then_inc(s_nst, 16)
                        sy.wait_ge(s_nst, 16 * ch)
                        sy.sem_inc(s_cd, 1)
                    for j in range(4 * ch, min(4 * ch + 4, len(WT_GU))):
                        src_, dst, k, hq = WT_GU[j]
                        r0 = hq * (DFF // 4)
                        e2 = j % 2
                        if j == 0:
                            sy.wait_ge(E["s_pc"], 16 * E["NPC_EARLY"])
                        if j >= 2:
                            sy.wait_ge(s_wtS[e2], 16 * (j // 2))
                        sy.dma_start_transpose(
                            wtb3[e2],
                            src_.ap()[r0:r0 + DFF // 4, k * 128:(k + 1) * 128]
                        ).then_inc(s_wtT[e2], 16)
                        sy.wait_ge(s_wtT[e2], 16 * (j // 2 + 1))
                        sy.dma_start(
                            dst.ap()[k * 128:(k + 1) * 128, r0:r0 + DFF // 4],
                            wtb3[e2]).then_inc(s_wtS[e2], 16)
                sy.wait_ge(s_nrm, nch3)
                sy.dma_start(E["d_norm"].ap()[(nch3 - 1) * Q:nch3 * Q, :],
                             normb[(nch3 - 1) % 2][:, :]).then_inc(s_nst, 16)
                sy.wait_ge(s_nst, 16 * nch3)
                sy.sem_inc(s_cd, 1)

            @blk.tensor
            def _(t_):
                for ch in range(nch3):
                    e = ch % 2
                    t_.wait_ge(s_tev, 3 * ch)
                    t_.wait_ge(s_gme, 8 * ch)
                    if "tr" not in SKIP:
                        t_.transpose(pA[:, 0:128], pack1[:, ch * Q:(ch + 1) * Q],
                                     E["id128f"][:, :])
                        t_.transpose(pA[:, 128:256],
                                     pack2[:, ch * Q:(ch + 1) * Q],
                                     E["id128f"][:, :]).then_inc(s_ttr, 1)
                    else:
                        t_.nop().then_inc(s_ttr, 1)
                    t_.wait_ge(s_l1[e], 16 * (ch // 2 + 1))
                    t_.wait_ge(s_tev, 3 * ch + 3)
                    for g in range(8):
                        if "g" not in SKIP:
                            t_.matmul(
                                bankA(g // 4)[:, (g % 4) * 128:(g % 4 + 1) * 128],
                                bcF[e][:, g * 128:(g + 1) * 128],
                                bcF[e][:, 1024 + g * 128:1024 + (g + 1) * 128],
                                start=True, stop=True).then_inc(s_gmm, 1)
                        else:
                            t_.nop().then_inc(s_gmm, 1)
                    t_.wait_ge(s_stc, ch)
                    for g in range(8):
                        idx = ch * 8 + g
                        if idx >= 2:
                            t_.wait_ge(s_yis, idx - 1)
                        if "y" not in SKIP:
                            for hh in range(8):
                                h = g * 8 + hh
                                inst = t_.matmul(
                                    bankA(2 + g % 2)[:, hh * 64:(hh + 1) * 64],
                                    bcF[e][:, 1024 + g * 128:1024 + (g + 1) * 128],
                                    state_bf[:, h * 64:(h + 1) * 64],
                                    start=True, stop=True)
                            inst.then_inc(s_yim, 1)
                        else:
                            t_.nop().then_inc(s_yim, 1)
                    t_.wait_ge(s_l2[e], 16 * (ch // 2 + 1))
                    for g in range(8):
                        idx = ch * 8 + g
                        t_.wait_ge(s_m4, ch * 16 + 2 * (g + 1))
                        if idx >= 2:
                            t_.wait_ge(s_yfn, idx - 1)
                        if "y" not in SKIP:
                            for hh in range(8):
                                h = g * 8 + hh
                                inst = t_.matmul(
                                    bankB(g % 2)[:, hh * 64:(hh + 1) * 64],
                                    M4[:, h * 128:(h + 1) * 128],
                                    xtok[e][:, h * 64:(h + 1) * 64],
                                    start=True, stop=True)
                            inst.then_inc(s_yxm, 1)
                        else:
                            t_.nop().then_inc(s_yxm, 1)
                    for g in range(8):
                        idx = ch * 8 + g
                        t_.wait_ge(s_xs, idx + 1)
                        if idx >= 2:
                            t_.wait_ge(s_stu, idx - 1)
                        if "c" not in SKIP:
                            for hh in range(8):
                                h = g * 8 + hh
                                inst = t_.matmul(
                                    bankB(2 + g % 2)[:, hh * 64:(hh + 1) * 64],
                                    xtok[e][:, 4096 + g * 128:4096 + (g + 1) * 128],
                                    xs8[:, h * 64:(h + 1) * 64],
                                    start=True, stop=True)
                            inst.then_inc(s_cmm, 1)
                        else:
                            t_.nop().then_inc(s_cmm, 1)

            @blk.vector
            def _(v):
                for ch in range(nch3):
                    e = ch % 2
                    if ch >= 2:
                        v.wait_ge(s_cd, ch - 1)
                    v.wait_ge(s_ttr, ch + 1)
                    if ch >= 1:
                        v.wait_ge(s_ext, ch)
                    if "tr" not in SKIP:
                        v.tensor_copy(tabT[:, :], pA[:, 0:192])
                    v.engine_nop().then_inc(s_tev, 3)
                    for g in range(8):
                        v.wait_ge(s_gmm, ch * 8 + (4 if g < 4 else 8))
                        if "g" not in SKIP:
                            v.tensor_mul(
                                Gm[:, g * 128:(g + 1) * 128],
                                bankA(g // 4)[:, (g % 4) * 128:(g % 4 + 1) * 128],
                                E["caus_sb"][:, :]).then_inc(s_gme, 1)
                        else:
                            v.engine_nop().then_inc(s_gme, 1)
                    if ch >= 1:
                        v.wait_ge(s_yxm, 8 * ch)
                    for qq in range(18):
                        if qq < 16:
                            q = qq
                            v.wait_ge(s_l3[q % 4],
                                      16 * ((ch * 16 + q) // 4 + 1))
                            if ch * 16 + q >= 4:
                                v.wait_ge(s_exp, ch * 16 + q - 3)
                            if "u" not in SKIP:
                                for hh in range(4):
                                    h = q * 4 + hh
                                    v.tensor_scalar(
                                        u4[q % 4][:, hh * 128:(hh + 1) * 128],
                                        ubc[q % 4][:, hh * 128:(hh + 1) * 128],
                                        tabT[:, 64 + h:64 + h + 1], CLAMP,
                                        ALU.add, ALU.min)
                            v.engine_nop().then_inc(s_u4, 1)
                        if qq >= 2:
                            q = qq - 2
                            v.wait_ge(s_exp, ch * 16 + q + 1)
                            g = q // 2
                            v.wait_ge(s_gme, ch * 8 + g + 1)
                            if "m" not in SKIP:
                                gm_ap = AP(Gm, g * 128,
                                           [[1024, 128], [0, 4], [1, 128]])
                                v.tensor_tensor(M4[:, q * 512:(q + 1) * 512],
                                                e4[q % 4], gm_ap, ALU.mult)
                                v.tensor_add(M4[:, q * 512:(q + 1) * 512],
                                             M4[:, q * 512:(q + 1) * 512],
                                             DIall[:, q * 512:(q + 1) * 512]
                                             ).then_inc(s_m4, 1)
                            else:
                                v.engine_nop().then_inc(s_m4, 1)
                    v.wait_ge(s_l2[e], 16 * (ch // 2 + 1))
                    v.wait_ge(s_ext, ch + 1)
                    if ch >= 1:
                        v.wait_ge(s_cmm, 8 * ch)
                    for g in range(8):
                        if "c" not in SKIP:
                            erd_ap = AP(erdtT, 8 * g,
                                        [[64, 128], [1, 8], [0, 64]])
                            v.tensor_mul(xs8[:, g * 512:(g + 1) * 512],
                                         xtok[e][:, g * 512:(g + 1) * 512],
                                         erd_ap).then_inc(s_xs, 1)
                        else:
                            v.engine_nop().then_inc(s_xs, 1)
                    for g in range(8):
                        v.wait_ge(s_yim, ch * 8 + g + 1)
                        if "y" not in SKIP:
                            ex_ap = AP(expsT, 8 * g,
                                       [[64, 128], [1, 8], [0, 64]])
                            v.tensor_mul(ybig[:, g * 512:(g + 1) * 512],
                                         bankA(2 + g % 2), ex_ap
                                         ).then_inc(s_yis, 1)
                        else:
                            v.engine_nop().then_inc(s_yis, 1)
                    for g in range(8):
                        v.wait_ge(s_yxm, ch * 8 + g + 1)
                        if "y" not in SKIP:
                            v.tensor_add(ybig[:, g * 512:(g + 1) * 512],
                                         bankB(g % 2),
                                         ybig[:, g * 512:(g + 1) * 512]
                                         ).then_inc(s_yfn, 1)
                        else:
                            v.engine_nop().then_inc(s_yfn, 1)
                    v.wait_ge(s_yim, ch * 8 + 8)
                    v.wait_ge(s_l5[e], 16 * (ch // 2 + 1))
                    for g in range(8):
                        v.wait_ge(s_cmm, ch * 8 + g + 1)
                        if "c" not in SKIP:
                            dec_ap = AP(decbcb, e * 64 + 8 * g,
                                        [[128, 128], [1, 8], [0, 64]])
                            v.tensor_mul(tmpg[:, :],
                                         state[:, g * 512:(g + 1) * 512],
                                         dec_ap)
                            v.tensor_add(state[:, g * 512:(g + 1) * 512],
                                         tmpg[:, :], bankB(2 + g % 2)
                                         ).then_inc(s_stu, 1)
                        else:
                            v.engine_nop().then_inc(s_stu, 1)
                    v.tensor_copy(state_bf[:, :], state[:, :]).then_inc(s_stc, 1)
                    v.wait_ge(s_l4, 16 * (ch + 1))
                    if "n" not in SKIP:
                        v.tensor_mul(ybig[:, :], ybig[:, :], sgch[:, :])
                    v.engine_nop().then_inc(s_g, 1)
                    v.wait_ge(s_rst, ch + 1)
                    if "n" not in SKIP:
                        v.tensor_mul(normb[e], ybig[:, :],
                                     AP(rstdc, 0, [[1, 128], [0, INTER]]))
                    v.engine_nop().then_inc(s_nrm, 1)

            @blk.gpsimd
            def _(gp):
                for ch in range(nch3):
                    if ch >= 1:
                        gp.wait_ge(s_g, ch)
                    gp.dma_start(sgch[:, :],
                                 E["d_sg"].ap()[ch * Q:(ch + 1) * Q, :]
                                 ).then_inc(s_l4, 16)

            @blk.scalar
            def _(sc):
                for ch in range(nch3):
                    e = ch % 2
                    if ch >= 1:
                        sc.wait_ge(s_yis, 8 * ch)
                        sc.wait_ge(s_xs, 8 * ch)
                    sc.wait_ge(s_tev, 3 * ch + 3)
                    if "tr" not in SKIP:
                        sc.activation(expsT[:, :], tabT[:, 0:64], AF.Exp)
                        sc.activation(erdtT[:, :], tabT[:, 128:192],
                                      AF.Exp).then_inc(s_ext, 1)
                    else:
                        sc.nop().then_inc(s_ext, 1)
                    for q in range(16):
                        sc.wait_ge(s_u4, ch * 16 + q + 1)
                        if ch * 16 + q >= 4:
                            sc.wait_ge(s_m4, ch * 16 + q - 3)
                        if "e" not in SKIP:
                            sc.activation(e4[q % 4], u4[q % 4],
                                          AF.Exp).then_inc(s_exp, 1)
                        else:
                            sc.nop().then_inc(s_exp, 1)
                    sc.wait_ge(s_g, ch + 1)
                    if ch >= 2:
                        sc.wait_ge(s_cd, ch - 1)
                    if "n" not in SKIP:
                        sc.activation(normb[e], ybig[:, :], AF.Square,
                                      accum_out=sscol[:, :])
                        sc.drain()
                        sc.activation(rstdc[:, :], sscol[:, :], AF.Ln,
                                      scale=1.0 / INTER, bias=EPS)
                        sc.drain()
                        sc.activation(rstdc[:, :], rstdc[:, :], AF.Exp,
                                      scale=-0.5)
                        sc.drain()
                        sc.nop().then_inc(s_rst, 1)
                    else:
                        sc.nop().then_inc(s_rst, 1)


def build_p4(nc, sem, E):
    SKIP = set(os.environ.get("P4_SKIP", "").split(","))
    pA, pB = E["pA"], E["pB"]
    s_pc, NPC = E["s_pc"], E["NPC"]

    s_wl = sem("p4_wl"); s_ws = sem("p4_ws"); s_nt = sem("p4_nt")
    s_h0 = sem("p4_h0"); s_mm = sem("p4_mm"); s_x0 = sem("p4_x0")
    s_st = sem("p4_st"); s_xc = sem("p4_xc"); s_rs = sem("p4_rs")
    s_ap = sem("p4_ap"); s_dbg = sem("p4_dbg")
    E["d_dbgln"] = nc.dram_tensor("dbgln", [S // 128 * 128, 16], F32,
                                  kind="ExternalOutput")
    s_sf = sem("p4_sf"); s_sb2 = sem("p4_sb")

    NSUP = 16
    TPS = S // NSUP
    TT = TPS // 128
    NTILE = S // 128


    with (
        nc.sbuf_tensor("wouts", [128, KI * 2048], BF16) as woutsb,
        nc.sbuf_tensor("nT", [128, 2 * KI * TPS], BF16) as nTb,
        nc.sbuf_tensor("h0t", [128, 2048], F32) as h0t,
        nc.sbuf_tensor("x0b", [128, 2 * 2048], F32) as x0bb,
        nc.sbuf_tensor("x1b", [128, 2 * 2048], F32) as x1bb,
        nc.sbuf_tensor("lncol", [128, 16], F32) as lncol,
    ):
        wouts = [woutsb[:, k * 2048:(k + 1) * 2048] for k in range(KI)]
        nT = [[nTb[:, (e * KI + k) * TPS:(e * KI + k + 1) * TPS]
               for k in range(KI)] for e in range(2)]
        x0b = [x0bb[:, 0:2048], x0bb[:, 2048:4096]]
        x1b = [x1bb[:, 0:2048], x1bb[:, 2048:4096]]

        def LC(par, i):
            return lncol[:, par * 8 + i:par * 8 + i + 1]

        def LCbc(par, i):
            return AP(lncol, par * 8 + i, [[16, 128], [0, H]])

        with nc.Block() as blk:
            @blk.sync
            def _(sy):
                sy.wait_ge(s_pc, 16 * NPC)
                for k in range(KI):
                    sy.dma_start(wouts[k],
                                 E["d_woutb"].ap()[k * 128:(k + 1) * 128, :]
                                 ).then_inc(s_wl, 16)
                for sp in range(NSUP):
                    if sp >= 2:
                        sy.wait_ge(s_mm, (sp - 1) * TT)
                    for k in range(KI):
                        sy.dma_start_transpose(
                            nT[sp % 2][k],
                            E["d_norm"].ap()[sp * TPS:(sp + 1) * TPS,
                                             k * 128:(k + 1) * 128]
                        ).then_inc(s_nt, 16)
                sy.wait_ge(s_sf, 16 * NTILE)

            @blk.tensor
            def _(t_):
                t_.wait_ge(s_ws, KI)
                for sp in range(NSUP):
                    t_.wait_ge(s_nt, 16 * KI * (sp + 1))
                    for tt in range(TT):
                        gi = sp * TT + tt
                        psum = pA if gi % 2 == 0 else pB
                        if gi >= 2:
                            t_.wait_ge(s_x0, gi - 1)
                        for db in range(4):
                            for k in range(KI):
                                inst = t_.matmul(
                                    psum[:, db * 512:(db + 1) * 512],
                                    nT[sp % 2][k][:, tt * 128:(tt + 1) * 128],
                                    wouts[k][:, db * 512:(db + 1) * 512],
                                    start=(k == 0), stop=(k == KI - 1))
                        inst.then_inc(s_mm, 1)

            @blk.vector
            def _(v):
                v.wait_ge(s_wl, 16 * KI)
                for k in range(KI):
                    v.tensor_scalar_mul(wouts[k], wouts[k],
                                        E["nw_sb"][:, k:k + 1]).then_inc(s_ws, 1)
                for gi in range(NTILE):
                    par = gi % 2
                    v.wait_ge(s_mm, gi + 1)
                    v.wait_ge(s_h0, 16 * (gi + 1))
                    if gi >= 2:
                        v.wait_ge(s_ap, gi - 1)
                    psum = pA if par == 0 else pB
                    v.tensor_add(x0b[par], psum[:, :], h0t[:, :]).then_inc(s_x0, 1)
                    v.wait_ge(s_st, gi + 1)
                    v.scalar_tensor_tensor(x1b[par],
                                           LCbc(par, 0), 1.0 / H, x0b[par],
                                           ALU.mult, ALU.subtract
                                           ).then_inc(s_xc, 1)
                    v.wait_ge(s_rs, gi + 1)
                    v.tensor_mul(x1b[par], x1b[par], LCbc(par, 6)
                                 ).then_inc(s_ap, 1)

            @blk.scalar
            def _(sc):
                for gi in range(NTILE):
                    par = gi % 2
                    sc.wait_ge(s_x0, gi + 1)
                    if gi >= 2:
                        sc.wait_ge(s_sf, 16 * (gi - 1))
                        sc.wait_ge(s_sb2, 16 * (gi - 1))
                    sc.activation(x1b[par], x0b[par], AF.Copy,
                                  accum_out=LC(par, 0))
                    sc.drain()
                    sc.nop().then_inc(s_st, 1)
                    sc.wait_ge(s_xc, gi + 1)
                    sc.activation(x0b[par], x1b[par], AF.Square,
                                  accum_out=LC(par, 1))
                    sc.drain()
                    sc.activation(LC(par, 4), LC(par, 1), AF.Ln,
                                  scale=1.0 / H, bias=EPS)
                    sc.drain()
                    sc.activation(LC(par, 5), LC(par, 4), AF.Exp, scale=-0.5)
                    sc.drain()
                    sc.activation(LC(par, 6), LC(par, 5), AF.Copy, scale=-1.0)
                    sc.drain()
                    sc.nop().then_inc(s_rs, 1)
                    if "dbgln" in SKIP:
                        sc.dma_start(E["d_dbgln"].ap()[gi * 128:(gi + 1) * 128, :],
                                     lncol[:, :]).then_inc(s_dbg, 16)

            @blk.gpsimd
            def _(g):
                for gi in range(NTILE):
                    if gi >= 1:
                        g.wait_ge(s_x0, gi)
                    if gi >= 2:
                        g.wait_ge(s_rs, gi - 1)
                    g.dma_start(
                        h0t[:, :],
                        E["d_h0"].ap()[gi * 128:(gi + 1) * 128, :]
                    ).then_inc(s_h0, 16)
                    if gi >= 2:
                        g.wait_ge(s_ap, gi - 1)
                        g.dma_start(
                            E["d_x1b"].ap()[(gi - 2) * 128:(gi - 1) * 128, :],
                            x1b[gi % 2]).then_inc(s_sb2, 16)
                        g.dma_start(
                            E["d_x1f"].ap()[(gi - 2) * 128:(gi - 1) * 128, :],
                            x1b[gi % 2]).then_inc(s_sf, 16)
                for gi in range(NTILE - 2, NTILE):
                    g.wait_ge(s_ap, gi + 1)
                    g.dma_start(E["d_x1b"].ap()[gi * 128:(gi + 1) * 128, :],
                                x1b[gi % 2]).then_inc(s_sb2, 16)
                    g.dma_start(E["d_x1f"].ap()[gi * 128:(gi + 1) * 128, :],
                                x1b[gi % 2]).then_inc(s_sf, 16)
                g.wait_ge(s_sb2, 16 * NTILE)


def build_p5(nc, sem, E):
    bankA, bankB = E["bankA"], E["bankB"]

    s_xt = sem("p5_xt"); s_gm = sem("p5_gm")
    s_wl = [sem("p5_wla"), sem("p5_wlb")]
    s_um = sem("p5_um"); s_se = sem("p5_se"); s_gd = sem("p5_gd")
    s_gs = [sem("p5_gsa"), sem("p5_gsb")]; s_wt = sem("p5_wt")

    NOF = DFF // 128
    TB = S // 512

    WT_JOBS = [(E["d_wdb"], E["d_wdT"], k) for k in range(NOF)]
    E["s_wt5"] = s_wt

    def wT_lhsT_ap(dw, o):
        return AP(dw, o * 128, [[DFF, 128], [128 * DFF, KH], [1, 128]])

    with (
        nc.sbuf_tensor("x1T", [128, KH * S], BF16) as x1Tb,
        nc.sbuf_tensor("wgb5", [128, 2 * KH * 128], BF16) as wgb5,
        nc.sbuf_tensor("wub5", [128, 2 * KH * 128], BF16) as wub5,
        nc.sbuf_tensor("sgb", [128, 4 * 512], BF16) as sgbb,
        nc.sbuf_tensor("gtd", [128, 2 * S], BF16) as gtdb,
        nc.sbuf_tensor("wtb5", [128, 2 * H], BF16, is_transpose_dest=True
                       ) as wtb5,
    ):
        x1T = [x1Tb[:, k * S:(k + 1) * S] for k in range(KH)]
        wgb = [wgb5[:, 0:KH * 128], wgb5[:, KH * 128:2 * KH * 128]]
        wub = [wub5[:, 0:KH * 128], wub5[:, KH * 128:2 * KH * 128]]
        sgb = [sgbb[:, i * 512:(i + 1) * 512] for i in range(4)]
        gtd = [gtdb[:, 0:S], gtdb[:, S:2 * S]]
        wtb = [wtb5[:, 0:H], wtb5[:, H:2 * H]]

        with nc.Block() as blk:
            @blk.sync
            def _(sy):
                for k in range(KH):
                    sy.dma_start_transpose(
                        x1T[k],
                        E["d_x1b"].ap()[0:S, k * 128:(k + 1) * 128]
                    ).then_inc(s_xt, 16)
                sy.wait_ge(E["s_wtS"][0], 16 * (E["N_WT_GU"] // 2))
                sy.wait_ge(E["s_wtS"][1], 16 * (E["N_WT_GU"] // 2))
                for o in range(NOF):
                    if o >= 2:
                        sy.wait_ge(s_gm, (o - 1) * TB)
                    sy.dma_start(wgb[o % 2][:, :],
                                 wT_lhsT_ap(E["d_wgT"], o)
                                 ).then_inc(s_wl[o % 2], 16)
                    sy.dma_start(wub[o % 2][:, :],
                                 wT_lhsT_ap(E["d_wuT"], o)
                                 ).then_inc(s_wl[o % 2], 16)
                    if o >= 2:
                        sy.wait_ge(s_gd, (o - 1) * TB)
                        sy.dma_start(
                            E["d_gated"].ap()[(o - 2) * 128:(o - 1) * 128, :],
                            gtd[o % 2]).then_inc(s_gs[o % 2], 16)
                    src_, dst, k = WT_JOBS[o]
                    sy.dma_start_transpose(
                        wtb[0],
                        src_.ap()[0:H, k * 128:(k + 1) * 128]).then_inc(s_wt, 16)
                    sy.wait_ge(s_wt, 32 * o + 16)
                    sy.dma_start(dst.ap()[k * 128:(k + 1) * 128, :],
                                 wtb[0]).then_inc(s_wt, 16)
                    sy.wait_ge(s_wt, 32 * (o + 1))
                for o in range(NOF - 2, NOF):
                    sy.wait_ge(s_gd, o * TB + TB)
                    sy.dma_start(E["d_gated"].ap()[o * 128:(o + 1) * 128, :],
                                 gtd[o % 2]).then_inc(s_gs[o % 2], 16)
                for i in range(2):
                    sy.wait_ge(s_gs[i], 16 * NOF // 2)

            @blk.tensor
            def _(t_):
                t_.wait_ge(s_xt, 16 * KH)
                for o in range(NOF):
                    t_.wait_ge(s_wl[o % 2], 32 * (o // 2 + 1))
                    for tb in range(TB):
                        idx = o * TB + tb
                        if idx >= 2:
                            t_.wait_ge(s_se, idx - 1)
                        for k in range(KH):
                            inst = t_.matmul(
                                bankA(idx % 2),
                                wgb[o % 2][:, k * 128:(k + 1) * 128],
                                x1T[k][:, tb * 512:(tb + 1) * 512],
                                start=(k == 0), stop=(k == KH - 1))
                        inst.then_inc(s_gm, 1)
                        if idx >= 2:
                            t_.wait_ge(s_gd, idx - 1)
                        for k in range(KH):
                            inst = t_.matmul(
                                bankB(idx % 2),
                                wub[o % 2][:, k * 128:(k + 1) * 128],
                                x1T[k][:, tb * 512:(tb + 1) * 512],
                                start=(k == 0), stop=(k == KH - 1))
                        inst.then_inc(s_um, 1)

            @blk.scalar
            def _(sc):
                for o in range(NOF):
                    for tb in range(TB):
                        idx = o * TB + tb
                        sc.wait_ge(s_gm, idx + 1)
                        if idx >= 4:
                            sc.wait_ge(s_gd, idx - 3)
                        sc.activation(sgb[idx % 4], bankA(idx % 2),
                                      AF.Silu).then_inc(s_se, 1)


            @blk.vector
            def _(v):
                for o in range(NOF):
                    for tb in range(TB):
                        idx = o * TB + tb
                        v.wait_ge(s_um, idx + 1)
                        v.wait_ge(s_se, idx + 1)
                        if o >= 2 and tb == 0:
                            v.wait_ge(s_gs[o % 2], 16 * (o // 2))
                        v.tensor_mul(gtd[o % 2][:, tb * 512:(tb + 1) * 512],
                                     sgb[idx % 4], bankB(idx % 2)
                                     ).then_inc(s_gd, 1)


def build_p6(nc, sem, E):
    pA, pB = E["pA"], E["pB"]

    s_wl = sem("p6_wl"); s_mm = sem("p6_mm")
    s_gc = [sem("p6_gca"), sem("p6_gcb")]
    s_ev = sem("p6_ev"); s_ps = sem("p6_ps")
    s_pl = sem("p6_pl"); s_xl = sem("p6_xl")
    s_x0 = sem("p6_x0"); s_st = sem("p6_st"); s_xc = sem("p6_xc")
    s_rs = sem("p6_rs"); s_ap = sem("p6_ap")
    s_os = [sem("p6_osa"), sem("p6_osb")]

    NTILE = S // 128
    NHC = DFF // 128

    def gcol_ap(tt):
        return AP(E["d_gated"], tt * 128, [[S, 128], [128 * S, NHC], [1, 128]])

    with (
        nc.sbuf_tensor("wdh", [128, NHC * 1024], BF16) as wdhb,
        nc.sbuf_tensor("gcol", [128, 2 * DFF], BF16) as gcolb,
        nc.sbuf_tensor("prt", [128, 1024], F32) as prt,
        nc.sbuf_tensor("x1t6", [128, 2048], F32) as x1t6,
        nc.sbuf_tensor("dfl", [128, 2048], F32) as dfl,
        nc.sbuf_tensor("ob6", [128, 2 * 2048], F32) as ob6b,
        nc.sbuf_tensor("lncol6", [128, 16], F32) as lncol6,
    ):
        wdh = [wdhb[:, h * 1024:(h + 1) * 1024] for h in range(NHC)]
        gcol = [gcolb[:, 0:DFF], gcolb[:, DFF:2 * DFF]]
        ob6 = [ob6b[:, 0:2048], ob6b[:, 2048:4096]]

        def LC(par, i):
            return lncol6[:, par * 8 + i:par * 8 + i + 1]

        def LCbc(par, i):
            return AP(lncol6, par * 8 + i, [[16, 128], [0, H]])

        for dpass in range(2):
            with nc.Block() as blk:
                @blk.sync
                def _(sy):
                    if dpass == 0:
                        sy.wait_ge(E["s_wt5"], 32 * (DFF // 128))
                    for h in range(NHC):
                        sy.dma_start(
                            wdh[h],
                            E["d_wdT"].ap()[h * 128:(h + 1) * 128,
                                            dpass * 1024:(dpass + 1) * 1024]
                        ).then_inc(s_wl, 16)
                    for tt in range(NTILE):
                        gi = dpass * NTILE + tt
                        if tt >= 2:
                            sy.wait_ge(s_mm, gi - 1)
                        sy.dma_start(gcol[tt % 2], gcol_ap(tt)
                                     ).then_inc(s_gc[tt % 2], 16)
                        if dpass == 1:
                            if tt >= 1:
                                sy.wait_ge(s_x0, tt)
                            sy.dma_start(
                                x1t6[:, :],
                                E["d_x1f"].ap()[tt * 128:(tt + 1) * 128, :]
                            ).then_inc(s_xl, 16)
                            sy.dma_start(
                                prt[:, :],
                                E["d_part0"].ap()[tt * 128:(tt + 1) * 128, :]
                            ).then_inc(s_pl, 16)

                @blk.tensor
                def _(t_):
                    t_.wait_ge(s_wl, 16 * NHC * (dpass + 1))
                    for tt in range(NTILE):
                        gi = dpass * NTILE + tt
                        t_.wait_ge(s_gc[tt % 2],
                                   16 * (dpass * NTILE // 2 + tt // 2 + 1))
                        psum = pA[:, 0:1024] if tt % 2 == 0 else pB[:, 0:1024]
                        if tt >= 2:
                            if dpass == 0:
                                t_.wait_ge(s_ev, tt - 1)
                            else:
                                t_.wait_ge(s_x0, tt - 1)
                        for h in range(NHC):
                            for n in range(2):
                                inst = t_.matmul(
                                    psum[:, n * 512:(n + 1) * 512],
                                    gcol[tt % 2][:, h * 128:(h + 1) * 128],
                                    wdh[h][:, n * 512:(n + 1) * 512],
                                    start=(h == 0), stop=(h == NHC - 1))
                        inst.then_inc(s_mm, 1)

                if dpass == 0:
                    @blk.vector
                    def _(v):
                        for tt in range(NTILE):
                            v.wait_ge(s_mm, tt + 1)
                            if tt >= 1:
                                v.wait_ge(s_ps, 16 * tt)
                            psum = (pA[:, 0:1024] if tt % 2 == 0
                                    else pB[:, 0:1024])
                            v.tensor_copy(prt[:, :], psum).then_inc(s_ev, 1)

                    @blk.scalar
                    def _(sc):
                        for tt in range(NTILE):
                            sc.wait_ge(s_ev, tt + 1)
                            sc.dma_start(
                                E["d_part0"].ap()[tt * 128:(tt + 1) * 128, :],
                                prt[:, :]).then_inc(s_ps, 16)
                        sc.wait_ge(s_ps, 16 * NTILE)
                else:
                    @blk.vector
                    def _(v):
                        for tt in range(NTILE):
                            par = tt % 2
                            v.wait_ge(s_mm, NTILE + tt + 1)
                            v.wait_ge(s_xl, 16 * (tt + 1))
                            v.wait_ge(s_pl, 16 * (tt + 1))
                            if tt >= 1:
                                v.wait_ge(s_rs, tt)
                            psum = (pA[:, 0:1024] if par == 0
                                    else pB[:, 0:1024])
                            v.tensor_add(dfl[:, 0:1024], prt[:, :],
                                         x1t6[:, 0:1024])
                            v.tensor_add(dfl[:, 1024:2048], psum,
                                         x1t6[:, 1024:2048]).then_inc(s_x0, 1)
                            v.wait_ge(s_st, tt + 1)
                            v.scalar_tensor_tensor(ob6[par], LCbc(par, 0),
                                                   1.0 / H, dfl[:, :],
                                                   ALU.mult, ALU.subtract
                                                   ).then_inc(s_xc, 1)
                            v.wait_ge(s_rs, tt + 1)
                            v.tensor_mul(ob6[par], ob6[par], LCbc(par, 6)
                                         ).then_inc(s_ap, 1)

                if dpass == 1:
                    @blk.scalar
                    def _(sc):
                        for tt in range(NTILE):
                            par = tt % 2
                            sc.wait_ge(s_x0, tt + 1)
                            if tt >= 2:
                                sc.wait_ge(s_os[tt % 2], 16 * (tt // 2))
                            sc.activation(ob6[par], dfl[:, :], AF.Copy,
                                          accum_out=LC(par, 0))
                            sc.drain()
                            sc.nop().then_inc(s_st, 1)
                            sc.wait_ge(s_xc, tt + 1)
                            sc.activation(dfl[:, :], ob6[par], AF.Square,
                                          accum_out=LC(par, 1))
                            sc.drain()
                            sc.activation(LC(par, 4), LC(par, 1), AF.Ln,
                                          scale=1.0 / H, bias=EPS)
                            sc.drain()
                            sc.activation(LC(par, 5), LC(par, 4), AF.Exp,
                                          scale=-0.5)
                            sc.drain()
                            sc.activation(LC(par, 6), LC(par, 5), AF.Copy,
                                          scale=-1.0)
                            sc.drain()
                            sc.nop().then_inc(s_rs, 1)
                            sc.wait_ge(s_ap, tt + 1)
                            sc.dma_start(
                                E["d_out"].ap()[tt * 128:(tt + 1) * 128, :],
                                ob6[par]).then_inc(s_os[tt % 2], 16)
                        for i in range(2):
                            sc.wait_ge(s_os[i], 16 * NTILE // 2)


INPUT_MAP = [
    ("h0", "L0_attn_hidden_states"), ("wz", "L0_attn_Wz"), ("wx", "L0_attn_Wx"),
    ("wb", "L0_attn_WB"), ("wc", "L0_attn_WC"), ("wdt", "L0_attn_Wdt"),
    ("convw", "L0_attn_conv_w"), ("convb", "L0_attn_conv_b"),
    ("dtb", "L0_attn_dt_bias"), ("alog", "L0_attn_A_log"),
    ("dvec", "L0_attn_D"), ("nw", "L0_attn_norm_w"), ("wout", "L0_attn_Wout"),
    ("wgate", "L0_ffn_w_gate"), ("wup", "L0_ffn_w_up"), ("wdown", "L0_ffn_w_down"),
]


def make_in_maps(inputs, n_cores=8):
    hs = np.ascontiguousarray(np.asarray(inputs["L0_attn_hidden_states"],
                                         dtype=np.float32))
    shared = {}
    for dev, ref in INPUT_MAP:
        if dev == "h0":
            continue
        shared[dev] = np.ascontiguousarray(np.asarray(inputs[ref],
                                                      dtype=np.float32))
    maps = []
    for b in range(n_cores):
        m = dict(shared)
        m["h0"] = np.ascontiguousarray(hs[b])
        maps.append(m)
    return maps


def kernel(**inputs):
    nc = build_program()
    maps = make_in_maps(inputs)
    res = run_bass_kernel_spmd(nc, maps, core_ids=list(range(8)))
    out = np.stack([np.asarray(r["out"], dtype=np.float32)
                    for r in res.results], axis=0)
    return out
